# version 30
# baseline (speedup 1.0000x reference)
import contextlib
import numpy as np
import ml_dtypes

import concourse.bass as bass
import concourse.mybir as mybir
from concourse.bass_utils import run_bass_kernel_spmd

F32 = mybir.dt.float32
BF16 = mybir.dt.bfloat16
AF = mybir.ActivationFunctionType
ALU = mybir.AluOpType
AX = mybir.AxisListType.X

D = 2048
S = 2048
H = 5632
NE, R = 8, 16
KD = 16
JH = 44
JD = 16
TOWN = 1024
OWN0 = 1024
CH = 512
ALPHA = 2.0

ENGS = ("sync", "gpsimd", "tensor", "vector", "scalar")
SEM_NAMES = ("pe", "dv", "ac", "si", "gs", "sx0", "sx1", "sh0", "sh1",
             "gq0", "gq1", "gf0", "gf1", "gf2", "gf3", "gd0", "gd1", "gd2", "gd3", "sr0", "sr1",
             "sg0", "sg1", "sg2", "sg3", "so0", "so1")


class Prog:

    def __init__(self):
        self.st = {e: [] for e in ENGS}
        self.cnt = {s: 0 for s in SEM_NAMES}
        self.audit = []

    def v(self, sem):
        return self.cnt[sem]

    def snap(self):
        return dict(self.cnt)

    def emit(self, eng, fn=None, waits=(), inc=None, incn=1):
        if fn is not None and eng in ("vector", "scalar"):
            own = "dv" if eng == "vector" else "ac"
            waits = list(waits) + [(own, self.cnt[own])]
            if inc is None:
                inc, incn = own, 1
            else:
                assert inc == own
        wd = {}
        for s, v in waits:
            if v > 0:
                wd[s] = max(wd.get(s, 0), v)
        waits = list(wd.items())
        if inc is not None:
            self.cnt[inc] += incn
        self.st[eng].append((fn, tuple(waits), inc, incn))
        self.audit.extend(waits)
        return self.cnt[inc] if inc else None

    def check(self):
        for s, v in self.audit:
            assert v <= self.cnt[s], f"wait {s}>={v} but final {self.cnt[s]} -> hang"

    def realize(self, block, sems):
        for eng in ENGS:
            stream = self.st[eng]
            if not stream:
                continue

            def body(E, stream=stream):
                for fn, waits, inc, incn in stream:
                    for s, v in waits:
                        E.wait_ge(sems[s], v)
                    inst = fn(E) if fn is not None else None
                    if inc is not None:
                        inst.then_inc(sems[inc], incn)

            getattr(block, eng)(body)


def _retile_inputs(ins):
    W = np.ascontiguousarray(np.asarray(ins["L0_attn_W"], np.float32))
    w_up = np.ascontiguousarray(np.asarray(ins["L0_ffn_w_up"], np.float32))
    w_gate = np.ascontiguousarray(np.asarray(ins["L0_ffn_w_gate"], np.float32))
    w_down = np.ascontiguousarray(np.asarray(ins["L0_ffn_w_down"], np.float32))
    up_A = np.asarray(ins["L0_ffn_up_A"], np.float32)
    up_B = np.asarray(ins["L0_ffn_up_B"], np.float32)
    gate_A = np.asarray(ins["L0_ffn_gate_A"], np.float32)
    gate_B = np.asarray(ins["L0_ffn_gate_B"], np.float32)
    down_A = np.asarray(ins["L0_ffn_down_A"], np.float32)
    down_B = np.asarray(ins["L0_ffn_down_B"], np.float32)
    gate_w = np.asarray(ins["L0_ffn_gate_w"], np.float32)

    wqg = np.ascontiguousarray(
        W.reshape(2, 16, 128, KD, 128).transpose(1, 4, 0, 3, 2).reshape(16, 128, 4096))
    wu = w_up.reshape(JH, 128, KD, 128).transpose(0, 3, 2, 1).reshape(JH, 128, 2048)
    wg = w_gate.reshape(JH, 128, KD, 128).transpose(0, 3, 2, 1).reshape(JH, 128, 2048)
    ub = up_B.transpose(0, 2, 1).reshape(128, JH, 128).transpose(1, 0, 2)
    gb = gate_B.transpose(0, 2, 1).reshape(128, JH, 128).transpose(1, 0, 2)
    da = down_A.transpose(2, 0, 1).reshape(JH, 128, 128)
    wfn = np.ascontiguousarray(np.concatenate([wu, wg, ub, gb, da], axis=2))
    wdn = np.ascontiguousarray(
        w_down.reshape(JD, 128, JH, 128).transpose(0, 3, 2, 1).reshape(JD, 128, H))
    ua = up_A.transpose(2, 0, 1).reshape(D, 128).reshape(KD, 128, 128).transpose(1, 0, 2).reshape(128, 2048)
    ga = gate_A.transpose(2, 0, 1).reshape(D, 128).reshape(KD, 128, 128).transpose(1, 0, 2).reshape(128, 2048)
    db = down_B.transpose(0, 2, 1).reshape(128, 2048)
    gwt = gate_w.T.reshape(KD, 128, 8).transpose(1, 0, 2).reshape(128, 128)
    wsmall = np.ascontiguousarray(np.concatenate([ua, ga, db, gwt], axis=1))
    return wqg, wfn, wdn, wsmall


def build_program():
    nc = bass.Bass("TRN2", target_bir_lowering=False, debug=False)
    x_in = nc.dram_tensor("x", [S, D], F32, kind="ExternalInput").ap()
    wqg_in = nc.dram_tensor("wqg", [16, 128, 4096], F32, kind="ExternalInput").ap()
    wfn_in = nc.dram_tensor("wfn", [JH, 128, 4480], F32, kind="ExternalInput").ap()
    wdn_in = nc.dram_tensor("wdn", [JD, 128, H], F32, kind="ExternalInput").ap()
    wsm_in = nc.dram_tensor("wsmall", [128, 6272], F32, kind="ExternalInput").ap()
    gfix_in = nc.dram_tensor("gfix", [128, 1], F32, kind="ExternalInput").ap()
    out_hbm = nc.dram_tensor("out", [TOWN, D], F32, kind="ExternalOutput").ap()
    hscr = nc.dram_tensor("hscr", [D, TOWN], F32, kind="Internal").ap()

    ident = np.eye(128, dtype=np.float32)
    id32_d = nc.inline_tensor(ident, name="id32c")
    idb_d = nc.inline_tensor(ident.astype(ml_dtypes.bfloat16), name="idbc")
    expt_np = np.zeros((8, 128), dtype=np.float32)
    for e in range(NE):
        expt_np[e, e * R:(e + 1) * R] = ALPHA
    expt_d = nc.inline_tensor(expt_np, name="exptc")
    ones1_d = nc.inline_tensor(np.ones((1, 128), np.float32), name="ones1c")
    onescol_d = nc.inline_tensor(np.ones((128, 1), np.float32), name="onescolc")

    P = Prog()
    es = contextlib.ExitStack()
    with es:
        C = es.enter_context(nc.sbuf_tensor("consts", [128, 8], F32))
        id32 = es.enter_context(nc.sbuf_tensor("id32", [128, 128], F32))
        idb = es.enter_context(nc.sbuf_tensor("idb", [128, 128], BF16))
        expT = es.enter_context(nc.sbuf_tensor("expT", [8, 128], F32))
        ones1 = es.enter_context(nc.sbuf_tensor("ones1", [1, 128], F32))
        onescol = es.enter_context(nc.sbuf_tensor("onescol", [128, 1], F32))
        gfix = es.enter_context(nc.sbuf_tensor("gfixs", [128, 1], F32))
        wsm = es.enter_context(nc.sbuf_tensor("wsm", [128, 6272], BF16))
        xnT_own = es.enter_context(nc.sbuf_tensor("xnT_own", [128, KD * TOWN], BF16))
        M1s = es.enter_context(nc.sbuf_tensor("M1s", [8, TOWN], F32))
        M2s = es.enter_context(nc.sbuf_tensor("M2s", [8, TOWN], F32))
        w1own = es.enter_context(nc.sbuf_tensor("w1own", [1, TOWN], F32))
        w2own = es.enter_context(nc.sbuf_tensor("w2own", [1, TOWN], F32))
        sems = {s: es.enter_context(nc.semaphore(s)) for s in SEM_NAMES}

        xnT_oth_holder = [None]

        def xn_slice(k, t0, w):
            if t0 >= OWN0:
                assert t0 + w <= OWN0 + TOWN
                return xnT_own[:, k * TOWN + (t0 - OWN0): k * TOWN + (t0 - OWN0) + w]
            assert t0 + w <= TOWN
            return xnT_oth_holder[0][:, k * TOWN + t0: k * TOWN + t0 + w]

        P.emit("sync", lambda E: E.dma_start(out=id32[:, :], in_=id32_d.ap()), inc="si", incn=16)
        P.emit("sync", lambda E: E.dma_start(out=idb[:, :], in_=idb_d.ap()), inc="si", incn=16)
        P.emit("sync", lambda E: E.dma_start(out=expT[:, :], in_=expt_d.ap()), inc="si", incn=16)
        P.emit("sync", lambda E: E.dma_start(out=ones1[:, :], in_=ones1_d.ap()), inc="si", incn=16)
        P.emit("sync", lambda E: E.dma_start(out=onescol[:, :], in_=onescol_d.ap()), inc="si", incn=16)
        P.emit("sync", lambda E: E.dma_start(out=gfix[:, :], in_=gfix_in), inc="si", incn=16)
        P.emit("gpsimd", lambda E: E.dma_start(out=wsm[:, :], in_=wsm_in), inc="gs", incn=16)
        for col, val in ((0, 1.0 / 2048.0), (1, 1e-5), (2, 1e-20), (3, 1.0), (4, -1e30), (5, -1.0)):
            P.emit("vector", lambda E, c=col, v=val: E.memset(C[:, c:c + 1], v),
                   inc="dv" if col == 5 else None)
        consts_dv = P.v("dv")
        init_si = P.v("si")

        s1 = contextlib.ExitStack()
        xnT_oth = s1.enter_context(nc.sbuf_tensor("xnT_oth", [128, KD * TOWN], BF16))
        xnT_oth_holder[0] = xnT_oth

        with contextlib.ExitStack() as sla:
            xld = [sla.enter_context(nc.sbuf_tensor(f"xld{i}", [128, D], F32)) for i in range(2)]
            xnb = [sla.enter_context(nc.sbuf_tensor(f"xnb{i}", [128, D], BF16)) for i in range(2)]
            bnst = [sla.enter_context(nc.sbuf_tensor(f"bnst{i}", [128, 24], F32)) for i in range(2)]
            lst = sla.enter_context(nc.sbuf_tensor("lnstats", [128, 16], F32))
            wqg_bf = [sla.enter_context(nc.sbuf_tensor(f"wqgbf{i}", [128, 4096], BF16)) for i in range(2)]
            qsb = sla.enter_context(nc.sbuf_tensor("qsb", [128, S], F32))
            gsb = sla.enter_context(nc.sbuf_tensor("gsb", [128, S], F32))
            hsb = [sla.enter_context(nc.sbuf_tensor(f"hsb{i}", [128, S], F32)) for i in range(2)]
            ltok = sla.enter_context(nc.sbuf_tensor("ltok", [128, 128], F32))
            m12all = sla.enter_context(nc.sbuf_tensor("m12all", [128, 32], F32))
            etok = sla.enter_context(nc.sbuf_tensor("etok", [128, 32], F32))
            l2buf = sla.enter_context(nc.sbuf_tensor("l2buf", [128, 8], F32))
            maskall = sla.enter_context(nc.sbuf_tensor("maskall", [128, 128], F32))
            mtmp = sla.enter_context(nc.sbuf_tensor("mtmp", [128, 8], F32))
            part12 = sla.enter_context(nc.sbuf_tensor("part12", [128, 2], F32))
            Ssb = [sla.enter_context(nc.sbuf_tensor(f"Ssb{i}", [1, 1], F32)) for i in range(2)]
            erow = [sla.enter_context(nc.sbuf_tensor(f"erow{i}", [1, TOWN], F32)) for i in range(2)]
            spq = contextlib.ExitStack()
            psq = spq.enter_context(nc.psum_tensor("psq", [128, 1024], F32, side="right"))
            psg = spq.enter_context(nc.psum_tensor("psg", [128, 2048], F32, side="right"))
            ptok = sla.enter_context(nc.psum_tensor("ptok", [128, 128], F32))
            strp = contextlib.ExitStack()
            trp = [strp.enter_context(nc.psum_tensor("trp0", [128, 1024], BF16))]

            P.emit("gpsimd", lambda E: E.dma_start(out=wqg_bf[0][:, :], in_=wqg_in[0]),
                   inc="gq0", incn=16)
            P.emit("gpsimd", lambda E: E.dma_start(out=wqg_bf[1][:, :], in_=wqg_in[1]),
                   inc="gq1", incn=16)
            wqg_ready = [P.v("gq0"), P.v("gq1")]
            wqg_free = [0, 0]
            bank_q = [0, 0]
            bank_g = [0, 0, 0, 0]
            q_pe = [0] * 4
            g_pe = [0] * 4
            evac_dv = [0] * 4
            evac_ac = [0] * 4
            hsb_free = [0, 0]
            scan_dv = 0

            def attn_mms(j, ncs, extra, defer_g=False):
                b = j % 2
                for qg in range(2):
                    ps = psq if qg == 0 else psg
                    gnc = ncs % 4
                    P.emit("tensor", lambda E, b=b, qg=qg, ncs=ncs, ps=ps, gnc=gnc: [
                        E.matmul(ps[:, ((ncs % 2) if qg == 0 else gnc) * 512:
                                    ((ncs % 2) if qg == 0 else gnc) * 512 + 512],
                                 wqg_bf[b][:, qg * 2048 + k * 128:qg * 2048 + (k + 1) * 128],
                                 xn_slice(k, ncs * 512, 512),
                                 start=(k == 0), stop=(k == KD - 1)) for k in range(KD)][-1],
                        waits=extra + [(f"gq{b}", wqg_ready[b])] +
                              ([("dv", bank_q[ncs % 2])] if qg == 0 else [("ac", bank_g[gnc])]),
                        inc="pe")
                    if qg == 0:
                        q_pe[ncs] = P.v("pe")
                    else:
                        g_pe[ncs] = P.v("pe")
                P.emit("vector", lambda E, ncs=ncs: E.tensor_scalar(
                    qsb[:, ncs * 512:(ncs + 1) * 512], psq[:, (ncs % 2) * 512:(ncs % 2 + 1) * 512],
                    C[:, 2:3], None, ALU.max),
                    waits=[("pe", q_pe[ncs]), ("dv", scan_dv)], inc="dv")
                evac_dv[ncs] = P.v("dv")
                bank_q[ncs % 2] = P.v("dv")
                if not defer_g:
                    P.emit("scalar", lambda E, ncs=ncs: E.activation(
                        gsb[:, ncs * 512:(ncs + 1) * 512],
                        psg[:, (ncs % 4) * 512:(ncs % 4 + 1) * 512], AF.Sigmoid),
                        waits=[("pe", g_pe[ncs]), ("dv", scan_dv)], inc="ac")
                    evac_ac[ncs] = P.v("ac")
                    bank_g[ncs % 4] = P.v("ac")

            def attn_tail(j):
                b = j % 2
                P.emit("vector", lambda E: E.tensor_scalar(
                    gsb[:, OWN0:OWN0 + 1], gsb[:, OWN0:OWN0 + 1], gfix[:, 0:1], None, ALU.mult),
                    waits=[("ac", evac_ac[3]), ("si", init_si)], inc="dv")
                P.emit("vector", lambda E, b=b: E.tensor_tensor_scan(
                    hsb[b][:, :], gsb[:, :], qsb[:, :], 0.0, ALU.mult, ALU.add),
                    waits=[("dv", P.v("dv")), ("ac", evac_ac[3]), (f"sh{b}", hsb_free[b])],
                    inc="dv")
                sc = P.v("dv")
                P.emit("sync", lambda E, j=j, b=b: E.dma_start(
                    out=hscr[j * 128:(j + 1) * 128, :], in_=hsb[b][:, OWN0:OWN0 + TOWN]),
                    waits=[("dv", sc)], inc=f"sh{b}", incn=16)
                hsb_free[b] = P.v(f"sh{b}")
                return sc

            xld_free = [0, 0]
            xnb_free = [0, 0]
            trp_free = [0]
            for i in range(16):
                b = i % 2
                P.emit("sync", lambda E, b=b, i=i: E.dma_start(
                    out=xld[b][:, :], in_=x_in[i * 128:(i + 1) * 128, :]),
                    waits=[("dv", xld_free[b])], inc=f"sx{b}", incn=16)
                ld_tgt = P.v(f"sx{b}")
                o8 = 8 * b
                for cch in range(4):
                    P.emit("vector", lambda E, b=b, cch=cch: E.bn_stats(
                        bnst[b][:, cch * 6:(cch + 1) * 6],
                        xld[b][:, cch * 512:(cch + 1) * 512]),
                        waits=[(f"sx{b}", ld_tgt), ("dv", consts_dv)] if cch == 0 else [],
                        inc="dv")
                P.emit("vector", lambda E, b=b, o8=o8: E.bn_aggr(
                    lst[:, o8:o8 + 2], bnst[b][:, :]),
                    waits=[("dv", P.v("dv"))], inc="dv")
                bn_dv = P.v("dv")
                P.emit("scalar", lambda E, o8=o8: E.activation(
                    lst[:, o8 + 3:o8 + 4], lst[:, o8 + 1:o8 + 2], AF.Ln, bias=C[:, 1:2]),
                    waits=[("dv", bn_dv)], inc="ac")
                P.emit("scalar", lambda E, o8=o8: E.activation(
                    lst[:, o8 + 4:o8 + 5], lst[:, o8 + 3:o8 + 4], AF.Exp, scale=-0.5),
                    waits=[("ac", P.v("ac"))], inc="ac")
                P.emit("vector", lambda E, b=b, o8=o8: E.tensor_scalar(
                    xnb[b][:, :], xld[b][:, :], lst[:, o8:o8 + 1], lst[:, o8 + 4:o8 + 5],
                    ALU.subtract, ALU.mult),
                    waits=[("ac", P.v("ac")), ("pe", xnb_free[b])], inc="dv")
                xld_free[b] = P.v("dv")
                xn_dv = P.v("dv")
                dst = xnT_own if i >= 8 else xnT_oth
                off = (i - 8 if i >= 8 else i) * 128
                for hh in range(2):
                    P.emit("tensor", lambda E, b=b, hh=hh: [
                        E.transpose(trp[0][:, jj * 128:(jj + 1) * 128],
                                    xnb[b][:, (hh * 8 + jj) * 128:(hh * 8 + jj + 1) * 128],
                                    idb[:, :]) for jj in range(8)][-1],
                        waits=[("dv", xn_dv), ("dv", trp_free[0]), ("si", init_si)], inc="pe")
                    pe_tr = P.v("pe")
                    P.emit("vector", lambda E, hh=hh, dst=dst, off=off: E.tensor_copy(
                        dst[:, hh * 8 * TOWN:(hh * 8 + 8) * TOWN].rearrange(
                            "p (j t) -> p j t", j=8)[:, :, off:off + 128],
                        trp[0][:, :].rearrange("p (j t) -> p j t", j=8)),
                        waits=[("pe", pe_tr)], inc="dv")
                    trp_free[0] = P.v("dv")
                xnb_free[b] = P.v("pe")
                P.emit("tensor", lambda E, i=i: [
                    E.matmul(ptok[:, i * 8:(i + 1) * 8],
                             xn_slice(k, i * 128, 128),
                             wsm[:, 6144 + k * 8:6144 + (k + 1) * 8],
                             start=(k == 0), stop=(k == KD - 1))
                    for k in range(KD)][-1],
                    waits=[("gs", 16), ("dv", P.v("dv"))], inc="pe")
                if i % 4 == 3:
                    cseq = i // 4
                    attn_mms(0, cseq, [("dv", P.v("dv"))], defer_g=True)
            P.emit("scalar", lambda E: E.activation(gsb[:, :], psg[:, :], AF.Sigmoid),
                   waits=[("pe", g_pe[3])], inc="ac")
            for ncs in range(4):
                evac_ac[ncs] = P.v("ac")
                bank_g[ncs] = P.v("ac")
            scan_dv = attn_tail(0)
            wqg_free[0] = P.v("pe")
            lgt_pe = P.v("pe")
            ln_all_dv = P.v("dv")
            strp.close()

            ltok_dv = 0
            maskall_dv = [0] * 8
            m12_dv = 0

            def routing_piece(step):
                nonlocal ltok_dv, m12_dv
                if step == 0:
                    P.emit("vector", lambda E: E.tensor_copy(ltok[:, :], ptok[:, :]),
                           waits=[("pe", lgt_pe)], inc="dv")
                    ltok_dv = P.v("dv")
                elif 1 <= step <= 8:
                    for i in (2 * (step - 1), 2 * (step - 1) + 1):
                        own = i >= 8
                        m1dst = maskall[:, (i - 8) * 16:(i - 8) * 16 + 8] if own else mtmp[:, 0:8]
                        m2dst = maskall[:, (i - 8) * 16 + 8:(i - 8) * 16 + 16] if own else mtmp[:, 0:8]
                        L = ltok[:, i * 8:(i + 1) * 8]
                        P.emit("vector", lambda E, L=L, i=i: E.reduce_max(
                            m12all[:, 2 * i:2 * i + 1], L, axis=AX),
                            waits=[("dv", ltok_dv)], inc="dv")
                        P.emit("vector", lambda E, L=L, d=m1dst, i=i: E.tensor_scalar(
                            d, L, m12all[:, 2 * i:2 * i + 1], None, ALU.is_ge),
                            waits=[("dv", P.v("dv"))], inc="dv")
                        P.emit("vector", lambda E, L=L, d=m1dst: E.scalar_tensor_tensor(
                            l2buf[:, :], d, C[:, 4:5], L, ALU.mult, ALU.add),
                            waits=[("dv", P.v("dv"))], inc="dv")
                        P.emit("vector", lambda E, i=i: E.reduce_max(
                            m12all[:, 2 * i + 1:2 * i + 2], l2buf[:, :], axis=AX),
                            waits=[("dv", P.v("dv"))], inc="dv")
                        P.emit("vector", lambda E, d=m2dst, i=i: E.tensor_scalar(
                            d, l2buf[:, :], m12all[:, 2 * i + 1:2 * i + 2], None, ALU.is_ge),
                            waits=[("dv", P.v("dv"))], inc="dv")
                        if i >= 8:
                            maskall_dv[i - 8] = P.v("dv")
                    if step == 8:
                        m12_dv = P.v("dv")

            for j in range(1, 16):
                b = j % 2
                if j >= 2:
                    P.emit("gpsimd", lambda E, j=j, b=b: E.dma_start(
                        out=wqg_bf[b][:, :], in_=wqg_in[j]),
                        waits=[("pe", wqg_free[b])], inc=f"gq{b}", incn=16)
                    wqg_ready[b] = P.v(f"gq{b}")
                routing_piece(j - 1)
                for ncs in range(4):
                    attn_mms(j, ncs, [])
                wqg_free[b] = P.v("pe")
                scan_dv = attn_tail(j)
            spq.close()
            plog = sla.enter_context(nc.psum_tensor("plog", [8, 1024], F32))

            P.emit("scalar", lambda E: E.activation(etok[:, :], m12all[:, :], AF.Exp),
                   waits=[("dv", m12_dv)], inc="ac")
            etok_ac = P.v("ac")
            P.emit("vector", lambda E: E.reduce_sum(
                part12[:, 0:1], etok[:, :].rearrange("p (i two) -> p two i", two=2)[:, 0, :],
                axis=AX), waits=[("ac", etok_ac)], inc="dv")
            P.emit("vector", lambda E: E.reduce_sum(
                part12[:, 1:2], etok[:, :].rearrange("p (i two) -> p two i", two=2)[:, 1, :],
                axis=AX), waits=[("dv", P.v("dv"))], inc="dv")
            part_dv = P.v("dv")
            with contextlib.ExitStack() as sco:
                pmsk = sco.enter_context(nc.psum_tensor("pmsk", [8, 1024], F32))
                psS = [sco.enter_context(nc.psum_tensor(f"psS{i}", [1, 1], F32)) for i in range(2)]
                for mk in range(2):
                    P.emit("tensor", lambda E, mk=mk: E.matmul(
                        psS[mk][:, :], onescol[:, :], part12[:, mk:mk + 1], start=True, stop=True),
                        waits=[("dv", part_dv)], inc="pe")
                sum_pe = P.v("pe")
                for oi in range(8):
                    P.emit("tensor", lambda E, oi=oi: E.transpose(
                        plog[0:8, oi * 128:(oi + 1) * 128],
                        maskall[:, oi * 16:oi * 16 + 8], id32[:, :]),
                        waits=[("dv", maskall_dv[oi])], inc="pe")
                    P.emit("tensor", lambda E, oi=oi: E.transpose(
                        pmsk[0:8, oi * 128:(oi + 1) * 128],
                        maskall[:, oi * 16 + 8:oi * 16 + 16], id32[:, :]), inc="pe")
                msk_pe = P.v("pe")
                P.emit("vector", lambda E: E.tensor_copy(M1s[:, :], plog[:, :]),
                       waits=[("pe", msk_pe)], inc="dv")
                P.emit("vector", lambda E: E.tensor_copy(M2s[:, :], pmsk[:, :]), inc="dv")
                P.emit("vector", lambda E: E.tensor_copy(Ssb[0][:, :], psS[0][:, :]),
                       waits=[("pe", sum_pe)], inc="dv")
                P.emit("vector", lambda E: E.tensor_copy(Ssb[1][:, :], psS[1][:, :]), inc="dv")
                coda1_dv = P.v("dv")
            with contextlib.ExitStack() as sco:
                pmr = [sco.enter_context(nc.psum_tensor(f"pmr{i}", [1, TOWN], F32)) for i in range(2)]
                for mk in range(2):
                    P.emit("tensor", lambda E, mk=mk: [
                        E.transpose(pmr[mk][0:1, oi * 128:(oi + 1) * 128],
                                    etok[:, 2 * (8 + oi) + mk:2 * (8 + oi) + mk + 1], id32[:, :])
                        for oi in range(8)][-1],
                        waits=[("ac", etok_ac), ("dv", coda1_dv)], inc="pe")
                row_pe = P.v("pe")
                for mk in range(2):
                    P.emit("vector", lambda E, mk=mk: E.tensor_copy(erow[mk][:, :], pmr[mk][0:1, :]),
                           waits=[("pe", row_pe)], inc="dv")
                P.emit("vector", lambda E: E.reciprocal(Ssb[0][:, :], Ssb[0][:, :]),
                       waits=[("dv", P.v("dv"))], inc="dv")
                P.emit("vector", lambda E: E.reciprocal(Ssb[1][:, :], Ssb[1][:, :]),
                       waits=[("dv", P.v("dv"))], inc="dv")
                P.emit("vector", lambda E: E.tensor_scalar(
                    w1own[:, :], erow[0][0:1, :], Ssb[0][:, :], None, ALU.mult),
                    waits=[("dv", P.v("dv"))], inc="dv")
                P.emit("vector", lambda E: E.tensor_scalar(
                    w2own[:, :], erow[1][0:1, :], Ssb[1][:, :], None, ALU.mult),
                    waits=[("dv", P.v("dv"))], inc="dv")
        rt_done = P.snap()
        at_done = rt_done
        s1.close()

        sR = es.enter_context(contextlib.ExitStack())
        hmix = sR.enter_context(nc.sbuf_tensor("hmix", [128, JH * CH], BF16, side="right"))
        sF = es.enter_context(contextlib.ExitStack())
        Qsb = sF.enter_context(nc.sbuf_tensor("Qsb", [128, CH], BF16))
        stage = [sF.enter_context(nc.sbuf_tensor(f"stage{i}", [128, D], F32)) for i in range(4)]

        def emit_ffn_chunk(c, prev, hmix_free_pe, stage_free_so):
            tloc = c * CH
            tglob = OWN0 + tloc
            bar = [(s, prev[s]) for s in ("pe", "dv", "ac", "sh0", "sh1")]
            with contextlib.ExitStack() as sup:
                NWF = 4
                wfn_bf = [sup.enter_context(nc.sbuf_tensor(f"wfnbf{c}_{i}", [128, 4480], BF16))
                          for i in range(NWF)]
                Mx = [sup.enter_context(nc.sbuf_tensor(f"Mx{c}_{i}", [128, CH], F32)) for i in range(2)]
                wbc = [sup.enter_context(nc.sbuf_tensor(f"wbc{c}_{i}", [128, CH], F32)) for i in range(2)]
                Pm = [sup.enter_context(nc.sbuf_tensor(f"Pm{c}_{i}", [128, CH], BF16)) for i in range(4)]
                Mxd = sup.enter_context(nc.sbuf_tensor(f"Mxd{c}", [128, CH], F32))
                uk = [sup.enter_context(nc.sbuf_tensor(f"uk{c}_{i}", [128, CH], F32)) for i in range(2)]
                gk = [sup.enter_context(nc.sbuf_tensor(f"gk{c}_{i}", [128, CH], F32)) for i in range(2)]
                sk = [sup.enter_context(nc.sbuf_tensor(f"sk{c}_{i}", [128, CH], F32)) for i in range(2)]
                hb = [[sup.enter_context(nc.sbuf_tensor(f"hb{c}_{k}_{i}", [128, CH], BF16))
                       for i in range(2)] for k in range(2)]
                pA = [sup.enter_context(nc.psum_tensor(f"pA{c}_{i}", [128, CH], F32)) for i in range(2)]
                pB = [sup.enter_context(nc.psum_tensor(f"pB{c}_{i}", [128, CH], F32)) for i in range(2)]
                pC = sup.enter_context(nc.psum_tensor(f"pC{c}", [128, CH], F32))
                pDg = sup.enter_context(nc.psum_tensor(f"pDg{c}", [128, CH], F32))
                pM = [sup.enter_context(nc.psum_tensor(f"pM{c}_{i}", [128, CH], F32)) for i in range(2)]

                P.emit("tensor", lambda E: [
                    E.matmul(pA[0][:, :], wsm[:, k * 128:(k + 1) * 128],
                             xn_slice(k, tglob, CH), start=(k == 0), stop=(k == KD - 1))
                    for k in range(KD)][-1], waits=[("dv", prev["dv"])], inc="pe")
                P.emit("tensor", lambda E: [
                    E.matmul(pB[0][:, :], wsm[:, 2048 + k * 128:2048 + (k + 1) * 128],
                             xn_slice(k, tglob, CH), start=(k == 0), stop=(k == KD - 1))
                    for k in range(KD)][-1], inc="pe")
                P.emit("tensor", lambda E: E.matmul(pA[1][:, :], expT[:, :],
                                                    M1s[0:8, tloc:tloc + CH], start=True, stop=True),
                       inc="pe")
                P.emit("tensor", lambda E: E.matmul(pB[1][:, :], expT[:, :],
                                                    M2s[0:8, tloc:tloc + CH], start=True, stop=True),
                       inc="pe")
                P.emit("tensor", lambda E: E.matmul(pM[0][:, :], ones1[:, :],
                                                    w1own[0:1, tloc:tloc + CH], start=True, stop=True),
                       inc="pe")
                P.emit("tensor", lambda E: E.matmul(pM[1][:, :], ones1[:, :],
                                                    w2own[0:1, tloc:tloc + CH], start=True, stop=True),
                       inc="pe")
                prep_pe = P.v("pe")
                P.emit("vector", lambda E: E.tensor_copy(Mx[0][:, :], pA[1][:, :]),
                       waits=[("pe", prep_pe)] + bar, inc="dv")
                P.emit("vector", lambda E: E.tensor_copy(Mx[1][:, :], pB[1][:, :]), inc="dv")
                P.emit("vector", lambda E: E.tensor_copy(wbc[0][:, :], pM[0][:, :]), inc="dv")
                P.emit("vector", lambda E: E.tensor_copy(wbc[1][:, :], pM[1][:, :]), inc="dv")
                P.emit("vector", lambda E: E.tensor_tensor(Mxd[:, :], Mx[1][:, :], Mx[0][:, :], ALU.subtract), inc="dv")
                P.emit("vector", lambda E: E.tensor_tensor(Pm[0][:, :], pA[0][:, :], Mx[0][:, :], ALU.mult), inc="dv")
                P.emit("vector", lambda E: E.tensor_tensor(Pm[1][:, :], pA[0][:, :], Mxd[:, :], ALU.mult), inc="dv")
                P.emit("vector", lambda E: E.tensor_tensor(Pm[2][:, :], pB[0][:, :], Mx[0][:, :], ALU.mult), inc="dv")
                P.emit("vector", lambda E: E.tensor_tensor(Pm[3][:, :], pB[0][:, :], Mxd[:, :], ALU.mult), inc="dv")
                prep_dv = P.v("dv")

                wfn_ready = [0] * NWF
                hmix_dv = [0] * JH
                mdown_pe = [0] * JH
                hdv2 = [0, 0]
                hb_free = [0, 0]
                delta_free_dv = prep_dv
                prev_tile_dv = prep_dv
                u1cp_ac = [0, 0]
                g1cp_ac = [0, 0]
                for j in range(JH + 1):
                    bj = j % 2
                    wj = j % NWF
                    if j < JH:
                        wfree = mdown_pe[j - NWF] if j >= NWF else 0
                        ph = ([("pe", prev["pe"]), ("dv", prev["dv"]), ("ac", prev["ac"])]
                              if j < NWF else [])
                        P.emit("gpsimd", lambda E, j=j, wj=wj: E.dma_start(
                            out=wfn_bf[wj][:, :], in_=wfn_in[j]),
                            waits=[("pe", wfree)] + ph, inc=f"gf{wj}", incn=16)
                        wfn_ready[wj] = P.v(f"gf{wj}")
                    if j < JH:
                        pp = [("dv", prep_dv)] if j < 2 else []
                        P.emit("tensor", lambda E, bj=bj, wj=wj: [
                            E.matmul(pA[bj][:, :], wfn_bf[wj][:, k * 128:(k + 1) * 128],
                                     xn_slice(k, tglob, CH), start=(k == 0), stop=False)
                            for k in range(KD)][-1],
                            waits=[("ac", u1cp_ac[bj]), (f"gf{wj}", wfn_ready[wj])] + pp,
                            inc="pe")
                        P.emit("tensor", lambda E, bj=bj, wj=wj: E.matmul(
                            pA[bj][:, :], wfn_bf[wj][:, 4096:4224], Pm[0][:, :],
                            start=False, stop=True), inc="pe")
                        bA_pe = P.v("pe")
                        P.emit("tensor", lambda E, bj=bj, wj=wj: [
                            E.matmul(pB[bj][:, :], wfn_bf[wj][:, 2048 + k * 128:2048 + (k + 1) * 128],
                                     xn_slice(k, tglob, CH), start=(k == 0), stop=False)
                            for k in range(KD)][-1],
                            waits=[("ac", g1cp_ac[bj])] + pp, inc="pe")
                        P.emit("tensor", lambda E, bj=bj, wj=wj: E.matmul(
                            pB[bj][:, :], wfn_bf[wj][:, 4224:4352], Pm[2][:, :],
                            start=False, stop=True), inc="pe")
                        bB_pe = P.v("pe")
                        P.emit("tensor", lambda E, bj=bj, wj=wj: E.matmul(
                            pC[:, :], wfn_bf[wj][:, 4096:4224], Pm[1][:, :],
                            start=True, stop=True),
                            waits=[("dv", delta_free_dv)] + pp, inc="pe")
                        P.emit("tensor", lambda E, bj=bj, wj=wj: E.matmul(
                            pDg[:, :], wfn_bf[wj][:, 4224:4352], Pm[3][:, :],
                            start=True, stop=True), inc="pe")
                        dj_pe = P.v("pe")
                    if j >= 1:
                        pj = j - 1
                        for mk in range(2):
                            P.emit("tensor", lambda E, pj=pj, mk=mk: E.matmul(
                                pM[mk][:, :], wfn_bf[pj % NWF][:, 4352:4480], hb[mk][pj % 2][:, :],
                                start=(pj == 0), stop=(pj == JH - 1)),
                                waits=[("dv", hdv2[pj % 2])] if mk == 0 else [], inc="pe")
                        mdown_pe[pj] = P.v("pe")
                    if j >= JH:
                        break
                    P.emit("scalar", lambda E, bj=bj: E.activation(sk[0][:, :], pA[bj][:, :], AF.Silu),
                           waits=[("pe", bA_pe), ("dv", prev_tile_dv)], inc="ac")
                    P.emit("scalar", lambda E, bj=bj: E.copy(uk[0][:, :], pA[bj][:, :]), inc="ac")
                    u1cp_ac[bj] = P.v("ac")
                    P.emit("scalar", lambda E, bj=bj: E.copy(gk[0][:, :], pB[bj][:, :]),
                           waits=[("pe", bB_pe)], inc="ac")
                    g1cp_ac[bj] = P.v("ac")
                    s1_ac = P.v("ac")
                    P.emit("vector", lambda E: E.tensor_tensor(
                        uk[1][:, :], uk[0][:, :], pC[:, :], ALU.add),
                        waits=[("ac", u1cp_ac[bj]), ("pe", dj_pe)], inc="dv")
                    P.emit("scalar", lambda E: E.activation(sk[1][:, :], uk[1][:, :], AF.Silu),
                           waits=[("dv", P.v("dv"))], inc="ac")
                    s2_ac = P.v("ac")
                    P.emit("vector", lambda E: E.tensor_tensor(
                        gk[1][:, :], gk[0][:, :], pDg[:, :], ALU.add),
                        waits=[("ac", g1cp_ac[bj])], inc="dv")
                    delta_free_dv = P.v("dv")
                    P.emit("vector", lambda E: E.tensor_tensor(
                        sk[0][:, :], sk[0][:, :], wbc[0][:, :], ALU.mult),
                        waits=[("ac", s1_ac)], inc="dv")
                    P.emit("vector", lambda E, bj=bj: E.tensor_tensor(
                        hb[0][bj][:, :], sk[0][:, :], gk[0][:, :], ALU.mult),
                        waits=[("pe", hb_free[bj])], inc="dv")
                    P.emit("vector", lambda E: E.tensor_tensor(
                        sk[1][:, :], sk[1][:, :], wbc[1][:, :], ALU.mult),
                        waits=[("ac", s2_ac)], inc="dv")
                    P.emit("vector", lambda E, bj=bj: E.tensor_tensor(
                        hb[1][bj][:, :], sk[1][:, :], gk[1][:, :], ALU.mult), inc="dv")
                    hdv2[bj] = P.v("dv")
                    P.emit("vector", lambda E, bj=bj, j=j: E.tensor_tensor(
                        hmix[:, j * CH:(j + 1) * CH], hb[0][bj][:, :], hb[1][bj][:, :], ALU.add),
                        waits=[("pe", hmix_free_pe)], inc="dv")
                    hmix_dv[j] = P.v("dv")
                    prev_tile_dv = P.v("dv")
                    if j >= 1:
                        hb_free[(j - 1) % 2] = mdown_pe[j - 1]
                mq_pe = P.v("pe")
                P.emit("vector", lambda E: E.tensor_tensor(uk[0][:, :], pM[0][:, :], Mx[0][:, :], ALU.mult),
                       waits=[("pe", mq_pe)], inc="dv")
                P.emit("vector", lambda E: E.tensor_tensor(uk[1][:, :], pM[1][:, :], Mx[1][:, :], ALU.mult),
                       inc="dv")
                P.emit("vector", lambda E: E.tensor_tensor(Qsb[:, :], uk[0][:, :], uk[1][:, :], ALU.add),
                       waits=[("dv", P.v("dv"))], inc="dv")
            up_done = P.snap()

            with contextlib.ExitStack() as sdn:
                NWD = 4
                wdn_bf = [sdn.enter_context(nc.sbuf_tensor(f"wdnbf{c}_{i}", [128, H], BF16))
                          for i in range(NWD)]
                hrel = [sdn.enter_context(nc.sbuf_tensor(f"hrel{c}_{i}", [128, CH], F32)) for i in range(2)]
                sumfm = [sdn.enter_context(nc.sbuf_tensor(f"sumfm{c}_{i}", [128, CH], F32)) for i in range(2)]
                pso = [sdn.enter_context(nc.psum_tensor(f"pso{c}_{i}", [128, CH], F32)) for i in range(2)]
                pstr = [sdn.enter_context(nc.psum_tensor(f"pstr{c}_{i}", [128, CH], F32)) for i in range(2)]

                bar2 = [(s, up_done[s]) for s in ("dv", "ac", "pe")]
                stage_sg = {}
                for tb in range(4):
                    P.emit("sync", lambda E, tb=tb: E.dma_start(
                        out=stage[tb][:, :],
                        in_=x_in[tglob + tb * 128:tglob + (tb + 1) * 128, :]),
                        waits=list(stage_free_so) + bar2, inc=f"sg{tb}", incn=16)
                    stage_sg[tb] = P.v(f"sg{tb}")
                mm_pe_arr = [0] * JD
                hrel_free = [0, 0]
                sum_free = [0, 0]
                pso_free = [0, 0]
                pstr_free = [0, 0]
                sf_dv = [0, 0]
                stage_dv = 0
                for jd in range(JD + 1):
                    bd = jd % 2
                    wd = jd % NWD
                    if jd < JD:
                        wfree = mm_pe_arr[jd - NWD] if jd >= NWD else 0
                        ph = bar2 if jd < NWD else []
                        P.emit("gpsimd", lambda E, jd=jd, wd=wd: E.dma_start(
                            out=wdn_bf[wd][:, :], in_=wdn_in[jd]),
                            waits=[("pe", wfree)] + ph, inc=f"gd{wd}", incn=16)
                        wd_tgt = P.v(f"gd{wd}")
                        P.emit("sync", lambda E, jd=jd, bd=bd: E.dma_start(
                            out=hrel[bd][:, :], in_=hscr[jd * 128:(jd + 1) * 128, tloc:tloc + CH]),
                            waits=[("dv", hrel_free[bd]), ("sh0", at_done["sh0"]),
                                   ("sh1", at_done["sh1"])] + ph,
                            inc=f"sr{bd}", incn=16)
                        h_tgt = P.v(f"sr{bd}")
                        P.emit("tensor", lambda E, bd=bd, wd=wd: [
                            E.matmul(pso[bd][:, :], wdn_bf[wd][:, k * 128:(k + 1) * 128],
                                     hmix[:, k * CH:(k + 1) * CH], start=(k == 0), stop=False)
                            for k in range(JH - 4)][-1],
                            waits=[("pe", up_done["pe"]), ("ac", up_done["ac"]),
                                   ("dv", hmix_dv[JH - 5]), (f"gd{wd}", wd_tgt),
                                   ("dv", pso_free[bd])], inc="pe")
                        P.emit("tensor", lambda E, bd=bd, wd=wd: [
                            E.matmul(pso[bd][:, :], wdn_bf[wd][:, k * 128:(k + 1) * 128],
                                     hmix[:, k * CH:(k + 1) * CH], start=False, stop=False)
                            for k in range(JH - 4, JH)][-1],
                            waits=[("dv", up_done["dv"])], inc="pe")
                        P.emit("tensor", lambda E, jd=jd, bd=bd: E.matmul(
                            pso[bd][:, :], wsm[:, 4096 + jd * 128:4096 + (jd + 1) * 128],
                            Qsb[:, :], start=False, stop=True), inc="pe")
                        mm_pe_arr[jd] = P.v("pe")
                    if jd >= 1:
                        pd = jd - 1
                        bp = pd % 2
                        P.emit("tensor", lambda E, bp=bp: [
                            E.transpose(pstr[bp][:, bb * 128:(bb + 1) * 128],
                                        sumfm[bp][:, bb * 128:(bb + 1) * 128], id32[:, :])
                            for bb in range(4)][-1],
                            waits=[("dv", sf_dv[bp]), ("dv", pstr_free[bp])], inc="pe")
                        tr_pe = P.v("pe")
                        sum_free[bp] = tr_pe
                    if jd < JD:
                        P.emit("vector", lambda E, bd=bd: E.tensor_tensor(
                            sumfm[bd][:, :], hrel[bd][:, :], pso[bd][:, :], ALU.add),
                            waits=[("pe", mm_pe_arr[jd]), (f"sr{bd}", h_tgt),
                                   ("pe", sum_free[bd])], inc="dv")
                        pso_free[bd] = P.v("dv")
                        hrel_free[bd] = P.v("dv")
                        sf_dv[bd] = P.v("dv")
                    if jd >= 1:
                        pd = jd - 1
                        bp = pd % 2
                        for bb in range(4):
                            P.emit("vector", lambda E, bb=bb, bp=bp, pd=pd: E.tensor_tensor(
                                stage[bb][:, pd * 128:(pd + 1) * 128],
                                stage[bb][:, pd * 128:(pd + 1) * 128],
                                pstr[bp][:, bb * 128:(bb + 1) * 128], ALU.add),
                                waits=[("pe", tr_pe), (f"sg{bb}", stage_sg[bb])], inc="dv")
                        pstr_free[bp] = P.v("dv")
                        stage_dv = P.v("dv")
                hmix_free_pe = P.v("pe")
                for tb in range(4):
                    P.emit("sync", lambda E, tb=tb: E.dma_start(
                        out=out_hbm[tloc + tb * 128:tloc + (tb + 1) * 128, :],
                        in_=stage[tb][:, :]),
                        waits=[("dv", stage_dv)], inc=f"so{c}", incn=16)
                stage_free_so = [(f"so{c}", P.v(f"so{c}"))]
            return P.snap(), hmix_free_pe, stage_free_so

        prev = rt_done
        hmix_free_pe = 0
        stage_free_so = []
        for c in range(2):
            prev, hmix_free_pe, stage_free_so = emit_ffn_chunk(
                c, prev, hmix_free_pe, stage_free_so)

        P.emit("sync", waits=[("so0", P.v("so0")), ("so1", P.v("so1"))])
        P.check()
        with nc.Block() as block:
            P.realize(block, sems)
    return nc


_CACHE = {}


def kernel(**inputs):
    x = np.ascontiguousarray(np.asarray(inputs["L0_attn_x"], np.float32))
    wqg, wfn, wdn, wsmall = _retile_inputs(inputs)
    nc = build_program()

    in_maps = []
    for c in range(8):
        b, half = c // 2, c % 2
        if half == 1:
            xc = np.ascontiguousarray(x[b])
            gfix = np.ones((128, 1), np.float32)
        else:
            xc = np.ascontiguousarray(np.concatenate([x[b][1024:], x[b][:1024]], axis=0))
            gfix = np.zeros((128, 1), np.float32)
        in_maps.append({"x": xc, "wqg": wqg, "wfn": wfn, "wdn": wdn,
                        "wsmall": wsmall, "gfix": gfix})

    res = run_bass_kernel_spmd(nc, in_maps, core_ids=list(range(8)))
    out = np.empty((4, S, D), np.float32)
    for c in range(8):
        b, half = c // 2, c % 2
        out[b, half * 1024:(half + 1) * 1024, :] = res.results[c]["out"]
    return out


if __name__ == "__main__":
    import reference
    ins = {k: np.asarray(v) for k, v in reference.setup_inputs().items()}
    got = kernel(**ins)
    want = np.asarray(reference.reference(**{k: v for k, v in ins.items()}))
    err = np.abs(got - want)
    rel = np.sqrt(((got - want) ** 2).mean() / (want ** 2).mean())
    print("max abs err:", err.max(), "rel l2:", rel)


# version 31
# speedup vs baseline: 1.0328x; 1.0328x over previous
import contextlib
import numpy as np
import ml_dtypes

import concourse.bass as bass
import concourse.mybir as mybir
from concourse.bass_utils import run_bass_kernel_spmd

F32 = mybir.dt.float32
BF16 = mybir.dt.bfloat16
AF = mybir.ActivationFunctionType
ALU = mybir.AluOpType
AX = mybir.AxisListType.X

D = 2048
S = 2048
H = 5632
NE, R = 8, 16
KD = 16
JH = 44
JD = 16
TOWN = 1024
OWN0 = 1024
CH = 512
ALPHA = 2.0

ENGS = ("sync", "gpsimd", "tensor", "vector", "scalar")
SEM_NAMES = ("pe", "dv", "ac", "si", "gs", "sx0", "sx1", "sh0", "sh1",
             "gq0", "gq1", "gf0", "gf1", "gf2", "gf3", "gd0", "gd1", "gd2", "gd3", "sr0", "sr1",
             "sg0", "sg1", "sg2", "sg3", "so0", "so1")


class Prog:

    def __init__(self):
        self.st = {e: [] for e in ENGS}
        self.cnt = {s: 0 for s in SEM_NAMES}
        self.audit = []

    def v(self, sem):
        return self.cnt[sem]

    def snap(self):
        return dict(self.cnt)

    def emit(self, eng, fn=None, waits=(), inc=None, incn=1):
        if fn is not None and eng in ("vector", "scalar"):
            own = "dv" if eng == "vector" else "ac"
            waits = list(waits) + [(own, self.cnt[own])]
            if inc is None:
                inc, incn = own, 1
            else:
                assert inc == own
        wd = {}
        for s, v in waits:
            if v > 0:
                wd[s] = max(wd.get(s, 0), v)
        waits = list(wd.items())
        if inc is not None:
            self.cnt[inc] += incn
        self.st[eng].append((fn, tuple(waits), inc, incn))
        self.audit.extend(waits)
        return self.cnt[inc] if inc else None

    def check(self):
        for s, v in self.audit:
            assert v <= self.cnt[s], f"wait {s}>={v} but final {self.cnt[s]} -> hang"

    def realize(self, block, sems):
        for eng in ENGS:
            stream = self.st[eng]
            if not stream:
                continue

            def body(E, stream=stream):
                for fn, waits, inc, incn in stream:
                    for s, v in waits:
                        E.wait_ge(sems[s], v)
                    inst = fn(E) if fn is not None else None
                    if inc is not None:
                        inst.then_inc(sems[inc], incn)

            getattr(block, eng)(body)


def _retile_inputs(ins):
    W = np.ascontiguousarray(np.asarray(ins["L0_attn_W"], np.float32))
    w_up = np.ascontiguousarray(np.asarray(ins["L0_ffn_w_up"], np.float32))
    w_gate = np.ascontiguousarray(np.asarray(ins["L0_ffn_w_gate"], np.float32))
    w_down = np.ascontiguousarray(np.asarray(ins["L0_ffn_w_down"], np.float32))
    up_A = np.asarray(ins["L0_ffn_up_A"], np.float32)
    up_B = np.asarray(ins["L0_ffn_up_B"], np.float32)
    gate_A = np.asarray(ins["L0_ffn_gate_A"], np.float32)
    gate_B = np.asarray(ins["L0_ffn_gate_B"], np.float32)
    down_A = np.asarray(ins["L0_ffn_down_A"], np.float32)
    down_B = np.asarray(ins["L0_ffn_down_B"], np.float32)
    gate_w = np.asarray(ins["L0_ffn_gate_w"], np.float32)

    wqg = np.ascontiguousarray(
        W.reshape(2, 16, 128, KD, 128).transpose(1, 4, 0, 3, 2).reshape(16, 128, 4096))
    wu = w_up.reshape(JH, 128, KD, 128).transpose(0, 3, 2, 1).reshape(JH, 128, 2048)
    wg = w_gate.reshape(JH, 128, KD, 128).transpose(0, 3, 2, 1).reshape(JH, 128, 2048)
    ub = up_B.transpose(0, 2, 1).reshape(128, JH, 128).transpose(1, 0, 2)
    gb = gate_B.transpose(0, 2, 1).reshape(128, JH, 128).transpose(1, 0, 2)
    da = down_A.transpose(2, 0, 1).reshape(JH, 128, 128)
    wfn = np.ascontiguousarray(np.concatenate([wu, wg, ub, gb, da], axis=2))
    wdn = np.ascontiguousarray(
        w_down.reshape(JD, 128, JH, 128).transpose(0, 3, 2, 1).reshape(JD, 128, H))
    ua = up_A.transpose(2, 0, 1).reshape(D, 128).reshape(KD, 128, 128).transpose(1, 0, 2).reshape(128, 2048)
    ga = gate_A.transpose(2, 0, 1).reshape(D, 128).reshape(KD, 128, 128).transpose(1, 0, 2).reshape(128, 2048)
    db = down_B.transpose(0, 2, 1).reshape(128, 2048)
    gwt = gate_w.T.reshape(KD, 128, 8).transpose(1, 0, 2).reshape(128, 128)
    wsmall = np.ascontiguousarray(np.concatenate([ua, ga, db, gwt], axis=1))
    return wqg, wfn, wdn, wsmall


def build_program():
    nc = bass.Bass("TRN2", target_bir_lowering=False, debug=False)
    x_in = nc.dram_tensor("x", [S, D], F32, kind="ExternalInput").ap()
    wqg_in = nc.dram_tensor("wqg", [16, 128, 4096], F32, kind="ExternalInput").ap()
    wfn_in = nc.dram_tensor("wfn", [JH, 128, 4480], F32, kind="ExternalInput").ap()
    wdn_in = nc.dram_tensor("wdn", [JD, 128, H], F32, kind="ExternalInput").ap()
    wsm_in = nc.dram_tensor("wsmall", [128, 6272], F32, kind="ExternalInput").ap()
    gfix_in = nc.dram_tensor("gfix", [128, 1], F32, kind="ExternalInput").ap()
    out_hbm = nc.dram_tensor("out", [TOWN, D], F32, kind="ExternalOutput").ap()
    hscr = nc.dram_tensor("hscr", [D, TOWN], F32, kind="Internal").ap()

    ident = np.eye(128, dtype=np.float32)
    id32_d = nc.inline_tensor(ident, name="id32c")
    idb_d = nc.inline_tensor(ident.astype(ml_dtypes.bfloat16), name="idbc")
    expt_np = np.zeros((8, 128), dtype=np.float32)
    for e in range(NE):
        expt_np[e, e * R:(e + 1) * R] = ALPHA
    expt_d = nc.inline_tensor(expt_np, name="exptc")
    ones1_d = nc.inline_tensor(np.ones((1, 128), np.float32), name="ones1c")
    onescol_d = nc.inline_tensor(np.ones((128, 1), np.float32), name="onescolc")

    P = Prog()
    es = contextlib.ExitStack()
    with es:
        C = es.enter_context(nc.sbuf_tensor("consts", [128, 8], F32))
        id32 = es.enter_context(nc.sbuf_tensor("id32", [128, 128], F32))
        idb = es.enter_context(nc.sbuf_tensor("idb", [128, 128], BF16))
        expT = es.enter_context(nc.sbuf_tensor("expT", [8, 128], F32))
        ones1 = es.enter_context(nc.sbuf_tensor("ones1", [1, 128], F32))
        onescol = es.enter_context(nc.sbuf_tensor("onescol", [128, 1], F32))
        gfix = es.enter_context(nc.sbuf_tensor("gfixs", [128, 1], F32))
        wsm = es.enter_context(nc.sbuf_tensor("wsm", [128, 6272], BF16))
        xnT_own = es.enter_context(nc.sbuf_tensor("xnT_own", [128, KD * TOWN], BF16))
        M1s = es.enter_context(nc.sbuf_tensor("M1s", [8, TOWN], F32))
        M2s = es.enter_context(nc.sbuf_tensor("M2s", [8, TOWN], F32))
        w1own = es.enter_context(nc.sbuf_tensor("w1own", [1, TOWN], F32))
        w2own = es.enter_context(nc.sbuf_tensor("w2own", [1, TOWN], F32))
        sems = {s: es.enter_context(nc.semaphore(s)) for s in SEM_NAMES}

        xnT_oth_holder = [None]

        def xn_slice(k, t0, w):
            if t0 >= OWN0:
                assert t0 + w <= OWN0 + TOWN
                return xnT_own[:, k * TOWN + (t0 - OWN0): k * TOWN + (t0 - OWN0) + w]
            assert t0 + w <= TOWN
            return xnT_oth_holder[0][:, k * TOWN + t0: k * TOWN + t0 + w]

        P.emit("sync", lambda E: E.dma_start(out=id32[:, :], in_=id32_d.ap()), inc="si", incn=16)
        P.emit("sync", lambda E: E.dma_start(out=idb[:, :], in_=idb_d.ap()), inc="si", incn=16)
        P.emit("sync", lambda E: E.dma_start(out=expT[:, :], in_=expt_d.ap()), inc="si", incn=16)
        P.emit("sync", lambda E: E.dma_start(out=ones1[:, :], in_=ones1_d.ap()), inc="si", incn=16)
        P.emit("sync", lambda E: E.dma_start(out=onescol[:, :], in_=onescol_d.ap()), inc="si", incn=16)
        P.emit("sync", lambda E: E.dma_start(out=gfix[:, :], in_=gfix_in), inc="si", incn=16)
        P.emit("gpsimd", lambda E: E.dma_start(out=wsm[:, :], in_=wsm_in), inc="gs", incn=16)
        for col, val in ((0, 1.0 / 2048.0), (1, 1e-5), (2, 1e-20), (3, 1.0), (4, -1e30), (5, -1.0)):
            P.emit("vector", lambda E, c=col, v=val: E.memset(C[:, c:c + 1], v),
                   inc="dv" if col == 5 else None)
        consts_dv = P.v("dv")
        init_si = P.v("si")

        s1 = contextlib.ExitStack()
        xnT_oth = s1.enter_context(nc.sbuf_tensor("xnT_oth", [128, KD * TOWN], BF16))
        xnT_oth_holder[0] = xnT_oth

        with contextlib.ExitStack() as sla:
            xld = [sla.enter_context(nc.sbuf_tensor(f"xld{i}", [128, D], F32)) for i in range(2)]
            xnb = [sla.enter_context(nc.sbuf_tensor(f"xnb{i}", [128, D], BF16)) for i in range(2)]
            lnjunk = sla.enter_context(nc.sbuf_tensor("lnjunk", [128, D], F32))
            lst = sla.enter_context(nc.sbuf_tensor("lnstats", [128, 16], F32))
            wqg_bf = [sla.enter_context(nc.sbuf_tensor(f"wqgbf{i}", [128, 4096], BF16)) for i in range(2)]
            qsb = sla.enter_context(nc.sbuf_tensor("qsb", [128, S], F32))
            gsb = sla.enter_context(nc.sbuf_tensor("gsb", [128, S], F32))
            hsb = [sla.enter_context(nc.sbuf_tensor(f"hsb{i}", [128, S], F32)) for i in range(2)]
            ltok = sla.enter_context(nc.sbuf_tensor("ltok", [128, 128], F32))
            m12all = sla.enter_context(nc.sbuf_tensor("m12all", [128, 32], F32))
            etok = sla.enter_context(nc.sbuf_tensor("etok", [128, 32], F32))
            l2buf = sla.enter_context(nc.sbuf_tensor("l2buf", [128, 8], F32))
            maskall = sla.enter_context(nc.sbuf_tensor("maskall", [128, 128], F32))
            mtmp = sla.enter_context(nc.sbuf_tensor("mtmp", [128, 8], F32))
            part12 = sla.enter_context(nc.sbuf_tensor("part12", [128, 2], F32))
            Ssb = [sla.enter_context(nc.sbuf_tensor(f"Ssb{i}", [1, 1], F32)) for i in range(2)]
            erow = [sla.enter_context(nc.sbuf_tensor(f"erow{i}", [1, TOWN], F32)) for i in range(2)]
            spq = contextlib.ExitStack()
            psq = spq.enter_context(nc.psum_tensor("psq", [128, 1024], F32, side="right"))
            psg = spq.enter_context(nc.psum_tensor("psg", [128, 2048], F32, side="right"))
            ptok = sla.enter_context(nc.psum_tensor("ptok", [128, 128], F32))
            strp = contextlib.ExitStack()
            trp = [strp.enter_context(nc.psum_tensor("trp0", [128, 1024], BF16))]

            P.emit("gpsimd", lambda E: E.dma_start(out=wqg_bf[0][:, :], in_=wqg_in[0]),
                   inc="gq0", incn=16)
            P.emit("gpsimd", lambda E: E.dma_start(out=wqg_bf[1][:, :], in_=wqg_in[1]),
                   inc="gq1", incn=16)
            wqg_ready = [P.v("gq0"), P.v("gq1")]
            wqg_free = [0, 0]
            bank_q = [0, 0]
            bank_g = [0, 0, 0, 0]
            q_pe = [0] * 4
            g_pe = [0] * 4
            evac_dv = [0] * 4
            evac_ac = [0] * 4
            hsb_free = [0, 0]
            scan_dv = 0

            def attn_mms(j, ncs, extra, defer_g=False):
                b = j % 2
                for qg in range(2):
                    ps = psq if qg == 0 else psg
                    gnc = ncs % 4
                    P.emit("tensor", lambda E, b=b, qg=qg, ncs=ncs, ps=ps, gnc=gnc: [
                        E.matmul(ps[:, ((ncs % 2) if qg == 0 else gnc) * 512:
                                    ((ncs % 2) if qg == 0 else gnc) * 512 + 512],
                                 wqg_bf[b][:, qg * 2048 + k * 128:qg * 2048 + (k + 1) * 128],
                                 xn_slice(k, ncs * 512, 512),
                                 start=(k == 0), stop=(k == KD - 1)) for k in range(KD)][-1],
                        waits=extra + [(f"gq{b}", wqg_ready[b])] +
                              ([("dv", bank_q[ncs % 2])] if qg == 0 else [("ac", bank_g[gnc])]),
                        inc="pe")
                    if qg == 0:
                        q_pe[ncs] = P.v("pe")
                    else:
                        g_pe[ncs] = P.v("pe")
                P.emit("vector", lambda E, ncs=ncs: E.tensor_scalar(
                    qsb[:, ncs * 512:(ncs + 1) * 512], psq[:, (ncs % 2) * 512:(ncs % 2 + 1) * 512],
                    C[:, 2:3], None, ALU.max),
                    waits=[("pe", q_pe[ncs]), ("dv", scan_dv)], inc="dv")
                evac_dv[ncs] = P.v("dv")
                bank_q[ncs % 2] = P.v("dv")
                if not defer_g:
                    P.emit("scalar", lambda E, ncs=ncs: E.activation(
                        gsb[:, ncs * 512:(ncs + 1) * 512],
                        psg[:, (ncs % 4) * 512:(ncs % 4 + 1) * 512], AF.Sigmoid),
                        waits=[("pe", g_pe[ncs]), ("dv", scan_dv)], inc="ac")
                    evac_ac[ncs] = P.v("ac")
                    bank_g[ncs % 4] = P.v("ac")

            def attn_tail(j):
                b = j % 2
                P.emit("vector", lambda E: E.tensor_scalar(
                    gsb[:, OWN0:OWN0 + 1], gsb[:, OWN0:OWN0 + 1], gfix[:, 0:1], None, ALU.mult),
                    waits=[("ac", evac_ac[3]), ("si", init_si)], inc="dv")
                P.emit("vector", lambda E, b=b: E.tensor_tensor_scan(
                    hsb[b][:, :], gsb[:, :], qsb[:, :], 0.0, ALU.mult, ALU.add),
                    waits=[("dv", P.v("dv")), ("ac", evac_ac[3]), (f"sh{b}", hsb_free[b])],
                    inc="dv")
                sc = P.v("dv")
                P.emit("sync", lambda E, j=j, b=b: E.dma_start(
                    out=hscr[j * 128:(j + 1) * 128, :], in_=hsb[b][:, OWN0:OWN0 + TOWN]),
                    waits=[("dv", sc)], inc=f"sh{b}", incn=16)
                hsb_free[b] = P.v(f"sh{b}")
                return sc

            xld_free = [0, 0]
            xnb_free = [0, 0]
            trp_free = [0]
            for i in range(16):
                b = i % 2
                P.emit("sync", lambda E, b=b, i=i: E.dma_start(
                    out=xld[b][:, :], in_=x_in[i * 128:(i + 1) * 128, :]),
                    waits=[("dv", xld_free[b])], inc=f"sx{b}", incn=16)
                ld_tgt = P.v(f"sx{b}")
                o8 = 8 * b
                P.emit("scalar", lambda E, b=b, o8=o8: E.activation(
                    lnjunk[:, :], xld[b][:, :], AF.Identity, accum_out=lst[:, o8:o8 + 1]),
                    waits=[(f"sx{b}", ld_tgt), ("dv", consts_dv)], inc="ac")
                P.emit("scalar", lambda E, o8=o8: E.activation(
                    lst[:, o8 + 1:o8 + 2], lst[:, o8:o8 + 1], AF.Identity, scale=-1.0 / 2048.0),
                    waits=[("ac", P.v("ac"))], inc="ac")
                P.emit("scalar", lambda E, b=b, o8=o8: E.activation(
                    lnjunk[:, :], xld[b][:, :], AF.Square, bias=lst[:, o8 + 1:o8 + 2],
                    accum_out=lst[:, o8 + 2:o8 + 3]),
                    waits=[("ac", P.v("ac"))], inc="ac")
                P.emit("scalar", lambda E, o8=o8: E.activation(
                    lst[:, o8 + 3:o8 + 4], lst[:, o8 + 2:o8 + 3], AF.Ln, bias=C[:, 1:2],
                    scale=1.0 / 2048.0),
                    waits=[("ac", P.v("ac"))], inc="ac")
                P.emit("scalar", lambda E, o8=o8: E.activation(
                    lst[:, o8 + 4:o8 + 5], lst[:, o8 + 3:o8 + 4], AF.Exp, scale=-0.5),
                    waits=[("ac", P.v("ac"))], inc="ac")
                P.emit("vector", lambda E, b=b, o8=o8: E.tensor_scalar(
                    xnb[b][:, :], xld[b][:, :], lst[:, o8 + 1:o8 + 2], lst[:, o8 + 4:o8 + 5],
                    ALU.add, ALU.mult),
                    waits=[("ac", P.v("ac")), ("pe", xnb_free[b])], inc="dv")
                xld_free[b] = P.v("dv")
                xn_dv = P.v("dv")
                dst = xnT_own if i >= 8 else xnT_oth
                off = (i - 8 if i >= 8 else i) * 128
                for hh in range(2):
                    P.emit("tensor", lambda E, b=b, hh=hh: [
                        E.transpose(trp[0][:, jj * 128:(jj + 1) * 128],
                                    xnb[b][:, (hh * 8 + jj) * 128:(hh * 8 + jj + 1) * 128],
                                    idb[:, :]) for jj in range(8)][-1],
                        waits=[("dv", xn_dv), ("dv", trp_free[0]), ("si", init_si)], inc="pe")
                    pe_tr = P.v("pe")
                    P.emit("vector", lambda E, hh=hh, dst=dst, off=off: E.tensor_copy(
                        dst[:, hh * 8 * TOWN:(hh * 8 + 8) * TOWN].rearrange(
                            "p (j t) -> p j t", j=8)[:, :, off:off + 128],
                        trp[0][:, :].rearrange("p (j t) -> p j t", j=8)),
                        waits=[("pe", pe_tr)], inc="dv")
                    trp_free[0] = P.v("dv")
                xnb_free[b] = P.v("pe")
                P.emit("tensor", lambda E, i=i: [
                    E.matmul(ptok[:, i * 8:(i + 1) * 8],
                             xn_slice(k, i * 128, 128),
                             wsm[:, 6144 + k * 8:6144 + (k + 1) * 8],
                             start=(k == 0), stop=(k == KD - 1))
                    for k in range(KD)][-1],
                    waits=[("gs", 16), ("dv", P.v("dv"))], inc="pe")
                if i % 4 == 3:
                    cseq = i // 4
                    attn_mms(0, cseq, [("dv", P.v("dv"))], defer_g=True)
            P.emit("scalar", lambda E: E.activation(gsb[:, :], psg[:, :], AF.Sigmoid),
                   waits=[("pe", g_pe[3])], inc="ac")
            for ncs in range(4):
                evac_ac[ncs] = P.v("ac")
                bank_g[ncs] = P.v("ac")
            scan_dv = attn_tail(0)
            wqg_free[0] = P.v("pe")
            lgt_pe = P.v("pe")
            ln_all_dv = P.v("dv")
            strp.close()

            ltok_dv = 0
            maskall_dv = [0] * 8
            m12_dv = 0

            def routing_piece(step):
                nonlocal ltok_dv, m12_dv
                if step == 0:
                    P.emit("vector", lambda E: E.tensor_copy(ltok[:, :], ptok[:, :]),
                           waits=[("pe", lgt_pe)], inc="dv")
                    ltok_dv = P.v("dv")
                elif 1 <= step <= 8:
                    for i in (2 * (step - 1), 2 * (step - 1) + 1):
                        own = i >= 8
                        m1dst = maskall[:, (i - 8) * 16:(i - 8) * 16 + 8] if own else mtmp[:, 0:8]
                        m2dst = maskall[:, (i - 8) * 16 + 8:(i - 8) * 16 + 16] if own else mtmp[:, 0:8]
                        L = ltok[:, i * 8:(i + 1) * 8]
                        P.emit("vector", lambda E, L=L, i=i: E.reduce_max(
                            m12all[:, 2 * i:2 * i + 1], L, axis=AX),
                            waits=[("dv", ltok_dv)], inc="dv")
                        P.emit("vector", lambda E, L=L, d=m1dst, i=i: E.tensor_scalar(
                            d, L, m12all[:, 2 * i:2 * i + 1], None, ALU.is_ge),
                            waits=[("dv", P.v("dv"))], inc="dv")
                        P.emit("vector", lambda E, L=L, d=m1dst: E.scalar_tensor_tensor(
                            l2buf[:, :], d, C[:, 4:5], L, ALU.mult, ALU.add),
                            waits=[("dv", P.v("dv"))], inc="dv")
                        P.emit("vector", lambda E, i=i: E.reduce_max(
                            m12all[:, 2 * i + 1:2 * i + 2], l2buf[:, :], axis=AX),
                            waits=[("dv", P.v("dv"))], inc="dv")
                        P.emit("vector", lambda E, d=m2dst, i=i: E.tensor_scalar(
                            d, l2buf[:, :], m12all[:, 2 * i + 1:2 * i + 2], None, ALU.is_ge),
                            waits=[("dv", P.v("dv"))], inc="dv")
                        if i >= 8:
                            maskall_dv[i - 8] = P.v("dv")
                    if step == 8:
                        m12_dv = P.v("dv")

            for j in range(1, 16):
                b = j % 2
                if j >= 2:
                    P.emit("gpsimd", lambda E, j=j, b=b: E.dma_start(
                        out=wqg_bf[b][:, :], in_=wqg_in[j]),
                        waits=[("pe", wqg_free[b])], inc=f"gq{b}", incn=16)
                    wqg_ready[b] = P.v(f"gq{b}")
                routing_piece(j - 1)
                for ncs in range(4):
                    attn_mms(j, ncs, [])
                wqg_free[b] = P.v("pe")
                scan_dv = attn_tail(j)
            spq.close()
            plog = sla.enter_context(nc.psum_tensor("plog", [8, 1024], F32))

            P.emit("scalar", lambda E: E.activation(etok[:, :], m12all[:, :], AF.Exp),
                   waits=[("dv", m12_dv)], inc="ac")
            etok_ac = P.v("ac")
            P.emit("vector", lambda E: E.reduce_sum(
                part12[:, 0:1], etok[:, :].rearrange("p (i two) -> p two i", two=2)[:, 0, :],
                axis=AX), waits=[("ac", etok_ac)], inc="dv")
            P.emit("vector", lambda E: E.reduce_sum(
                part12[:, 1:2], etok[:, :].rearrange("p (i two) -> p two i", two=2)[:, 1, :],
                axis=AX), waits=[("dv", P.v("dv"))], inc="dv")
            part_dv = P.v("dv")
            with contextlib.ExitStack() as sco:
                pmsk = sco.enter_context(nc.psum_tensor("pmsk", [8, 1024], F32))
                psS = [sco.enter_context(nc.psum_tensor(f"psS{i}", [1, 1], F32)) for i in range(2)]
                for mk in range(2):
                    P.emit("tensor", lambda E, mk=mk: E.matmul(
                        psS[mk][:, :], onescol[:, :], part12[:, mk:mk + 1], start=True, stop=True),
                        waits=[("dv", part_dv)], inc="pe")
                sum_pe = P.v("pe")
                for oi in range(8):
                    P.emit("tensor", lambda E, oi=oi: E.transpose(
                        plog[0:8, oi * 128:(oi + 1) * 128],
                        maskall[:, oi * 16:oi * 16 + 8], id32[:, :]),
                        waits=[("dv", maskall_dv[oi])], inc="pe")
                    P.emit("tensor", lambda E, oi=oi: E.transpose(
                        pmsk[0:8, oi * 128:(oi + 1) * 128],
                        maskall[:, oi * 16 + 8:oi * 16 + 16], id32[:, :]), inc="pe")
                msk_pe = P.v("pe")
                P.emit("vector", lambda E: E.tensor_copy(M1s[:, :], plog[:, :]),
                       waits=[("pe", msk_pe)], inc="dv")
                P.emit("vector", lambda E: E.tensor_copy(M2s[:, :], pmsk[:, :]), inc="dv")
                P.emit("vector", lambda E: E.tensor_copy(Ssb[0][:, :], psS[0][:, :]),
                       waits=[("pe", sum_pe)], inc="dv")
                P.emit("vector", lambda E: E.tensor_copy(Ssb[1][:, :], psS[1][:, :]), inc="dv")
                coda1_dv = P.v("dv")
            with contextlib.ExitStack() as sco:
                pmr = [sco.enter_context(nc.psum_tensor(f"pmr{i}", [1, TOWN], F32)) for i in range(2)]
                for mk in range(2):
                    P.emit("tensor", lambda E, mk=mk: [
                        E.transpose(pmr[mk][0:1, oi * 128:(oi + 1) * 128],
                                    etok[:, 2 * (8 + oi) + mk:2 * (8 + oi) + mk + 1], id32[:, :])
                        for oi in range(8)][-1],
                        waits=[("ac", etok_ac), ("dv", coda1_dv)], inc="pe")
                row_pe = P.v("pe")
                for mk in range(2):
                    P.emit("vector", lambda E, mk=mk: E.tensor_copy(erow[mk][:, :], pmr[mk][0:1, :]),
                           waits=[("pe", row_pe)], inc="dv")
                P.emit("vector", lambda E: E.reciprocal(Ssb[0][:, :], Ssb[0][:, :]),
                       waits=[("dv", P.v("dv"))], inc="dv")
                P.emit("vector", lambda E: E.reciprocal(Ssb[1][:, :], Ssb[1][:, :]),
                       waits=[("dv", P.v("dv"))], inc="dv")
                P.emit("vector", lambda E: E.tensor_scalar(
                    w1own[:, :], erow[0][0:1, :], Ssb[0][:, :], None, ALU.mult),
                    waits=[("dv", P.v("dv"))], inc="dv")
                P.emit("vector", lambda E: E.tensor_scalar(
                    w2own[:, :], erow[1][0:1, :], Ssb[1][:, :], None, ALU.mult),
                    waits=[("dv", P.v("dv"))], inc="dv")
        rt_done = P.snap()
        at_done = rt_done
        s1.close()

        sR = es.enter_context(contextlib.ExitStack())
        hmix = sR.enter_context(nc.sbuf_tensor("hmix", [128, JH * CH], BF16, side="right"))
        sF = es.enter_context(contextlib.ExitStack())
        Qsb = sF.enter_context(nc.sbuf_tensor("Qsb", [128, CH], BF16))
        stage = [sF.enter_context(nc.sbuf_tensor(f"stage{i}", [128, D], F32)) for i in range(4)]

        def emit_ffn_chunk(c, prev, hmix_free_pe, stage_free_so, aux=None):
            tloc = c * CH
            tglob = OWN0 + tloc
            bar = [(s, prev[s]) for s in ("pe", "dv", "ac", "sh0", "sh1")]
            with contextlib.ExitStack() as sup:
                NWF = 4
                wfn_bf = [sup.enter_context(nc.sbuf_tensor(f"wfnbf{c}_{i}", [128, 4480], BF16))
                          for i in range(NWF)]
                Mx = [sup.enter_context(nc.sbuf_tensor(f"Mx{c}_{i}", [128, CH], F32)) for i in range(2)]
                wbc = [sup.enter_context(nc.sbuf_tensor(f"wbc{c}_{i}", [128, CH], F32)) for i in range(2)]
                Pm = [sup.enter_context(nc.sbuf_tensor(f"Pm{c}_{i}", [128, CH], BF16)) for i in range(4)]
                Mxd = sup.enter_context(nc.sbuf_tensor(f"Mxd{c}", [128, CH], F32))
                uk = [sup.enter_context(nc.sbuf_tensor(f"uk{c}_{i}", [128, CH], F32)) for i in range(2)]
                gk = [sup.enter_context(nc.sbuf_tensor(f"gk{c}_{i}", [128, CH], F32)) for i in range(2)]
                sk = [sup.enter_context(nc.sbuf_tensor(f"sk{c}_{i}", [128, CH], F32)) for i in range(2)]
                hb = [[sup.enter_context(nc.sbuf_tensor(f"hb{c}_{k}_{i}", [128, CH], BF16))
                       for i in range(2)] for k in range(2)]
                pA = [sup.enter_context(nc.psum_tensor(f"pA{c}_{i}", [128, CH], F32)) for i in range(2)]
                pB = [sup.enter_context(nc.psum_tensor(f"pB{c}_{i}", [128, CH], F32)) for i in range(2)]
                pC = sup.enter_context(nc.psum_tensor(f"pC{c}", [128, CH], F32))
                pDg = sup.enter_context(nc.psum_tensor(f"pDg{c}", [128, CH], F32))
                pM = [sup.enter_context(nc.psum_tensor(f"pM{c}_{i}", [128, CH], F32)) for i in range(2)]

                P.emit("tensor", lambda E: [
                    E.matmul(pA[0][:, :], wsm[:, k * 128:(k + 1) * 128],
                             xn_slice(k, tglob, CH), start=(k == 0), stop=(k == KD - 1))
                    for k in range(KD)][-1], waits=[("dv", prev["dv"])], inc="pe")
                P.emit("tensor", lambda E: [
                    E.matmul(pB[0][:, :], wsm[:, 2048 + k * 128:2048 + (k + 1) * 128],
                             xn_slice(k, tglob, CH), start=(k == 0), stop=(k == KD - 1))
                    for k in range(KD)][-1], inc="pe")
                P.emit("tensor", lambda E: E.matmul(pA[1][:, :], expT[:, :],
                                                    M1s[0:8, tloc:tloc + CH], start=True, stop=True),
                       inc="pe")
                P.emit("tensor", lambda E: E.matmul(pB[1][:, :], expT[:, :],
                                                    M2s[0:8, tloc:tloc + CH], start=True, stop=True),
                       inc="pe")
                P.emit("tensor", lambda E: E.matmul(pM[0][:, :], ones1[:, :],
                                                    w1own[0:1, tloc:tloc + CH], start=True, stop=True),
                       inc="pe")
                P.emit("tensor", lambda E: E.matmul(pM[1][:, :], ones1[:, :],
                                                    w2own[0:1, tloc:tloc + CH], start=True, stop=True),
                       inc="pe")
                prep_pe = P.v("pe")
                P.emit("vector", lambda E: E.tensor_copy(Mx[0][:, :], pA[1][:, :]),
                       waits=[("pe", prep_pe)] + bar, inc="dv")
                P.emit("vector", lambda E: E.tensor_copy(Mx[1][:, :], pB[1][:, :]), inc="dv")
                P.emit("vector", lambda E: E.tensor_copy(wbc[0][:, :], pM[0][:, :]), inc="dv")
                P.emit("vector", lambda E: E.tensor_copy(wbc[1][:, :], pM[1][:, :]), inc="dv")
                P.emit("vector", lambda E: E.tensor_tensor(Mxd[:, :], Mx[1][:, :], Mx[0][:, :], ALU.subtract), inc="dv")
                P.emit("vector", lambda E: E.tensor_tensor(Pm[0][:, :], pA[0][:, :], Mx[0][:, :], ALU.mult), inc="dv")
                P.emit("vector", lambda E: E.tensor_tensor(Pm[1][:, :], pA[0][:, :], Mxd[:, :], ALU.mult), inc="dv")
                P.emit("vector", lambda E: E.tensor_tensor(Pm[2][:, :], pB[0][:, :], Mx[0][:, :], ALU.mult), inc="dv")
                P.emit("vector", lambda E: E.tensor_tensor(Pm[3][:, :], pB[0][:, :], Mxd[:, :], ALU.mult), inc="dv")
                prep_dv = P.v("dv")

                wfn_ready = [0] * NWF
                hmix_dv = [0] * JH
                mdown_pe = [0] * JH
                hdv2 = [0, 0]
                hb_free = [0, 0]
                delta_free_dv = prep_dv
                prev_tile_dv = prep_dv
                u1cp_ac = [0, 0]
                g1cp_ac = [0, 0]
                for j in range(JH + 1):
                    bj = j % 2
                    wj = j % NWF
                    if j < JH:
                        wfree = mdown_pe[j - NWF] if j >= NWF else 0
                        if j >= NWF:
                            ph = []
                        elif aux is not None:
                            ph = [("pe", aux[j])]
                        else:
                            ph = [("pe", prev["pe"]), ("dv", prev["dv"]), ("ac", prev["ac"])]
                        P.emit("gpsimd", lambda E, j=j, wj=wj: E.dma_start(
                            out=wfn_bf[wj][:, :], in_=wfn_in[j]),
                            waits=[("pe", wfree)] + ph, inc=f"gf{wj}", incn=16)
                        wfn_ready[wj] = P.v(f"gf{wj}")
                    if j < JH:
                        pp = [("dv", prep_dv)] if j < 2 else []
                        P.emit("tensor", lambda E, bj=bj, wj=wj: [
                            E.matmul(pA[bj][:, :], wfn_bf[wj][:, k * 128:(k + 1) * 128],
                                     xn_slice(k, tglob, CH), start=(k == 0), stop=False)
                            for k in range(KD)][-1],
                            waits=[("ac", u1cp_ac[bj]), (f"gf{wj}", wfn_ready[wj])] + pp,
                            inc="pe")
                        P.emit("tensor", lambda E, bj=bj, wj=wj: E.matmul(
                            pA[bj][:, :], wfn_bf[wj][:, 4096:4224], Pm[0][:, :],
                            start=False, stop=True), inc="pe")
                        bA_pe = P.v("pe")
                        P.emit("tensor", lambda E, bj=bj, wj=wj: [
                            E.matmul(pB[bj][:, :], wfn_bf[wj][:, 2048 + k * 128:2048 + (k + 1) * 128],
                                     xn_slice(k, tglob, CH), start=(k == 0), stop=False)
                            for k in range(KD)][-1],
                            waits=[("ac", g1cp_ac[bj])] + pp, inc="pe")
                        P.emit("tensor", lambda E, bj=bj, wj=wj: E.matmul(
                            pB[bj][:, :], wfn_bf[wj][:, 4224:4352], Pm[2][:, :],
                            start=False, stop=True), inc="pe")
                        bB_pe = P.v("pe")
                        P.emit("tensor", lambda E, bj=bj, wj=wj: E.matmul(
                            pC[:, :], wfn_bf[wj][:, 4096:4224], Pm[1][:, :],
                            start=True, stop=True),
                            waits=[("dv", delta_free_dv)] + pp, inc="pe")
                        P.emit("tensor", lambda E, bj=bj, wj=wj: E.matmul(
                            pDg[:, :], wfn_bf[wj][:, 4224:4352], Pm[3][:, :],
                            start=True, stop=True), inc="pe")
                        dj_pe = P.v("pe")
                    if j >= 1:
                        pj = j - 1
                        for mk in range(2):
                            P.emit("tensor", lambda E, pj=pj, mk=mk: E.matmul(
                                pM[mk][:, :], wfn_bf[pj % NWF][:, 4352:4480], hb[mk][pj % 2][:, :],
                                start=(pj == 0), stop=(pj == JH - 1)),
                                waits=[("dv", hdv2[pj % 2])] if mk == 0 else [], inc="pe")
                        mdown_pe[pj] = P.v("pe")
                    if j >= JH:
                        break
                    P.emit("scalar", lambda E, bj=bj: E.activation(sk[0][:, :], pA[bj][:, :], AF.Silu),
                           waits=[("pe", bA_pe), ("dv", prev_tile_dv)], inc="ac")
                    P.emit("scalar", lambda E, bj=bj: E.copy(uk[0][:, :], pA[bj][:, :]), inc="ac")
                    u1cp_ac[bj] = P.v("ac")
                    P.emit("scalar", lambda E, bj=bj: E.copy(gk[0][:, :], pB[bj][:, :]),
                           waits=[("pe", bB_pe)], inc="ac")
                    g1cp_ac[bj] = P.v("ac")
                    s1_ac = P.v("ac")
                    P.emit("vector", lambda E: E.tensor_tensor(
                        uk[1][:, :], uk[0][:, :], pC[:, :], ALU.add),
                        waits=[("ac", u1cp_ac[bj]), ("pe", dj_pe)], inc="dv")
                    P.emit("scalar", lambda E: E.activation(sk[1][:, :], uk[1][:, :], AF.Silu),
                           waits=[("dv", P.v("dv"))], inc="ac")
                    s2_ac = P.v("ac")
                    P.emit("vector", lambda E: E.tensor_tensor(
                        gk[1][:, :], gk[0][:, :], pDg[:, :], ALU.add),
                        waits=[("ac", g1cp_ac[bj])], inc="dv")
                    delta_free_dv = P.v("dv")
                    P.emit("vector", lambda E: E.tensor_tensor(
                        sk[0][:, :], sk[0][:, :], wbc[0][:, :], ALU.mult),
                        waits=[("ac", s1_ac)], inc="dv")
                    P.emit("vector", lambda E, bj=bj: E.tensor_tensor(
                        hb[0][bj][:, :], sk[0][:, :], gk[0][:, :], ALU.mult),
                        waits=[("pe", hb_free[bj])], inc="dv")
                    P.emit("vector", lambda E: E.tensor_tensor(
                        sk[1][:, :], sk[1][:, :], wbc[1][:, :], ALU.mult),
                        waits=[("ac", s2_ac)], inc="dv")
                    P.emit("vector", lambda E, bj=bj: E.tensor_tensor(
                        hb[1][bj][:, :], sk[1][:, :], gk[1][:, :], ALU.mult), inc="dv")
                    hdv2[bj] = P.v("dv")
                    P.emit("vector", lambda E, bj=bj, j=j: E.tensor_tensor(
                        hmix[:, j * CH:(j + 1) * CH], hb[0][bj][:, :], hb[1][bj][:, :], ALU.add),
                        waits=[("pe", hmix_free_pe)], inc="dv")
                    hmix_dv[j] = P.v("dv")
                    prev_tile_dv = P.v("dv")
                    if j >= 1:
                        hb_free[(j - 1) % 2] = mdown_pe[j - 1]
                mq_pe = P.v("pe")
                P.emit("vector", lambda E: E.tensor_tensor(uk[0][:, :], pM[0][:, :], Mx[0][:, :], ALU.mult),
                       waits=[("pe", mq_pe)], inc="dv")
                P.emit("vector", lambda E: E.tensor_tensor(uk[1][:, :], pM[1][:, :], Mx[1][:, :], ALU.mult),
                       inc="dv")
                P.emit("vector", lambda E: E.tensor_tensor(Qsb[:, :], uk[0][:, :], uk[1][:, :], ALU.add),
                       waits=[("dv", P.v("dv"))], inc="dv")
            up_done = P.snap()

            with contextlib.ExitStack() as sdn:
                NWD = 4
                wdn_bf = [sdn.enter_context(nc.sbuf_tensor(f"wdnbf{c}_{i}", [128, H], BF16))
                          for i in range(NWD)]
                hrel = [sdn.enter_context(nc.sbuf_tensor(f"hrel{c}_{i}", [128, CH], F32)) for i in range(2)]
                sumfm = [sdn.enter_context(nc.sbuf_tensor(f"sumfm{c}_{i}", [128, CH], F32)) for i in range(2)]
                pso = [sdn.enter_context(nc.psum_tensor(f"pso{c}_{i}", [128, CH], F32)) for i in range(2)]
                pstr = [sdn.enter_context(nc.psum_tensor(f"pstr{c}_{i}", [128, CH], F32)) for i in range(2)]

                bar2 = [(s, up_done[s]) for s in ("dv", "ac", "pe")]
                stage_sg = {}
                for tb in range(4):
                    P.emit("sync", lambda E, tb=tb: E.dma_start(
                        out=stage[tb][:, :],
                        in_=x_in[tglob + tb * 128:tglob + (tb + 1) * 128, :]),
                        waits=list(stage_free_so) + bar2, inc=f"sg{tb}", incn=16)
                    stage_sg[tb] = P.v(f"sg{tb}")
                mm_pe_arr = [0] * JD
                hrel_free = [0, 0]
                sum_free = [0, 0]
                pso_free = [0, 0]
                pstr_free = [0, 0]
                sf_dv = [0, 0]
                stage_dv = 0
                for jd in range(JD + 1):
                    bd = jd % 2
                    wd = jd % NWD
                    if jd < JD:
                        if jd >= NWD:
                            wwaits = [("pe", mm_pe_arr[jd - NWD])]
                        elif jd < NWD - 1:
                            wwaits = [("pe", mdown_pe[JH - NWD + jd])]
                        else:
                            wwaits = [("dv", up_done["dv"])]
                        P.emit("gpsimd", lambda E, jd=jd, wd=wd: E.dma_start(
                            out=wdn_bf[wd][:, :], in_=wdn_in[jd]),
                            waits=wwaits, inc=f"gd{wd}", incn=16)
                        wd_tgt = P.v(f"gd{wd}")
                        P.emit("sync", lambda E, jd=jd, bd=bd: E.dma_start(
                            out=hrel[bd][:, :], in_=hscr[jd * 128:(jd + 1) * 128, tloc:tloc + CH]),
                            waits=[("dv", hrel_free[bd]), ("sh0", at_done["sh0"]),
                                   ("sh1", at_done["sh1"])] + ph,
                            inc=f"sr{bd}", incn=16)
                        h_tgt = P.v(f"sr{bd}")
                        P.emit("tensor", lambda E, bd=bd, wd=wd: [
                            E.matmul(pso[bd][:, :], wdn_bf[wd][:, k * 128:(k + 1) * 128],
                                     hmix[:, k * CH:(k + 1) * CH], start=(k == 0), stop=False)
                            for k in range(JH - 4)][-1],
                            waits=[("pe", up_done["pe"]), ("ac", up_done["ac"]),
                                   ("dv", hmix_dv[JH - 5]), (f"gd{wd}", wd_tgt),
                                   ("dv", pso_free[bd])], inc="pe")
                        P.emit("tensor", lambda E, bd=bd, wd=wd: [
                            E.matmul(pso[bd][:, :], wdn_bf[wd][:, k * 128:(k + 1) * 128],
                                     hmix[:, k * CH:(k + 1) * CH], start=False, stop=False)
                            for k in range(JH - 4, JH)][-1],
                            waits=[("dv", up_done["dv"])], inc="pe")
                        P.emit("tensor", lambda E, jd=jd, bd=bd: E.matmul(
                            pso[bd][:, :], wsm[:, 4096 + jd * 128:4096 + (jd + 1) * 128],
                            Qsb[:, :], start=False, stop=True), inc="pe")
                        mm_pe_arr[jd] = P.v("pe")
                    if jd >= 1:
                        pd = jd - 1
                        bp = pd % 2
                        P.emit("tensor", lambda E, bp=bp: [
                            E.transpose(pstr[bp][:, bb * 128:(bb + 1) * 128],
                                        sumfm[bp][:, bb * 128:(bb + 1) * 128], id32[:, :])
                            for bb in range(4)][-1],
                            waits=[("dv", sf_dv[bp]), ("dv", pstr_free[bp])], inc="pe")
                        tr_pe = P.v("pe")
                        sum_free[bp] = tr_pe
                    if jd < JD:
                        P.emit("vector", lambda E, bd=bd: E.tensor_tensor(
                            sumfm[bd][:, :], hrel[bd][:, :], pso[bd][:, :], ALU.add),
                            waits=[("pe", mm_pe_arr[jd]), (f"sr{bd}", h_tgt),
                                   ("pe", sum_free[bd])], inc="dv")
                        pso_free[bd] = P.v("dv")
                        hrel_free[bd] = P.v("dv")
                        sf_dv[bd] = P.v("dv")
                    if jd >= 1:
                        pd = jd - 1
                        bp = pd % 2
                        for bb in range(4):
                            P.emit("vector", lambda E, bb=bb, bp=bp, pd=pd: E.tensor_tensor(
                                stage[bb][:, pd * 128:(pd + 1) * 128],
                                stage[bb][:, pd * 128:(pd + 1) * 128],
                                pstr[bp][:, bb * 128:(bb + 1) * 128], ALU.add),
                                waits=[("pe", tr_pe), (f"sg{bb}", stage_sg[bb])], inc="dv")
                        pstr_free[bp] = P.v("dv")
                        stage_dv = P.v("dv")
                hmix_free_pe = P.v("pe")
                for tb in range(4):
                    P.emit("sync", lambda E, tb=tb: E.dma_start(
                        out=out_hbm[tloc + tb * 128:tloc + (tb + 1) * 128, :],
                        in_=stage[tb][:, :]),
                        waits=[("dv", stage_dv)], inc=f"so{c}", incn=16)
                stage_free_so = [(f"so{c}", P.v(f"so{c}"))]
            return P.snap(), hmix_free_pe, stage_free_so, [
                mm_pe_arr[JD - NWD + s] for s in range(NWD)]

        prev = rt_done
        hmix_free_pe = 0
        stage_free_so = []
        aux = None
        for c in range(2):
            prev, hmix_free_pe, stage_free_so, aux = emit_ffn_chunk(
                c, prev, hmix_free_pe, stage_free_so, aux)

        P.emit("sync", waits=[("so0", P.v("so0")), ("so1", P.v("so1"))])
        P.check()
        with nc.Block() as block:
            P.realize(block, sems)
    return nc


_CACHE = {}


def kernel(**inputs):
    x = np.ascontiguousarray(np.asarray(inputs["L0_attn_x"], np.float32))
    wqg, wfn, wdn, wsmall = _retile_inputs(inputs)
    nc = build_program()

    in_maps = []
    for c in range(8):
        b, half = c // 2, c % 2
        if half == 1:
            xc = np.ascontiguousarray(x[b])
            gfix = np.ones((128, 1), np.float32)
        else:
            xc = np.ascontiguousarray(np.concatenate([x[b][1024:], x[b][:1024]], axis=0))
            gfix = np.zeros((128, 1), np.float32)
        in_maps.append({"x": xc, "wqg": wqg, "wfn": wfn, "wdn": wdn,
                        "wsmall": wsmall, "gfix": gfix})

    res = run_bass_kernel_spmd(nc, in_maps, core_ids=list(range(8)))
    out = np.empty((4, S, D), np.float32)
    for c in range(8):
        b, half = c // 2, c % 2
        out[b, half * 1024:(half + 1) * 1024, :] = res.results[c]["out"]
    return out


if __name__ == "__main__":
    import reference
    ins = {k: np.asarray(v) for k, v in reference.setup_inputs().items()}
    got = kernel(**ins)
    want = np.asarray(reference.reference(**{k: v for k, v in ins.items()}))
    err = np.abs(got - want)
    rel = np.sqrt(((got - want) ** 2).mean() / (want ** 2).mean())
    print("max abs err:", err.max(), "rel l2:", rel)


# version 32
# speedup vs baseline: 1.0617x; 1.0280x over previous
import contextlib
import numpy as np
import ml_dtypes

import concourse.bass as bass
import concourse.mybir as mybir
from concourse.bass_utils import run_bass_kernel_spmd

F32 = mybir.dt.float32
BF16 = mybir.dt.bfloat16
AF = mybir.ActivationFunctionType
ALU = mybir.AluOpType
AX = mybir.AxisListType.X

D = 2048
S = 2048
H = 5632
NE, R = 8, 16
KD = 16
JH = 44
JD = 16
TOWN = 1024
OWN0 = 1024
CH = 512
ALPHA = 2.0

ENGS = ("sync", "gpsimd", "tensor", "vector", "scalar")
SEM_NAMES = ("pe", "dv", "ac", "si", "gs", "sx0", "sx1", "sh0", "sh1",
             "gq0", "gq1", "gf0", "gf1", "gf2", "gf3", "gd0", "gd1", "gd2", "gd3", "sr0", "sr1",
             "sg0", "sg1", "sg2", "sg3", "so0", "so1")


class Prog:

    def __init__(self):
        self.st = {e: [] for e in ENGS}
        self.cnt = {s: 0 for s in SEM_NAMES}
        self.audit = []

    def v(self, sem):
        return self.cnt[sem]

    def snap(self):
        return dict(self.cnt)

    def emit(self, eng, fn=None, waits=(), inc=None, incn=1):
        if fn is not None and eng in ("vector", "scalar"):
            own = "dv" if eng == "vector" else "ac"
            waits = list(waits) + [(own, self.cnt[own])]
            if inc is None:
                inc, incn = own, 1
            else:
                assert inc == own
        wd = {}
        for s, v in waits:
            if v > 0:
                wd[s] = max(wd.get(s, 0), v)
        waits = list(wd.items())
        if inc is not None:
            self.cnt[inc] += incn
        self.st[eng].append((fn, tuple(waits), inc, incn))
        self.audit.extend(waits)
        return self.cnt[inc] if inc else None

    def check(self):
        for s, v in self.audit:
            assert v <= self.cnt[s], f"wait {s}>={v} but final {self.cnt[s]} -> hang"

    def realize(self, block, sems):
        for eng in ENGS:
            stream = self.st[eng]
            if not stream:
                continue

            def body(E, stream=stream):
                for fn, waits, inc, incn in stream:
                    for s, v in waits:
                        E.wait_ge(sems[s], v)
                    inst = fn(E) if fn is not None else None
                    if inc is not None:
                        inst.then_inc(sems[inc], incn)

            getattr(block, eng)(body)


def _retile_inputs(ins):
    W = np.ascontiguousarray(np.asarray(ins["L0_attn_W"], np.float32))
    w_up = np.ascontiguousarray(np.asarray(ins["L0_ffn_w_up"], np.float32))
    w_gate = np.ascontiguousarray(np.asarray(ins["L0_ffn_w_gate"], np.float32))
    w_down = np.ascontiguousarray(np.asarray(ins["L0_ffn_w_down"], np.float32))
    up_A = np.asarray(ins["L0_ffn_up_A"], np.float32)
    up_B = np.asarray(ins["L0_ffn_up_B"], np.float32)
    gate_A = np.asarray(ins["L0_ffn_gate_A"], np.float32)
    gate_B = np.asarray(ins["L0_ffn_gate_B"], np.float32)
    down_A = np.asarray(ins["L0_ffn_down_A"], np.float32)
    down_B = np.asarray(ins["L0_ffn_down_B"], np.float32)
    gate_w = np.asarray(ins["L0_ffn_gate_w"], np.float32)

    wqg = np.ascontiguousarray(
        W.reshape(2, 16, 128, KD, 128).transpose(1, 4, 0, 3, 2).reshape(16, 128, 4096))
    wu = w_up.reshape(JH, 128, KD, 128).transpose(0, 3, 2, 1).reshape(JH, 128, 2048)
    wg = w_gate.reshape(JH, 128, KD, 128).transpose(0, 3, 2, 1).reshape(JH, 128, 2048)
    ub = up_B.transpose(0, 2, 1).reshape(128, JH, 128).transpose(1, 0, 2)
    gb = gate_B.transpose(0, 2, 1).reshape(128, JH, 128).transpose(1, 0, 2)
    da = down_A.transpose(2, 0, 1).reshape(JH, 128, 128)
    wfn = np.ascontiguousarray(np.concatenate([wu, wg, ub, gb, da], axis=2))
    wdn = np.ascontiguousarray(
        w_down.reshape(JD, 128, JH, 128).transpose(0, 3, 2, 1).reshape(JD, 128, H))
    ua = up_A.transpose(2, 0, 1).reshape(D, 128).reshape(KD, 128, 128).transpose(1, 0, 2).reshape(128, 2048)
    ga = gate_A.transpose(2, 0, 1).reshape(D, 128).reshape(KD, 128, 128).transpose(1, 0, 2).reshape(128, 2048)
    db = down_B.transpose(0, 2, 1).reshape(128, 2048)
    gwt = gate_w.T.reshape(KD, 128, 8).transpose(1, 0, 2).reshape(128, 128)
    wsmall = np.ascontiguousarray(np.concatenate([ua, ga, db, gwt], axis=1))
    return wqg, wfn, wdn, wsmall


def build_program():
    nc = bass.Bass("TRN2", target_bir_lowering=False, debug=False)
    x_in = nc.dram_tensor("x", [S, D], F32, kind="ExternalInput").ap()
    wqg_in = nc.dram_tensor("wqg", [16, 128, 4096], F32, kind="ExternalInput").ap()
    wfn_in = nc.dram_tensor("wfn", [JH, 128, 4480], F32, kind="ExternalInput").ap()
    wdn_in = nc.dram_tensor("wdn", [JD, 128, H], F32, kind="ExternalInput").ap()
    wsm_in = nc.dram_tensor("wsmall", [128, 6272], F32, kind="ExternalInput").ap()
    gfix_in = nc.dram_tensor("gfix", [128, 1], F32, kind="ExternalInput").ap()
    out_hbm = nc.dram_tensor("out", [TOWN, D], F32, kind="ExternalOutput").ap()
    hscr = nc.dram_tensor("hscr", [D, TOWN], F32, kind="Internal").ap()

    ident = np.eye(128, dtype=np.float32)
    id32_d = nc.inline_tensor(ident, name="id32c")
    idb_d = nc.inline_tensor(ident.astype(ml_dtypes.bfloat16), name="idbc")
    expt_np = np.zeros((8, 128), dtype=np.float32)
    for e in range(NE):
        expt_np[e, e * R:(e + 1) * R] = ALPHA
    expt_d = nc.inline_tensor(expt_np, name="exptc")
    ones1_d = nc.inline_tensor(np.ones((1, 128), np.float32), name="ones1c")
    onescol_d = nc.inline_tensor(np.ones((128, 1), np.float32), name="onescolc")

    P = Prog()
    es = contextlib.ExitStack()
    with es:
        C = es.enter_context(nc.sbuf_tensor("consts", [128, 8], F32))
        id32 = es.enter_context(nc.sbuf_tensor("id32", [128, 128], F32))
        idb = es.enter_context(nc.sbuf_tensor("idb", [128, 128], BF16))
        expT = es.enter_context(nc.sbuf_tensor("expT", [8, 128], F32))
        ones1 = es.enter_context(nc.sbuf_tensor("ones1", [1, 128], F32))
        onescol = es.enter_context(nc.sbuf_tensor("onescol", [128, 1], F32))
        gfix = es.enter_context(nc.sbuf_tensor("gfixs", [128, 1], F32))
        wsm = es.enter_context(nc.sbuf_tensor("wsm", [128, 6272], BF16))
        xnT_own = es.enter_context(nc.sbuf_tensor("xnT_own", [128, KD * TOWN], BF16))
        M1s = es.enter_context(nc.sbuf_tensor("M1s", [8, TOWN], F32))
        M2s = es.enter_context(nc.sbuf_tensor("M2s", [8, TOWN], F32))
        w1own = es.enter_context(nc.sbuf_tensor("w1own", [1, TOWN], F32))
        w2own = es.enter_context(nc.sbuf_tensor("w2own", [1, TOWN], F32))
        sems = {s: es.enter_context(nc.semaphore(s)) for s in SEM_NAMES}

        xnT_oth_holder = [None]

        def xn_slice(k, t0, w):
            if t0 >= OWN0:
                assert t0 + w <= OWN0 + TOWN
                return xnT_own[:, k * TOWN + (t0 - OWN0): k * TOWN + (t0 - OWN0) + w]
            assert t0 + w <= TOWN
            return xnT_oth_holder[0][:, k * TOWN + t0: k * TOWN + t0 + w]

        P.emit("sync", lambda E: E.dma_start(out=id32[:, :], in_=id32_d.ap()), inc="si", incn=16)
        P.emit("sync", lambda E: E.dma_start(out=idb[:, :], in_=idb_d.ap()), inc="si", incn=16)
        P.emit("sync", lambda E: E.dma_start(out=expT[:, :], in_=expt_d.ap()), inc="si", incn=16)
        P.emit("sync", lambda E: E.dma_start(out=ones1[:, :], in_=ones1_d.ap()), inc="si", incn=16)
        P.emit("sync", lambda E: E.dma_start(out=onescol[:, :], in_=onescol_d.ap()), inc="si", incn=16)
        P.emit("sync", lambda E: E.dma_start(out=gfix[:, :], in_=gfix_in), inc="si", incn=16)
        P.emit("gpsimd", lambda E: E.dma_start(out=wsm[:, :], in_=wsm_in), inc="gs", incn=16)
        for col, val in ((0, 1.0 / 2048.0), (1, 1e-5), (2, 1e-20), (3, 1.0), (4, -1e30), (5, -1.0)):
            P.emit("vector", lambda E, c=col, v=val: E.memset(C[:, c:c + 1], v),
                   inc="dv" if col == 5 else None)
        consts_dv = P.v("dv")
        init_si = P.v("si")

        s1 = contextlib.ExitStack()
        xnT_oth = s1.enter_context(nc.sbuf_tensor("xnT_oth", [128, KD * TOWN], BF16))
        xnT_oth_holder[0] = xnT_oth

        with contextlib.ExitStack() as sla:
            xld = [sla.enter_context(nc.sbuf_tensor(f"xld{i}", [128, D], F32)) for i in range(2)]
            xnb = [sla.enter_context(nc.sbuf_tensor(f"xnb{i}", [128, D], BF16)) for i in range(2)]
            lnjunk = sla.enter_context(nc.sbuf_tensor("lnjunk", [128, D], F32))
            lst = sla.enter_context(nc.sbuf_tensor("lnstats", [128, 16], F32))
            wqg_bf = [sla.enter_context(nc.sbuf_tensor(f"wqgbf{i}", [128, 4096], BF16)) for i in range(2)]
            qsb = sla.enter_context(nc.sbuf_tensor("qsb", [128, S], F32))
            gsb = sla.enter_context(nc.sbuf_tensor("gsb", [128, S], F32))
            hsb = [sla.enter_context(nc.sbuf_tensor(f"hsb{i}", [128, S], F32)) for i in range(2)]
            ltok = sla.enter_context(nc.sbuf_tensor("ltok", [128, 128], F32))
            m12all = sla.enter_context(nc.sbuf_tensor("m12all", [128, 32], F32))
            etok = sla.enter_context(nc.sbuf_tensor("etok", [128, 32], F32))
            l2buf = sla.enter_context(nc.sbuf_tensor("l2buf", [128, 8], F32))
            maskall = sla.enter_context(nc.sbuf_tensor("maskall", [128, 128], F32))
            mtmp = sla.enter_context(nc.sbuf_tensor("mtmp", [128, 8], F32))
            part12 = sla.enter_context(nc.sbuf_tensor("part12", [128, 2], F32))
            Ssb = [sla.enter_context(nc.sbuf_tensor(f"Ssb{i}", [1, 1], F32)) for i in range(2)]
            erow = [sla.enter_context(nc.sbuf_tensor(f"erow{i}", [1, TOWN], F32)) for i in range(2)]
            spq = contextlib.ExitStack()
            psq = spq.enter_context(nc.psum_tensor("psq", [128, 1024], F32, side="right"))
            psg = spq.enter_context(nc.psum_tensor("psg", [128, 2048], F32, side="right"))
            ptok = sla.enter_context(nc.psum_tensor("ptok", [128, 128], F32))
            strp = contextlib.ExitStack()
            trp = [strp.enter_context(nc.psum_tensor("trp0", [128, 1024], BF16))]

            P.emit("gpsimd", lambda E: E.dma_start(out=wqg_bf[0][:, :], in_=wqg_in[0]),
                   inc="gq0", incn=16)
            P.emit("gpsimd", lambda E: E.dma_start(out=wqg_bf[1][:, :], in_=wqg_in[1]),
                   inc="gq1", incn=16)
            wqg_ready = [P.v("gq0"), P.v("gq1")]
            wqg_free = [0, 0]
            bank_q = [0, 0]
            bank_g = [0, 0, 0, 0]
            q_pe = [0] * 4
            g_pe = [0] * 4
            evac_dv = [0] * 4
            evac_ac = [0] * 4
            hsb_free = [0, 0]
            scan_dv = 0

            def attn_mms(j, ncs, extra, defer_g=False):
                b = j % 2
                for qg in range(2):
                    ps = psq if qg == 0 else psg
                    gnc = ncs % 4
                    P.emit("tensor", lambda E, b=b, qg=qg, ncs=ncs, ps=ps, gnc=gnc: [
                        E.matmul(ps[:, ((ncs % 2) if qg == 0 else gnc) * 512:
                                    ((ncs % 2) if qg == 0 else gnc) * 512 + 512],
                                 wqg_bf[b][:, qg * 2048 + k * 128:qg * 2048 + (k + 1) * 128],
                                 xn_slice(k, ncs * 512, 512),
                                 start=(k == 0), stop=(k == KD - 1)) for k in range(KD)][-1],
                        waits=extra + [(f"gq{b}", wqg_ready[b])] +
                              ([("dv", bank_q[ncs % 2])] if qg == 0 else [("ac", bank_g[gnc])]),
                        inc="pe")
                    if qg == 0:
                        q_pe[ncs] = P.v("pe")
                    else:
                        g_pe[ncs] = P.v("pe")
                P.emit("vector", lambda E, ncs=ncs: E.tensor_scalar(
                    qsb[:, ncs * 512:(ncs + 1) * 512], psq[:, (ncs % 2) * 512:(ncs % 2 + 1) * 512],
                    C[:, 2:3], None, ALU.max),
                    waits=[("pe", q_pe[ncs]), ("dv", scan_dv)], inc="dv")
                evac_dv[ncs] = P.v("dv")
                bank_q[ncs % 2] = P.v("dv")
                if not defer_g:
                    P.emit("scalar", lambda E, ncs=ncs: E.activation(
                        gsb[:, ncs * 512:(ncs + 1) * 512],
                        psg[:, (ncs % 4) * 512:(ncs % 4 + 1) * 512], AF.Sigmoid),
                        waits=[("pe", g_pe[ncs]), ("dv", scan_dv)], inc="ac")
                    evac_ac[ncs] = P.v("ac")
                    bank_g[ncs % 4] = P.v("ac")

            def attn_tail(j):
                b = j % 2
                P.emit("vector", lambda E: E.tensor_scalar(
                    gsb[:, OWN0:OWN0 + 1], gsb[:, OWN0:OWN0 + 1], gfix[:, 0:1], None, ALU.mult),
                    waits=[("ac", evac_ac[3]), ("si", init_si)], inc="dv")
                P.emit("vector", lambda E, b=b: E.tensor_tensor_scan(
                    hsb[b][:, :], gsb[:, :], qsb[:, :], 0.0, ALU.mult, ALU.add),
                    waits=[("dv", P.v("dv")), ("ac", evac_ac[3]), (f"sh{b}", hsb_free[b])],
                    inc="dv")
                sc = P.v("dv")
                P.emit("sync", lambda E, j=j, b=b: E.dma_start(
                    out=hscr[j * 128:(j + 1) * 128, :], in_=hsb[b][:, OWN0:OWN0 + TOWN]),
                    waits=[("dv", sc)], inc=f"sh{b}", incn=16)
                hsb_free[b] = P.v(f"sh{b}")
                return sc

            xld_free = [0, 0]
            xnb_free = [0, 0]
            trp_free = [0]
            for i in range(16):
                b = i % 2
                P.emit("sync", lambda E, b=b, i=i: E.dma_start(
                    out=xld[b][:, :], in_=x_in[i * 128:(i + 1) * 128, :]),
                    waits=[("dv", xld_free[b])], inc=f"sx{b}", incn=16)
                ld_tgt = P.v(f"sx{b}")
                o8 = 8 * b
                P.emit("scalar", lambda E, b=b, o8=o8: E.activation(
                    lnjunk[:, :], xld[b][:, :], AF.Identity, accum_out=lst[:, o8:o8 + 1]),
                    waits=[(f"sx{b}", ld_tgt), ("dv", consts_dv)], inc="ac")
                P.emit("scalar", lambda E, o8=o8: E.activation(
                    lst[:, o8 + 1:o8 + 2], lst[:, o8:o8 + 1], AF.Identity, scale=-1.0 / 2048.0),
                    waits=[("ac", P.v("ac"))], inc="ac")
                P.emit("scalar", lambda E, b=b, o8=o8: E.activation(
                    lnjunk[:, :], xld[b][:, :], AF.Square, bias=lst[:, o8 + 1:o8 + 2],
                    accum_out=lst[:, o8 + 2:o8 + 3]),
                    waits=[("ac", P.v("ac"))], inc="ac")
                P.emit("scalar", lambda E, o8=o8: E.activation(
                    lst[:, o8 + 3:o8 + 4], lst[:, o8 + 2:o8 + 3], AF.Ln, bias=C[:, 1:2],
                    scale=1.0 / 2048.0),
                    waits=[("ac", P.v("ac"))], inc="ac")
                P.emit("scalar", lambda E, o8=o8: E.activation(
                    lst[:, o8 + 4:o8 + 5], lst[:, o8 + 3:o8 + 4], AF.Exp, scale=-0.5),
                    waits=[("ac", P.v("ac"))], inc="ac")
                P.emit("vector", lambda E, b=b, o8=o8: E.tensor_scalar(
                    xnb[b][:, :], xld[b][:, :], lst[:, o8 + 1:o8 + 2], lst[:, o8 + 4:o8 + 5],
                    ALU.add, ALU.mult),
                    waits=[("ac", P.v("ac")), ("pe", xnb_free[b])], inc="dv")
                xld_free[b] = P.v("dv")
                xn_dv = P.v("dv")
                dst = xnT_own if i >= 8 else xnT_oth
                off = (i - 8 if i >= 8 else i) * 128
                for hh in range(2):
                    P.emit("tensor", lambda E, b=b, hh=hh: [
                        E.transpose(trp[0][:, jj * 128:(jj + 1) * 128],
                                    xnb[b][:, (hh * 8 + jj) * 128:(hh * 8 + jj + 1) * 128],
                                    idb[:, :]) for jj in range(8)][-1],
                        waits=[("dv", xn_dv), ("dv", trp_free[0]), ("si", init_si)], inc="pe")
                    pe_tr = P.v("pe")
                    P.emit("vector", lambda E, hh=hh, dst=dst, off=off: E.tensor_copy(
                        dst[:, hh * 8 * TOWN:(hh * 8 + 8) * TOWN].rearrange(
                            "p (j t) -> p j t", j=8)[:, :, off:off + 128],
                        trp[0][:, :].rearrange("p (j t) -> p j t", j=8)),
                        waits=[("pe", pe_tr)], inc="dv")
                    trp_free[0] = P.v("dv")
                xnb_free[b] = P.v("pe")
                P.emit("tensor", lambda E, i=i: [
                    E.matmul(ptok[:, i * 8:(i + 1) * 8],
                             xn_slice(k, i * 128, 128),
                             wsm[:, 6144 + k * 8:6144 + (k + 1) * 8],
                             start=(k == 0), stop=(k == KD - 1))
                    for k in range(KD)][-1],
                    waits=[("gs", 16), ("dv", P.v("dv"))], inc="pe")
                if i % 4 == 3:
                    cseq = i // 4
                    attn_mms(0, cseq, [("dv", P.v("dv"))], defer_g=True)
            P.emit("scalar", lambda E: E.activation(gsb[:, :], psg[:, :], AF.Sigmoid),
                   waits=[("pe", g_pe[3])], inc="ac")
            for ncs in range(4):
                evac_ac[ncs] = P.v("ac")
                bank_g[ncs] = P.v("ac")
            scan_dv = attn_tail(0)
            wqg_free[0] = P.v("pe")
            lgt_pe = P.v("pe")
            ln_all_dv = P.v("dv")
            strp.close()

            ltok_dv = 0
            maskall_dv = [0] * 8
            m12_dv = 0

            def routing_piece(step):
                nonlocal ltok_dv, m12_dv
                if step == 0:
                    P.emit("vector", lambda E: E.tensor_copy(ltok[:, :], ptok[:, :]),
                           waits=[("pe", lgt_pe)], inc="dv")
                    ltok_dv = P.v("dv")
                elif 1 <= step <= 8:
                    for i in (2 * (step - 1), 2 * (step - 1) + 1):
                        own = i >= 8
                        m1dst = maskall[:, (i - 8) * 16:(i - 8) * 16 + 8] if own else mtmp[:, 0:8]
                        m2dst = maskall[:, (i - 8) * 16 + 8:(i - 8) * 16 + 16] if own else mtmp[:, 0:8]
                        L = ltok[:, i * 8:(i + 1) * 8]
                        P.emit("vector", lambda E, L=L, i=i: E.reduce_max(
                            m12all[:, 2 * i:2 * i + 1], L, axis=AX),
                            waits=[("dv", ltok_dv)], inc="dv")
                        P.emit("vector", lambda E, L=L, d=m1dst, i=i: E.tensor_scalar(
                            d, L, m12all[:, 2 * i:2 * i + 1], None, ALU.is_ge),
                            waits=[("dv", P.v("dv"))], inc="dv")
                        P.emit("vector", lambda E, L=L, d=m1dst: E.scalar_tensor_tensor(
                            l2buf[:, :], d, C[:, 4:5], L, ALU.mult, ALU.add),
                            waits=[("dv", P.v("dv"))], inc="dv")
                        P.emit("vector", lambda E, i=i: E.reduce_max(
                            m12all[:, 2 * i + 1:2 * i + 2], l2buf[:, :], axis=AX),
                            waits=[("dv", P.v("dv"))], inc="dv")
                        P.emit("vector", lambda E, d=m2dst, i=i: E.tensor_scalar(
                            d, l2buf[:, :], m12all[:, 2 * i + 1:2 * i + 2], None, ALU.is_ge),
                            waits=[("dv", P.v("dv"))], inc="dv")
                        if i >= 8:
                            maskall_dv[i - 8] = P.v("dv")
                    if step == 8:
                        m12_dv = P.v("dv")

            for j in range(1, 16):
                b = j % 2
                if j >= 2:
                    P.emit("gpsimd", lambda E, j=j, b=b: E.dma_start(
                        out=wqg_bf[b][:, :], in_=wqg_in[j]),
                        waits=[("pe", wqg_free[b])], inc=f"gq{b}", incn=16)
                    wqg_ready[b] = P.v(f"gq{b}")
                routing_piece(j - 1)
                for ncs in range(4):
                    attn_mms(j, ncs, [])
                wqg_free[b] = P.v("pe")
                scan_dv = attn_tail(j)
            spq.close()
            plog = sla.enter_context(nc.psum_tensor("plog", [8, 1024], F32))

            P.emit("scalar", lambda E: E.activation(etok[:, :], m12all[:, :], AF.Exp),
                   waits=[("dv", m12_dv)], inc="ac")
            etok_ac = P.v("ac")
            P.emit("vector", lambda E: E.reduce_sum(
                part12[:, 0:1], etok[:, :].rearrange("p (i two) -> p two i", two=2)[:, 0, :],
                axis=AX), waits=[("ac", etok_ac)], inc="dv")
            P.emit("vector", lambda E: E.reduce_sum(
                part12[:, 1:2], etok[:, :].rearrange("p (i two) -> p two i", two=2)[:, 1, :],
                axis=AX), waits=[("dv", P.v("dv"))], inc="dv")
            part_dv = P.v("dv")
            with contextlib.ExitStack() as sco:
                pmsk = sco.enter_context(nc.psum_tensor("pmsk", [8, 1024], F32))
                psS = [sco.enter_context(nc.psum_tensor(f"psS{i}", [1, 1], F32)) for i in range(2)]
                for mk in range(2):
                    P.emit("tensor", lambda E, mk=mk: E.matmul(
                        psS[mk][:, :], onescol[:, :], part12[:, mk:mk + 1], start=True, stop=True),
                        waits=[("dv", part_dv)], inc="pe")
                sum_pe = P.v("pe")
                for oi in range(8):
                    P.emit("tensor", lambda E, oi=oi: E.transpose(
                        plog[0:8, oi * 128:(oi + 1) * 128],
                        maskall[:, oi * 16:oi * 16 + 8], id32[:, :]),
                        waits=[("dv", maskall_dv[oi])], inc="pe")
                    P.emit("tensor", lambda E, oi=oi: E.transpose(
                        pmsk[0:8, oi * 128:(oi + 1) * 128],
                        maskall[:, oi * 16 + 8:oi * 16 + 16], id32[:, :]), inc="pe")
                msk_pe = P.v("pe")
                P.emit("vector", lambda E: E.tensor_copy(M1s[:, :], plog[:, :]),
                       waits=[("pe", msk_pe)], inc="dv")
                P.emit("vector", lambda E: E.tensor_copy(M2s[:, :], pmsk[:, :]), inc="dv")
                P.emit("vector", lambda E: E.tensor_copy(Ssb[0][:, :], psS[0][:, :]),
                       waits=[("pe", sum_pe)], inc="dv")
                P.emit("vector", lambda E: E.tensor_copy(Ssb[1][:, :], psS[1][:, :]), inc="dv")
                coda1_dv = P.v("dv")
            with contextlib.ExitStack() as sco:
                pmr = [sco.enter_context(nc.psum_tensor(f"pmr{i}", [1, TOWN], F32)) for i in range(2)]
                for mk in range(2):
                    P.emit("tensor", lambda E, mk=mk: [
                        E.transpose(pmr[mk][0:1, oi * 128:(oi + 1) * 128],
                                    etok[:, 2 * (8 + oi) + mk:2 * (8 + oi) + mk + 1], id32[:, :])
                        for oi in range(8)][-1],
                        waits=[("ac", etok_ac), ("dv", coda1_dv)], inc="pe")
                row_pe = P.v("pe")
                for mk in range(2):
                    P.emit("vector", lambda E, mk=mk: E.tensor_copy(erow[mk][:, :], pmr[mk][0:1, :]),
                           waits=[("pe", row_pe)], inc="dv")
                P.emit("vector", lambda E: E.reciprocal(Ssb[0][:, :], Ssb[0][:, :]),
                       waits=[("dv", P.v("dv"))], inc="dv")
                P.emit("vector", lambda E: E.reciprocal(Ssb[1][:, :], Ssb[1][:, :]),
                       waits=[("dv", P.v("dv"))], inc="dv")
                P.emit("vector", lambda E: E.tensor_scalar(
                    w1own[:, :], erow[0][0:1, :], Ssb[0][:, :], None, ALU.mult),
                    waits=[("dv", P.v("dv"))], inc="dv")
                P.emit("vector", lambda E: E.tensor_scalar(
                    w2own[:, :], erow[1][0:1, :], Ssb[1][:, :], None, ALU.mult),
                    waits=[("dv", P.v("dv"))], inc="dv")
        rt_done = P.snap()
        at_done = rt_done
        s1.close()

        sR = es.enter_context(contextlib.ExitStack())
        hmix = sR.enter_context(nc.sbuf_tensor("hmix", [128, JH * CH], BF16, side="right"))
        sF = es.enter_context(contextlib.ExitStack())
        Qsb = sF.enter_context(nc.sbuf_tensor("Qsb", [128, CH], BF16))
        stage = [sF.enter_context(nc.sbuf_tensor(f"stage{i}", [128, D], F32)) for i in range(4)]

        def emit_ffn_chunk(c, prev, hmix_free_pe, stage_free_so, aux=None):
            tloc = c * CH
            tglob = OWN0 + tloc
            bar = [(s, prev[s]) for s in ("pe", "dv", "ac", "sh0", "sh1")]
            with contextlib.ExitStack() as sup:
                NWF = 4
                wfn_bf = [sup.enter_context(nc.sbuf_tensor(f"wfnbf{c}_{i}", [128, 4480], BF16))
                          for i in range(NWF)]
                Mx = [sup.enter_context(nc.sbuf_tensor(f"Mx{c}_{i}", [128, CH], F32)) for i in range(2)]
                wbc = [sup.enter_context(nc.sbuf_tensor(f"wbc{c}_{i}", [128, CH], F32)) for i in range(2)]
                Pm = [sup.enter_context(nc.sbuf_tensor(f"Pm{c}_{i}", [128, CH], BF16)) for i in range(4)]
                Mxd = sup.enter_context(nc.sbuf_tensor(f"Mxd{c}", [128, CH], F32))
                uk = [sup.enter_context(nc.sbuf_tensor(f"uk{c}_{i}", [128, CH], F32)) for i in range(2)]
                gk = [sup.enter_context(nc.sbuf_tensor(f"gk{c}_{i}", [128, CH], F32)) for i in range(2)]
                sk = [sup.enter_context(nc.sbuf_tensor(f"sk{c}_{i}", [128, CH], F32)) for i in range(2)]
                hb = [[sup.enter_context(nc.sbuf_tensor(f"hb{c}_{k}_{i}", [128, CH], BF16))
                       for i in range(2)] for k in range(2)]
                pA = [sup.enter_context(nc.psum_tensor(f"pA{c}_{i}", [128, CH], F32)) for i in range(2)]
                pB = [sup.enter_context(nc.psum_tensor(f"pB{c}_{i}", [128, CH], F32)) for i in range(2)]
                pC = sup.enter_context(nc.psum_tensor(f"pC{c}", [128, CH], F32))
                pDg = sup.enter_context(nc.psum_tensor(f"pDg{c}", [128, CH], F32))
                pM = [sup.enter_context(nc.psum_tensor(f"pM{c}_{i}", [128, CH], F32)) for i in range(2)]

                P.emit("tensor", lambda E: [
                    E.matmul(pA[0][:, :], wsm[:, k * 128:(k + 1) * 128],
                             xn_slice(k, tglob, CH), start=(k == 0), stop=(k == KD - 1))
                    for k in range(KD)][-1], waits=[("dv", prev["dv"])], inc="pe")
                P.emit("tensor", lambda E: [
                    E.matmul(pB[0][:, :], wsm[:, 2048 + k * 128:2048 + (k + 1) * 128],
                             xn_slice(k, tglob, CH), start=(k == 0), stop=(k == KD - 1))
                    for k in range(KD)][-1], inc="pe")
                P.emit("tensor", lambda E: E.matmul(pA[1][:, :], expT[:, :],
                                                    M1s[0:8, tloc:tloc + CH], start=True, stop=True),
                       inc="pe")
                P.emit("tensor", lambda E: E.matmul(pB[1][:, :], expT[:, :],
                                                    M2s[0:8, tloc:tloc + CH], start=True, stop=True),
                       inc="pe")
                P.emit("tensor", lambda E: E.matmul(pM[0][:, :], ones1[:, :],
                                                    w1own[0:1, tloc:tloc + CH], start=True, stop=True),
                       inc="pe")
                P.emit("tensor", lambda E: E.matmul(pM[1][:, :], ones1[:, :],
                                                    w2own[0:1, tloc:tloc + CH], start=True, stop=True),
                       inc="pe")
                prep_pe = P.v("pe")
                P.emit("vector", lambda E: E.tensor_copy(Mx[0][:, :], pA[1][:, :]),
                       waits=[("pe", prep_pe)] + bar, inc="dv")
                P.emit("vector", lambda E: E.tensor_copy(Mx[1][:, :], pB[1][:, :]), inc="dv")
                P.emit("vector", lambda E: E.tensor_copy(wbc[0][:, :], pM[0][:, :]), inc="dv")
                P.emit("vector", lambda E: E.tensor_copy(wbc[1][:, :], pM[1][:, :]), inc="dv")
                P.emit("vector", lambda E: E.tensor_tensor(Mxd[:, :], Mx[1][:, :], Mx[0][:, :], ALU.subtract), inc="dv")
                P.emit("vector", lambda E: E.tensor_tensor(Pm[0][:, :], pA[0][:, :], Mx[0][:, :], ALU.mult), inc="dv")
                P.emit("vector", lambda E: E.tensor_tensor(Pm[1][:, :], pA[0][:, :], Mxd[:, :], ALU.mult), inc="dv")
                P.emit("vector", lambda E: E.tensor_tensor(Pm[2][:, :], pB[0][:, :], Mx[0][:, :], ALU.mult), inc="dv")
                P.emit("vector", lambda E: E.tensor_tensor(Pm[3][:, :], pB[0][:, :], Mxd[:, :], ALU.mult), inc="dv")
                prep_dv = P.v("dv")

                wfn_ready = [0] * NWF
                hmix_dv = [0] * JH
                mdown_pe = [0] * JH
                hdv2 = [0, 0]
                hb_free = [0, 0]
                delta_free_dv = prep_dv
                prev_tile_dv = prep_dv
                u1cp_ac = [0, 0]
                g1cp_ac = [0, 0]
                for j in range(JH + 1):
                    bj = j % 2
                    wj = j % NWF
                    if j < JH:
                        wfree = mdown_pe[j - NWF] if j >= NWF else 0
                        if j >= NWF:
                            ph = []
                        elif aux is not None:
                            ph = [("pe", aux[j])]
                        else:
                            ph = [("pe", prev["pe"]), ("dv", prev["dv"]), ("ac", prev["ac"])]
                        P.emit("gpsimd", lambda E, j=j, wj=wj: E.dma_start(
                            out=wfn_bf[wj][:, :], in_=wfn_in[j]),
                            waits=[("pe", wfree)] + ph, inc=f"gf{wj}", incn=16)
                        wfn_ready[wj] = P.v(f"gf{wj}")
                    if j < JH:
                        pp = [("dv", prep_dv)] if j < 2 else []
                        P.emit("tensor", lambda E, bj=bj, wj=wj: [
                            E.matmul(pA[bj][:, :], wfn_bf[wj][:, k * 128:(k + 1) * 128],
                                     xn_slice(k, tglob, CH), start=(k == 0), stop=False)
                            for k in range(KD)][-1],
                            waits=[("ac", u1cp_ac[bj]), (f"gf{wj}", wfn_ready[wj])] + pp,
                            inc="pe")
                        P.emit("tensor", lambda E, bj=bj, wj=wj: E.matmul(
                            pA[bj][:, :], wfn_bf[wj][:, 4096:4224], Pm[0][:, :],
                            start=False, stop=True), inc="pe")
                        bA_pe = P.v("pe")
                        P.emit("tensor", lambda E, bj=bj, wj=wj: [
                            E.matmul(pB[bj][:, :], wfn_bf[wj][:, 2048 + k * 128:2048 + (k + 1) * 128],
                                     xn_slice(k, tglob, CH), start=(k == 0), stop=False)
                            for k in range(KD)][-1],
                            waits=[("ac", g1cp_ac[bj])] + pp, inc="pe")
                        P.emit("tensor", lambda E, bj=bj, wj=wj: E.matmul(
                            pB[bj][:, :], wfn_bf[wj][:, 4224:4352], Pm[2][:, :],
                            start=False, stop=True), inc="pe")
                        bB_pe = P.v("pe")
                        P.emit("tensor", lambda E, bj=bj, wj=wj: E.matmul(
                            pC[:, :], wfn_bf[wj][:, 4096:4224], Pm[1][:, :],
                            start=True, stop=True),
                            waits=[("dv", delta_free_dv)] + pp, inc="pe")
                        P.emit("tensor", lambda E, bj=bj, wj=wj: E.matmul(
                            pDg[:, :], wfn_bf[wj][:, 4224:4352], Pm[3][:, :],
                            start=True, stop=True), inc="pe")
                        dj_pe = P.v("pe")
                    if j >= 1:
                        pj = j - 1
                        for mk in range(2):
                            P.emit("tensor", lambda E, pj=pj, mk=mk: E.matmul(
                                pM[mk][:, :], wfn_bf[pj % NWF][:, 4352:4480], hb[mk][pj % 2][:, :],
                                start=(pj == 0), stop=(pj == JH - 1)),
                                waits=[("dv", hdv2[pj % 2])] if mk == 0 else [], inc="pe")
                        mdown_pe[pj] = P.v("pe")
                    if j >= JH:
                        break
                    P.emit("scalar", lambda E, bj=bj: E.activation(sk[0][:, :], pA[bj][:, :], AF.Silu),
                           waits=[("pe", bA_pe), ("dv", prev_tile_dv)], inc="ac")
                    P.emit("scalar", lambda E, bj=bj: E.copy(uk[0][:, :], pA[bj][:, :]), inc="ac")
                    u1cp_ac[bj] = P.v("ac")
                    P.emit("scalar", lambda E, bj=bj: E.copy(gk[0][:, :], pB[bj][:, :]),
                           waits=[("pe", bB_pe)], inc="ac")
                    g1cp_ac[bj] = P.v("ac")
                    s1_ac = P.v("ac")
                    P.emit("vector", lambda E: E.tensor_tensor(
                        uk[1][:, :], uk[0][:, :], pC[:, :], ALU.add),
                        waits=[("ac", u1cp_ac[bj]), ("pe", dj_pe)], inc="dv")
                    P.emit("scalar", lambda E: E.activation(sk[1][:, :], uk[1][:, :], AF.Silu),
                           waits=[("dv", P.v("dv"))], inc="ac")
                    s2_ac = P.v("ac")
                    P.emit("vector", lambda E: E.tensor_tensor(
                        gk[1][:, :], gk[0][:, :], pDg[:, :], ALU.add),
                        waits=[("ac", g1cp_ac[bj])], inc="dv")
                    delta_free_dv = P.v("dv")
                    P.emit("vector", lambda E: E.tensor_tensor(
                        sk[0][:, :], sk[0][:, :], wbc[0][:, :], ALU.mult),
                        waits=[("ac", s1_ac)], inc="dv")
                    P.emit("vector", lambda E, bj=bj: E.tensor_tensor(
                        hb[0][bj][:, :], sk[0][:, :], gk[0][:, :], ALU.mult),
                        waits=[("pe", hb_free[bj])], inc="dv")
                    P.emit("vector", lambda E: E.tensor_tensor(
                        sk[1][:, :], sk[1][:, :], wbc[1][:, :], ALU.mult),
                        waits=[("ac", s2_ac)], inc="dv")
                    P.emit("vector", lambda E, bj=bj: E.tensor_tensor(
                        hb[1][bj][:, :], sk[1][:, :], gk[1][:, :], ALU.mult), inc="dv")
                    hdv2[bj] = P.v("dv")
                    P.emit("vector", lambda E, bj=bj, j=j: E.tensor_tensor(
                        hmix[:, j * CH:(j + 1) * CH], hb[0][bj][:, :], hb[1][bj][:, :], ALU.add),
                        waits=[("pe", hmix_free_pe)], inc="dv")
                    hmix_dv[j] = P.v("dv")
                    prev_tile_dv = P.v("dv")
                    if j >= 1:
                        hb_free[(j - 1) % 2] = mdown_pe[j - 1]
                mq_pe = P.v("pe")
                P.emit("vector", lambda E: E.tensor_tensor(uk[0][:, :], pM[0][:, :], Mx[0][:, :], ALU.mult),
                       waits=[("pe", mq_pe)], inc="dv")
                P.emit("vector", lambda E: E.tensor_tensor(uk[1][:, :], pM[1][:, :], Mx[1][:, :], ALU.mult),
                       inc="dv")
                P.emit("vector", lambda E: E.tensor_tensor(Qsb[:, :], uk[0][:, :], uk[1][:, :], ALU.add),
                       waits=[("dv", P.v("dv"))], inc="dv")
            up_done = P.snap()

            with contextlib.ExitStack() as sdn:
                NWD = 4
                wdn_bf = [sdn.enter_context(nc.sbuf_tensor(f"wdnbf{c}_{i}", [128, H], BF16))
                          for i in range(NWD)]
                hrel = [sdn.enter_context(nc.sbuf_tensor(f"hrel{c}_{i}", [128, CH], F32)) for i in range(2)]
                sumfm = [sdn.enter_context(nc.sbuf_tensor(f"sumfm{c}_{i}", [128, CH], F32)) for i in range(2)]
                pso = [sdn.enter_context(nc.psum_tensor(f"pso{c}_{i}", [128, CH], F32)) for i in range(2)]
                pstr = [sdn.enter_context(nc.psum_tensor(f"pstr{c}_{i}", [128, CH], F32)) for i in range(2)]

                bar2 = [(s, up_done[s]) for s in ("dv", "ac", "pe")]
                stage_bar = ([(s, prev[s]) for s in ("dv", "ac", "pe")]
                             if c == 0 else [])
                stage_sg = {}
                for tb in range(4):
                    P.emit("sync", lambda E, tb=tb: E.dma_start(
                        out=stage[tb][:, :],
                        in_=x_in[tglob + tb * 128:tglob + (tb + 1) * 128, :]),
                        waits=list(stage_free_so) + stage_bar, inc=f"sg{tb}", incn=16)
                    stage_sg[tb] = P.v(f"sg{tb}")
                mm_pe_arr = [0] * JD
                hrel_free = [0, 0]
                sum_free = [0, 0]
                pso_free = [0, 0]
                pstr_free = [0, 0]
                sf_dv = [0, 0]
                stage_dv = 0
                for jd in range(JD + 1):
                    bd = jd % 2
                    wd = jd % NWD
                    if jd < JD:
                        if jd >= NWD:
                            wwaits = [("pe", mm_pe_arr[jd - NWD])]
                        elif jd < NWD - 1:
                            wwaits = [("pe", mdown_pe[JH - NWD + jd])]
                        else:
                            wwaits = [("dv", up_done["dv"])]
                        P.emit("gpsimd", lambda E, jd=jd, wd=wd: E.dma_start(
                            out=wdn_bf[wd][:, :], in_=wdn_in[jd]),
                            waits=wwaits, inc=f"gd{wd}", incn=16)
                        wd_tgt = P.v(f"gd{wd}")
                        hph = [("pe", dj_pe)] if jd < NWD else []
                        P.emit("sync", lambda E, jd=jd, bd=bd: E.dma_start(
                            out=hrel[bd][:, :], in_=hscr[jd * 128:(jd + 1) * 128, tloc:tloc + CH]),
                            waits=[("dv", hrel_free[bd]), ("sh0", at_done["sh0"]),
                                   ("sh1", at_done["sh1"])] + hph,
                            inc=f"sr{bd}", incn=16)
                        h_tgt = P.v(f"sr{bd}")
                        P.emit("tensor", lambda E, bd=bd, wd=wd: [
                            E.matmul(pso[bd][:, :], wdn_bf[wd][:, k * 128:(k + 1) * 128],
                                     hmix[:, k * CH:(k + 1) * CH], start=(k == 0), stop=False)
                            for k in range(JH - 4)][-1],
                            waits=[("pe", up_done["pe"]), ("ac", up_done["ac"]),
                                   ("dv", hmix_dv[JH - 5]), (f"gd{wd}", wd_tgt),
                                   ("dv", pso_free[bd])], inc="pe")
                        P.emit("tensor", lambda E, bd=bd, wd=wd: [
                            E.matmul(pso[bd][:, :], wdn_bf[wd][:, k * 128:(k + 1) * 128],
                                     hmix[:, k * CH:(k + 1) * CH], start=False, stop=False)
                            for k in range(JH - 4, JH)][-1],
                            waits=[("dv", up_done["dv"])], inc="pe")
                        P.emit("tensor", lambda E, jd=jd, bd=bd: E.matmul(
                            pso[bd][:, :], wsm[:, 4096 + jd * 128:4096 + (jd + 1) * 128],
                            Qsb[:, :], start=False, stop=True), inc="pe")
                        mm_pe_arr[jd] = P.v("pe")
                    if jd >= 1:
                        pd = jd - 1
                        bp = pd % 2
                        P.emit("tensor", lambda E, bp=bp: [
                            E.transpose(pstr[bp][:, bb * 128:(bb + 1) * 128],
                                        sumfm[bp][:, bb * 128:(bb + 1) * 128], id32[:, :])
                            for bb in range(4)][-1],
                            waits=[("dv", sf_dv[bp]), ("dv", pstr_free[bp])], inc="pe")
                        tr_pe = P.v("pe")
                        sum_free[bp] = tr_pe
                    if jd < JD:
                        P.emit("vector", lambda E, bd=bd: E.tensor_tensor(
                            sumfm[bd][:, :], hrel[bd][:, :], pso[bd][:, :], ALU.add),
                            waits=[("pe", mm_pe_arr[jd]), (f"sr{bd}", h_tgt),
                                   ("pe", sum_free[bd])], inc="dv")
                        pso_free[bd] = P.v("dv")
                        hrel_free[bd] = P.v("dv")
                        sf_dv[bd] = P.v("dv")
                    if jd >= 1:
                        pd = jd - 1
                        bp = pd % 2
                        for bb in range(4):
                            P.emit("vector", lambda E, bb=bb, bp=bp, pd=pd: E.tensor_tensor(
                                stage[bb][:, pd * 128:(pd + 1) * 128],
                                stage[bb][:, pd * 128:(pd + 1) * 128],
                                pstr[bp][:, bb * 128:(bb + 1) * 128], ALU.add),
                                waits=[("pe", tr_pe), (f"sg{bb}", stage_sg[bb])], inc="dv")
                        pstr_free[bp] = P.v("dv")
                        stage_dv = P.v("dv")
                hmix_free_pe = P.v("pe")
                for tb in range(4):
                    P.emit("sync", lambda E, tb=tb: E.dma_start(
                        out=out_hbm[tloc + tb * 128:tloc + (tb + 1) * 128, :],
                        in_=stage[tb][:, :]),
                        waits=[("dv", stage_dv)], inc=f"so{c}", incn=16)
                stage_free_so = [(f"so{c}", P.v(f"so{c}"))]
            return P.snap(), hmix_free_pe, stage_free_so, [
                mm_pe_arr[JD - NWD + s] for s in range(NWD)]

        prev = rt_done
        hmix_free_pe = 0
        stage_free_so = []
        aux = None
        for c in range(2):
            prev, hmix_free_pe, stage_free_so, aux = emit_ffn_chunk(
                c, prev, hmix_free_pe, stage_free_so, aux)

        P.emit("sync", waits=[("so0", P.v("so0")), ("so1", P.v("so1"))])
        P.check()
        with nc.Block() as block:
            P.realize(block, sems)
    return nc


_CACHE = {}


def kernel(**inputs):
    x = np.ascontiguousarray(np.asarray(inputs["L0_attn_x"], np.float32))
    wqg, wfn, wdn, wsmall = _retile_inputs(inputs)
    nc = build_program()

    in_maps = []
    for c in range(8):
        b, half = c // 2, c % 2
        if half == 1:
            xc = np.ascontiguousarray(x[b])
            gfix = np.ones((128, 1), np.float32)
        else:
            xc = np.ascontiguousarray(np.concatenate([x[b][1024:], x[b][:1024]], axis=0))
            gfix = np.zeros((128, 1), np.float32)
        in_maps.append({"x": xc, "wqg": wqg, "wfn": wfn, "wdn": wdn,
                        "wsmall": wsmall, "gfix": gfix})

    res = run_bass_kernel_spmd(nc, in_maps, core_ids=list(range(8)))
    out = np.empty((4, S, D), np.float32)
    for c in range(8):
        b, half = c // 2, c % 2
        out[b, half * 1024:(half + 1) * 1024, :] = res.results[c]["out"]
    return out


if __name__ == "__main__":
    import reference
    ins = {k: np.asarray(v) for k, v in reference.setup_inputs().items()}
    got = kernel(**ins)
    want = np.asarray(reference.reference(**{k: v for k, v in ins.items()}))
    err = np.abs(got - want)
    rel = np.sqrt(((got - want) ** 2).mean() / (want ** 2).mean())
    print("max abs err:", err.max(), "rel l2:", rel)


# version 33
# speedup vs baseline: 1.0668x; 1.0048x over previous
import contextlib
import numpy as np
import ml_dtypes

import concourse.bass as bass
import concourse.mybir as mybir
from concourse.bass_utils import run_bass_kernel_spmd

F32 = mybir.dt.float32
BF16 = mybir.dt.bfloat16
AF = mybir.ActivationFunctionType
ALU = mybir.AluOpType
AX = mybir.AxisListType.X

D = 2048
S = 2048
H = 5632
NE, R = 8, 16
KD = 16
JH = 44
JD = 16
TOWN = 1024
OWN0 = 1024
CH = 512
ALPHA = 2.0

ENGS = ("sync", "gpsimd", "tensor", "vector", "scalar")
SEM_NAMES = ("pe", "dv", "ac", "si", "gs", "sx0", "sx1", "sh0", "sh1",
             "gq0", "gq1", "gf0", "gf1", "gf2", "gf3", "gd0", "gd1", "gd2", "gd3", "sr0", "sr1",
             "sg0", "sg1", "sg2", "sg3", "so0", "so1")


class Prog:

    def __init__(self):
        self.st = {e: [] for e in ENGS}
        self.cnt = {s: 0 for s in SEM_NAMES}
        self.audit = []

    def v(self, sem):
        return self.cnt[sem]

    def snap(self):
        return dict(self.cnt)

    def emit(self, eng, fn=None, waits=(), inc=None, incn=1):
        if fn is not None and eng in ("vector", "scalar"):
            own = "dv" if eng == "vector" else "ac"
            waits = list(waits) + [(own, self.cnt[own])]
            if inc is None:
                inc, incn = own, 1
            else:
                assert inc == own
        wd = {}
        for s, v in waits:
            if v > 0:
                wd[s] = max(wd.get(s, 0), v)
        waits = list(wd.items())
        if inc is not None:
            self.cnt[inc] += incn
        self.st[eng].append((fn, tuple(waits), inc, incn))
        self.audit.extend(waits)
        return self.cnt[inc] if inc else None

    def check(self):
        for s, v in self.audit:
            assert v <= self.cnt[s], f"wait {s}>={v} but final {self.cnt[s]} -> hang"

    def realize(self, block, sems):
        for eng in ENGS:
            stream = self.st[eng]
            if not stream:
                continue

            def body(E, stream=stream):
                for fn, waits, inc, incn in stream:
                    for s, v in waits:
                        E.wait_ge(sems[s], v)
                    inst = fn(E) if fn is not None else None
                    if inc is not None:
                        inst.then_inc(sems[inc], incn)

            getattr(block, eng)(body)


def _retile_inputs(ins):
    W = np.ascontiguousarray(np.asarray(ins["L0_attn_W"], np.float32))
    w_up = np.ascontiguousarray(np.asarray(ins["L0_ffn_w_up"], np.float32))
    w_gate = np.ascontiguousarray(np.asarray(ins["L0_ffn_w_gate"], np.float32))
    w_down = np.ascontiguousarray(np.asarray(ins["L0_ffn_w_down"], np.float32))
    up_A = np.asarray(ins["L0_ffn_up_A"], np.float32)
    up_B = np.asarray(ins["L0_ffn_up_B"], np.float32)
    gate_A = np.asarray(ins["L0_ffn_gate_A"], np.float32)
    gate_B = np.asarray(ins["L0_ffn_gate_B"], np.float32)
    down_A = np.asarray(ins["L0_ffn_down_A"], np.float32)
    down_B = np.asarray(ins["L0_ffn_down_B"], np.float32)
    gate_w = np.asarray(ins["L0_ffn_gate_w"], np.float32)

    wqg = np.ascontiguousarray(
        W.reshape(2, 16, 128, KD, 128).transpose(1, 4, 0, 3, 2).reshape(16, 128, 4096))
    wu = w_up.reshape(JH, 128, KD, 128).transpose(0, 3, 2, 1).reshape(JH, 128, 2048)
    wg = w_gate.reshape(JH, 128, KD, 128).transpose(0, 3, 2, 1).reshape(JH, 128, 2048)
    ub = up_B.transpose(0, 2, 1).reshape(128, JH, 128).transpose(1, 0, 2)
    gb = gate_B.transpose(0, 2, 1).reshape(128, JH, 128).transpose(1, 0, 2)
    da = down_A.transpose(2, 0, 1).reshape(JH, 128, 128)
    wfn = np.ascontiguousarray(np.concatenate([wu, wg, ub, gb, da], axis=2))
    wdn = np.ascontiguousarray(
        w_down.reshape(JD, 128, JH, 128).transpose(0, 3, 2, 1).reshape(JD, 128, H))
    ua = up_A.transpose(2, 0, 1).reshape(D, 128).reshape(KD, 128, 128).transpose(1, 0, 2).reshape(128, 2048)
    ga = gate_A.transpose(2, 0, 1).reshape(D, 128).reshape(KD, 128, 128).transpose(1, 0, 2).reshape(128, 2048)
    db = down_B.transpose(0, 2, 1).reshape(128, 2048)
    gwt = gate_w.T.reshape(KD, 128, 8).transpose(1, 0, 2).reshape(128, 128)
    wsmall = np.ascontiguousarray(np.concatenate([ua, ga, db, gwt], axis=1))
    return wqg, wfn, wdn, wsmall


def build_program():
    nc = bass.Bass("TRN2", target_bir_lowering=False, debug=False)
    x_in = nc.dram_tensor("x", [S, D], F32, kind="ExternalInput").ap()
    wqg_in = nc.dram_tensor("wqg", [16, 128, 4096], F32, kind="ExternalInput").ap()
    wfn_in = nc.dram_tensor("wfn", [JH, 128, 4480], F32, kind="ExternalInput").ap()
    wdn_in = nc.dram_tensor("wdn", [JD, 128, H], F32, kind="ExternalInput").ap()
    wsm_in = nc.dram_tensor("wsmall", [128, 6272], F32, kind="ExternalInput").ap()
    gfix_in = nc.dram_tensor("gfix", [128, 1], F32, kind="ExternalInput").ap()
    out_hbm = nc.dram_tensor("out", [TOWN, D], F32, kind="ExternalOutput").ap()
    hscr = nc.dram_tensor("hscr", [D, TOWN], F32, kind="Internal").ap()

    ident = np.eye(128, dtype=np.float32)
    id32_d = nc.inline_tensor(ident, name="id32c")
    idb_d = nc.inline_tensor(ident.astype(ml_dtypes.bfloat16), name="idbc")
    expt_np = np.zeros((8, 128), dtype=np.float32)
    for e in range(NE):
        expt_np[e, e * R:(e + 1) * R] = ALPHA
    expt_d = nc.inline_tensor(expt_np, name="exptc")
    ones1_d = nc.inline_tensor(np.ones((1, 128), np.float32), name="ones1c")
    onescol_d = nc.inline_tensor(np.ones((128, 1), np.float32), name="onescolc")

    P = Prog()
    es = contextlib.ExitStack()
    with es:
        C = es.enter_context(nc.sbuf_tensor("consts", [128, 8], F32))
        id32 = es.enter_context(nc.sbuf_tensor("id32", [128, 128], F32))
        idb = es.enter_context(nc.sbuf_tensor("idb", [128, 128], BF16))
        expT = es.enter_context(nc.sbuf_tensor("expT", [8, 128], F32))
        ones1 = es.enter_context(nc.sbuf_tensor("ones1", [1, 128], F32))
        onescol = es.enter_context(nc.sbuf_tensor("onescol", [128, 1], F32))
        gfix = es.enter_context(nc.sbuf_tensor("gfixs", [128, 1], F32))
        wsm = es.enter_context(nc.sbuf_tensor("wsm", [128, 6272], BF16))
        xnT_own = es.enter_context(nc.sbuf_tensor("xnT_own", [128, KD * TOWN], BF16))
        M1s = es.enter_context(nc.sbuf_tensor("M1s", [8, TOWN], F32))
        M2s = es.enter_context(nc.sbuf_tensor("M2s", [8, TOWN], F32))
        w1own = es.enter_context(nc.sbuf_tensor("w1own", [1, TOWN], F32))
        w2own = es.enter_context(nc.sbuf_tensor("w2own", [1, TOWN], F32))
        sems = {s: es.enter_context(nc.semaphore(s)) for s in SEM_NAMES}

        xnT_oth_holder = [None]

        def xn_slice(k, t0, w):
            if t0 >= OWN0:
                assert t0 + w <= OWN0 + TOWN
                return xnT_own[:, k * TOWN + (t0 - OWN0): k * TOWN + (t0 - OWN0) + w]
            assert t0 + w <= TOWN
            return xnT_oth_holder[0][:, k * TOWN + t0: k * TOWN + t0 + w]

        P.emit("sync", lambda E: E.dma_start(out=id32[:, :], in_=id32_d.ap()), inc="si", incn=16)
        P.emit("sync", lambda E: E.dma_start(out=idb[:, :], in_=idb_d.ap()), inc="si", incn=16)
        P.emit("sync", lambda E: E.dma_start(out=expT[:, :], in_=expt_d.ap()), inc="si", incn=16)
        P.emit("sync", lambda E: E.dma_start(out=ones1[:, :], in_=ones1_d.ap()), inc="si", incn=16)
        P.emit("sync", lambda E: E.dma_start(out=onescol[:, :], in_=onescol_d.ap()), inc="si", incn=16)
        P.emit("sync", lambda E: E.dma_start(out=gfix[:, :], in_=gfix_in), inc="si", incn=16)
        P.emit("gpsimd", lambda E: E.dma_start(out=wsm[:, :], in_=wsm_in), inc="gs", incn=16)
        for col, val in ((0, 1.0 / 2048.0), (1, 1e-5), (2, 1e-20), (3, 1.0), (4, -1e30), (5, -1.0)):
            P.emit("vector", lambda E, c=col, v=val: E.memset(C[:, c:c + 1], v),
                   inc="dv" if col == 5 else None)
        consts_dv = P.v("dv")
        init_si = P.v("si")

        s1 = contextlib.ExitStack()
        xnT_oth = s1.enter_context(nc.sbuf_tensor("xnT_oth", [128, KD * TOWN], BF16))
        xnT_oth_holder[0] = xnT_oth

        with contextlib.ExitStack() as sla:
            xld = [sla.enter_context(nc.sbuf_tensor(f"xld{i}", [128, D], F32)) for i in range(2)]
            xnb = [sla.enter_context(nc.sbuf_tensor(f"xnb{i}", [128, D], BF16)) for i in range(2)]
            lnjunk = sla.enter_context(nc.sbuf_tensor("lnjunk", [128, D], F32))
            lst = sla.enter_context(nc.sbuf_tensor("lnstats", [128, 16], F32))
            wqg_bf = [sla.enter_context(nc.sbuf_tensor(f"wqgbf{i}", [128, 4096], BF16)) for i in range(2)]
            qsb = sla.enter_context(nc.sbuf_tensor("qsb", [128, S], F32))
            gsb = sla.enter_context(nc.sbuf_tensor("gsb", [128, S], F32))
            hsb = [sla.enter_context(nc.sbuf_tensor(f"hsb{i}", [128, S], F32)) for i in range(2)]
            ltok = sla.enter_context(nc.sbuf_tensor("ltok", [128, 128], F32))
            m12all = sla.enter_context(nc.sbuf_tensor("m12all", [128, 32], F32))
            etok = sla.enter_context(nc.sbuf_tensor("etok", [128, 32], F32))
            l2buf = sla.enter_context(nc.sbuf_tensor("l2buf", [128, 8], F32))
            maskall = sla.enter_context(nc.sbuf_tensor("maskall", [128, 128], F32))
            mtmp = sla.enter_context(nc.sbuf_tensor("mtmp", [128, 8], F32))
            part12 = sla.enter_context(nc.sbuf_tensor("part12", [128, 2], F32))
            Ssb = [sla.enter_context(nc.sbuf_tensor(f"Ssb{i}", [1, 1], F32)) for i in range(2)]
            erow = [sla.enter_context(nc.sbuf_tensor(f"erow{i}", [1, TOWN], F32)) for i in range(2)]
            spq = contextlib.ExitStack()
            psq = spq.enter_context(nc.psum_tensor("psq", [128, 1024], F32, side="right"))
            psg = spq.enter_context(nc.psum_tensor("psg", [128, 2048], F32, side="right"))
            ptok = sla.enter_context(nc.psum_tensor("ptok", [128, 128], F32))
            strp = contextlib.ExitStack()
            trp = [strp.enter_context(nc.psum_tensor("trp0", [128, 1024], BF16))]

            P.emit("gpsimd", lambda E: E.dma_start(out=wqg_bf[0][:, :], in_=wqg_in[0]),
                   inc="gq0", incn=16)
            P.emit("gpsimd", lambda E: E.dma_start(out=wqg_bf[1][:, :], in_=wqg_in[1]),
                   inc="gq1", incn=16)
            wqg_ready = [P.v("gq0"), P.v("gq1")]
            wqg_free = [0, 0]
            bank_q = [0, 0]
            bank_g = [0, 0, 0, 0]
            q_pe = [0] * 4
            g_pe = [0] * 4
            evac_dv = [0] * 4
            evac_ac = [0] * 4
            hsb_free = [0, 0]
            scan_dv = 0

            def attn_mms(j, ncs, extra, defer_g=False):
                b = j % 2
                for qg in range(2):
                    ps = psq if qg == 0 else psg
                    gnc = ncs % 4
                    P.emit("tensor", lambda E, b=b, qg=qg, ncs=ncs, ps=ps, gnc=gnc: [
                        E.matmul(ps[:, ((ncs % 2) if qg == 0 else gnc) * 512:
                                    ((ncs % 2) if qg == 0 else gnc) * 512 + 512],
                                 wqg_bf[b][:, qg * 2048 + k * 128:qg * 2048 + (k + 1) * 128],
                                 xn_slice(k, ncs * 512, 512),
                                 start=(k == 0), stop=(k == KD - 1)) for k in range(KD)][-1],
                        waits=extra + [(f"gq{b}", wqg_ready[b])] +
                              ([("dv", bank_q[ncs % 2])] if qg == 0 else [("ac", bank_g[gnc])]),
                        inc="pe")
                    if qg == 0:
                        q_pe[ncs] = P.v("pe")
                    else:
                        g_pe[ncs] = P.v("pe")
                P.emit("vector", lambda E, ncs=ncs: E.tensor_scalar(
                    qsb[:, ncs * 512:(ncs + 1) * 512], psq[:, (ncs % 2) * 512:(ncs % 2 + 1) * 512],
                    C[:, 2:3], None, ALU.max),
                    waits=[("pe", q_pe[ncs]), ("dv", scan_dv)], inc="dv")
                evac_dv[ncs] = P.v("dv")
                bank_q[ncs % 2] = P.v("dv")
                if not defer_g:
                    P.emit("scalar", lambda E, ncs=ncs: E.activation(
                        gsb[:, ncs * 512:(ncs + 1) * 512],
                        psg[:, (ncs % 4) * 512:(ncs % 4 + 1) * 512], AF.Sigmoid),
                        waits=[("pe", g_pe[ncs]), ("dv", scan_dv)], inc="ac")
                    evac_ac[ncs] = P.v("ac")
                    bank_g[ncs % 4] = P.v("ac")

            def attn_tail(j):
                b = j % 2
                P.emit("vector", lambda E: E.tensor_scalar(
                    gsb[:, OWN0:OWN0 + 1], gsb[:, OWN0:OWN0 + 1], gfix[:, 0:1], None, ALU.mult),
                    waits=[("ac", evac_ac[3]), ("si", init_si)], inc="dv")
                P.emit("vector", lambda E, b=b: E.tensor_tensor_scan(
                    hsb[b][:, :], gsb[:, :], qsb[:, :], 0.0, ALU.mult, ALU.add),
                    waits=[("dv", P.v("dv")), ("ac", evac_ac[3]), (f"sh{b}", hsb_free[b])],
                    inc="dv")
                sc = P.v("dv")
                P.emit("sync", lambda E, j=j, b=b: E.dma_start(
                    out=hscr[j * 128:(j + 1) * 128, :], in_=hsb[b][:, OWN0:OWN0 + TOWN]),
                    waits=[("dv", sc)], inc=f"sh{b}", incn=16)
                hsb_free[b] = P.v(f"sh{b}")
                return sc

            xld_free = [0, 0]
            xnb_free = [0, 0]
            trp_free = [0]
            for i in range(16):
                b = i % 2
                P.emit("sync", lambda E, b=b, i=i: E.dma_start(
                    out=xld[b][:, :], in_=x_in[i * 128:(i + 1) * 128, :]),
                    waits=[("dv", xld_free[b])], inc=f"sx{b}", incn=16)
                ld_tgt = P.v(f"sx{b}")
                o8 = 8 * b
                P.emit("scalar", lambda E, b=b, o8=o8: E.activation(
                    lnjunk[:, :], xld[b][:, :], AF.Identity, accum_out=lst[:, o8:o8 + 1]),
                    waits=[(f"sx{b}", ld_tgt), ("dv", consts_dv)], inc="ac")
                P.emit("scalar", lambda E, o8=o8: E.activation(
                    lst[:, o8 + 1:o8 + 2], lst[:, o8:o8 + 1], AF.Identity, scale=-1.0 / 2048.0),
                    waits=[("ac", P.v("ac"))], inc="ac")
                P.emit("scalar", lambda E, b=b, o8=o8: E.activation(
                    lnjunk[:, :], xld[b][:, :], AF.Square, bias=lst[:, o8 + 1:o8 + 2],
                    accum_out=lst[:, o8 + 2:o8 + 3]),
                    waits=[("ac", P.v("ac"))], inc="ac")
                P.emit("scalar", lambda E, o8=o8: E.activation(
                    lst[:, o8 + 3:o8 + 4], lst[:, o8 + 2:o8 + 3], AF.Ln, bias=C[:, 1:2],
                    scale=1.0 / 2048.0),
                    waits=[("ac", P.v("ac"))], inc="ac")
                P.emit("scalar", lambda E, o8=o8: E.activation(
                    lst[:, o8 + 4:o8 + 5], lst[:, o8 + 3:o8 + 4], AF.Exp, scale=-0.5),
                    waits=[("ac", P.v("ac"))], inc="ac")
                P.emit("vector", lambda E, b=b, o8=o8: E.tensor_scalar(
                    xnb[b][:, :], xld[b][:, :], lst[:, o8 + 1:o8 + 2], lst[:, o8 + 4:o8 + 5],
                    ALU.add, ALU.mult),
                    waits=[("ac", P.v("ac")), ("pe", xnb_free[b])], inc="dv")
                xld_free[b] = P.v("dv")
                xn_dv = P.v("dv")
                dst = xnT_own if i >= 8 else xnT_oth
                off = (i - 8 if i >= 8 else i) * 128
                for hh in range(2):
                    P.emit("tensor", lambda E, b=b, hh=hh: [
                        E.transpose(trp[0][:, jj * 128:(jj + 1) * 128],
                                    xnb[b][:, (hh * 8 + jj) * 128:(hh * 8 + jj + 1) * 128],
                                    idb[:, :]) for jj in range(8)][-1],
                        waits=[("dv", xn_dv), ("dv", trp_free[0]), ("si", init_si)], inc="pe")
                    pe_tr = P.v("pe")
                    P.emit("vector", lambda E, hh=hh, dst=dst, off=off: E.tensor_copy(
                        dst[:, hh * 8 * TOWN:(hh * 8 + 8) * TOWN].rearrange(
                            "p (j t) -> p j t", j=8)[:, :, off:off + 128],
                        trp[0][:, :].rearrange("p (j t) -> p j t", j=8)),
                        waits=[("pe", pe_tr)], inc="dv")
                    trp_free[0] = P.v("dv")
                xnb_free[b] = P.v("pe")
                P.emit("tensor", lambda E, i=i: [
                    E.matmul(ptok[:, i * 8:(i + 1) * 8],
                             xn_slice(k, i * 128, 128),
                             wsm[:, 6144 + k * 8:6144 + (k + 1) * 8],
                             start=(k == 0), stop=(k == KD - 1))
                    for k in range(KD)][-1],
                    waits=[("gs", 16), ("dv", P.v("dv"))], inc="pe")
                if i % 4 == 3:
                    cseq = i // 4
                    attn_mms(0, cseq, [("dv", P.v("dv"))], defer_g=True)
            P.emit("scalar", lambda E: E.activation(gsb[:, :], psg[:, :], AF.Sigmoid),
                   waits=[("pe", g_pe[3])], inc="ac")
            for ncs in range(4):
                evac_ac[ncs] = P.v("ac")
                bank_g[ncs] = P.v("ac")
            scan_dv = attn_tail(0)
            wqg_free[0] = P.v("pe")
            lgt_pe = P.v("pe")
            ln_all_dv = P.v("dv")
            strp.close()

            ltok_dv = 0
            maskall_dv = [0] * 8
            m12_dv = 0

            def routing_piece(step):
                nonlocal ltok_dv, m12_dv
                if step == 0:
                    P.emit("vector", lambda E: E.tensor_copy(ltok[:, :], ptok[:, :]),
                           waits=[("pe", lgt_pe)], inc="dv")
                    ltok_dv = P.v("dv")
                elif 1 <= step <= 8:
                    for i in (2 * (step - 1), 2 * (step - 1) + 1):
                        own = i >= 8
                        m1dst = maskall[:, (i - 8) * 16:(i - 8) * 16 + 8] if own else mtmp[:, 0:8]
                        m2dst = maskall[:, (i - 8) * 16 + 8:(i - 8) * 16 + 16] if own else mtmp[:, 0:8]
                        L = ltok[:, i * 8:(i + 1) * 8]
                        P.emit("vector", lambda E, L=L, i=i: E.reduce_max(
                            m12all[:, 2 * i:2 * i + 1], L, axis=AX),
                            waits=[("dv", ltok_dv)], inc="dv")
                        P.emit("vector", lambda E, L=L, d=m1dst, i=i: E.tensor_scalar(
                            d, L, m12all[:, 2 * i:2 * i + 1], None, ALU.is_ge),
                            waits=[("dv", P.v("dv"))], inc="dv")
                        P.emit("vector", lambda E, L=L, d=m1dst: E.scalar_tensor_tensor(
                            l2buf[:, :], d, C[:, 4:5], L, ALU.mult, ALU.add),
                            waits=[("dv", P.v("dv"))], inc="dv")
                        P.emit("vector", lambda E, i=i: E.reduce_max(
                            m12all[:, 2 * i + 1:2 * i + 2], l2buf[:, :], axis=AX),
                            waits=[("dv", P.v("dv"))], inc="dv")
                        P.emit("vector", lambda E, d=m2dst, i=i: E.tensor_scalar(
                            d, l2buf[:, :], m12all[:, 2 * i + 1:2 * i + 2], None, ALU.is_ge),
                            waits=[("dv", P.v("dv"))], inc="dv")
                        if i >= 8:
                            maskall_dv[i - 8] = P.v("dv")
                    if step == 8:
                        m12_dv = P.v("dv")

            for j in range(1, 16):
                b = j % 2
                if j >= 2:
                    P.emit("gpsimd", lambda E, j=j, b=b: E.dma_start(
                        out=wqg_bf[b][:, :], in_=wqg_in[j]),
                        waits=[("pe", wqg_free[b])], inc=f"gq{b}", incn=16)
                    wqg_ready[b] = P.v(f"gq{b}")
                routing_piece(j - 1)
                for ncs in range(4):
                    attn_mms(j, ncs, [])
                wqg_free[b] = P.v("pe")
                scan_dv = attn_tail(j)
            spq.close()
            plog = sla.enter_context(nc.psum_tensor("plog", [8, 1024], F32))

            P.emit("scalar", lambda E: E.activation(etok[:, :], m12all[:, :], AF.Exp),
                   waits=[("dv", m12_dv)], inc="ac")
            etok_ac = P.v("ac")
            P.emit("vector", lambda E: E.reduce_sum(
                part12[:, 0:1], etok[:, :].rearrange("p (i two) -> p two i", two=2)[:, 0, :],
                axis=AX), waits=[("ac", etok_ac)], inc="dv")
            P.emit("vector", lambda E: E.reduce_sum(
                part12[:, 1:2], etok[:, :].rearrange("p (i two) -> p two i", two=2)[:, 1, :],
                axis=AX), waits=[("dv", P.v("dv"))], inc="dv")
            part_dv = P.v("dv")
            with contextlib.ExitStack() as sco:
                pmsk = sco.enter_context(nc.psum_tensor("pmsk", [8, 1024], F32))
                psS = [sco.enter_context(nc.psum_tensor(f"psS{i}", [1, 1], F32)) for i in range(2)]
                for mk in range(2):
                    P.emit("tensor", lambda E, mk=mk: E.matmul(
                        psS[mk][:, :], onescol[:, :], part12[:, mk:mk + 1], start=True, stop=True),
                        waits=[("dv", part_dv)], inc="pe")
                sum_pe = P.v("pe")
                for oi in range(8):
                    P.emit("tensor", lambda E, oi=oi: E.transpose(
                        plog[0:8, oi * 128:(oi + 1) * 128],
                        maskall[:, oi * 16:oi * 16 + 8], id32[:, :]),
                        waits=[("dv", maskall_dv[oi])], inc="pe")
                    P.emit("tensor", lambda E, oi=oi: E.transpose(
                        pmsk[0:8, oi * 128:(oi + 1) * 128],
                        maskall[:, oi * 16 + 8:oi * 16 + 16], id32[:, :]), inc="pe")
                msk_pe = P.v("pe")
                P.emit("vector", lambda E: E.tensor_copy(M1s[:, :], plog[:, :]),
                       waits=[("pe", msk_pe)], inc="dv")
                P.emit("vector", lambda E: E.tensor_copy(M2s[:, :], pmsk[:, :]), inc="dv")
                P.emit("vector", lambda E: E.tensor_copy(Ssb[0][:, :], psS[0][:, :]),
                       waits=[("pe", sum_pe)], inc="dv")
                P.emit("vector", lambda E: E.tensor_copy(Ssb[1][:, :], psS[1][:, :]), inc="dv")
                coda1_dv = P.v("dv")
            with contextlib.ExitStack() as sco:
                pmr = [sco.enter_context(nc.psum_tensor(f"pmr{i}", [1, TOWN], F32)) for i in range(2)]
                for mk in range(2):
                    P.emit("tensor", lambda E, mk=mk: [
                        E.transpose(pmr[mk][0:1, oi * 128:(oi + 1) * 128],
                                    etok[:, 2 * (8 + oi) + mk:2 * (8 + oi) + mk + 1], id32[:, :])
                        for oi in range(8)][-1],
                        waits=[("ac", etok_ac), ("dv", coda1_dv)], inc="pe")
                row_pe = P.v("pe")
                for mk in range(2):
                    P.emit("vector", lambda E, mk=mk: E.tensor_copy(erow[mk][:, :], pmr[mk][0:1, :]),
                           waits=[("pe", row_pe)], inc="dv")
                P.emit("vector", lambda E: E.reciprocal(Ssb[0][:, :], Ssb[0][:, :]),
                       waits=[("dv", P.v("dv"))], inc="dv")
                P.emit("vector", lambda E: E.reciprocal(Ssb[1][:, :], Ssb[1][:, :]),
                       waits=[("dv", P.v("dv"))], inc="dv")
                P.emit("vector", lambda E: E.tensor_scalar(
                    w1own[:, :], erow[0][0:1, :], Ssb[0][:, :], None, ALU.mult),
                    waits=[("dv", P.v("dv"))], inc="dv")
                P.emit("vector", lambda E: E.tensor_scalar(
                    w2own[:, :], erow[1][0:1, :], Ssb[1][:, :], None, ALU.mult),
                    waits=[("dv", P.v("dv"))], inc="dv")
        rt_done = P.snap()
        at_done = rt_done
        s1.close()

        sR = es.enter_context(contextlib.ExitStack())
        hmix = sR.enter_context(nc.sbuf_tensor("hmix", [128, JH * CH], BF16, side="right"))
        sF = es.enter_context(contextlib.ExitStack())
        Qsb = sF.enter_context(nc.sbuf_tensor("Qsb", [128, CH], BF16))
        stage = [sF.enter_context(nc.sbuf_tensor(f"stage{i}", [128, D], F32)) for i in range(4)]

        def emit_ffn_chunk(c, prev, hmix_free_pe, stage_free_so, aux=None):
            tloc = c * CH
            tglob = OWN0 + tloc
            bar = [(s, prev[s]) for s in ("pe", "dv", "ac", "sh0", "sh1")]
            with contextlib.ExitStack() as sup:
                NWF = 4
                wfn_bf = [sup.enter_context(nc.sbuf_tensor(f"wfnbf{c}_{i}", [128, 4480], BF16))
                          for i in range(NWF)]
                Mx = [sup.enter_context(nc.sbuf_tensor(f"Mx{c}_{i}", [128, CH], F32)) for i in range(2)]
                wbc = [sup.enter_context(nc.sbuf_tensor(f"wbc{c}_{i}", [128, CH], F32)) for i in range(2)]
                Pm = [sup.enter_context(nc.sbuf_tensor(f"Pm{c}_{i}", [128, CH], BF16)) for i in range(4)]
                Mxd = sup.enter_context(nc.sbuf_tensor(f"Mxd{c}", [128, CH], F32))
                uk = [sup.enter_context(nc.sbuf_tensor(f"uk{c}_{i}", [128, CH], F32)) for i in range(2)]
                gk = [sup.enter_context(nc.sbuf_tensor(f"gk{c}_{i}", [128, CH], F32)) for i in range(2)]
                sk = [sup.enter_context(nc.sbuf_tensor(f"sk{c}_{i}", [128, CH], F32)) for i in range(2)]
                hb = [[sup.enter_context(nc.sbuf_tensor(f"hb{c}_{k}_{i}", [128, CH], BF16))
                       for i in range(2)] for k in range(2)]
                pA = [sup.enter_context(nc.psum_tensor(f"pA{c}_{i}", [128, CH], F32)) for i in range(2)]
                pB = [sup.enter_context(nc.psum_tensor(f"pB{c}_{i}", [128, CH], F32)) for i in range(2)]
                pC = sup.enter_context(nc.psum_tensor(f"pC{c}", [128, CH], F32))
                pDg = sup.enter_context(nc.psum_tensor(f"pDg{c}", [128, CH], F32))
                pM = [sup.enter_context(nc.psum_tensor(f"pM{c}_{i}", [128, CH], F32)) for i in range(2)]

                P.emit("tensor", lambda E: [
                    E.matmul(pA[0][:, :], wsm[:, k * 128:(k + 1) * 128],
                             xn_slice(k, tglob, CH), start=(k == 0), stop=(k == KD - 1))
                    for k in range(KD)][-1], waits=[("dv", prev["dv"])], inc="pe")
                P.emit("tensor", lambda E: [
                    E.matmul(pB[0][:, :], wsm[:, 2048 + k * 128:2048 + (k + 1) * 128],
                             xn_slice(k, tglob, CH), start=(k == 0), stop=(k == KD - 1))
                    for k in range(KD)][-1], inc="pe")
                P.emit("tensor", lambda E: E.matmul(pA[1][:, :], expT[:, :],
                                                    M1s[0:8, tloc:tloc + CH], start=True, stop=True),
                       inc="pe")
                P.emit("tensor", lambda E: E.matmul(pB[1][:, :], expT[:, :],
                                                    M2s[0:8, tloc:tloc + CH], start=True, stop=True),
                       inc="pe")
                P.emit("tensor", lambda E: E.matmul(pM[0][:, :], ones1[:, :],
                                                    w1own[0:1, tloc:tloc + CH], start=True, stop=True),
                       inc="pe")
                P.emit("tensor", lambda E: E.matmul(pM[1][:, :], ones1[:, :],
                                                    w2own[0:1, tloc:tloc + CH], start=True, stop=True),
                       inc="pe")
                prep_pe = P.v("pe")
                P.emit("scalar", lambda E: E.copy(Mx[0][:, :], pA[1][:, :]),
                       waits=[("pe", prep_pe)] + bar, inc="ac")
                mx0_ac = P.v("ac")
                P.emit("scalar", lambda E: E.copy(Mx[1][:, :], pB[1][:, :]), inc="ac")
                mx1_ac = P.v("ac")
                P.emit("scalar", lambda E: E.copy(wbc[0][:, :], pM[0][:, :]), inc="ac")
                P.emit("scalar", lambda E: E.copy(wbc[1][:, :], pM[1][:, :]), inc="ac")
                P.emit("vector", lambda E: E.tensor_tensor(Pm[0][:, :], pA[0][:, :], Mx[0][:, :], ALU.mult),
                       waits=[("pe", prep_pe), ("ac", mx0_ac)] + bar, inc="dv")
                P.emit("vector", lambda E: E.tensor_tensor(Pm[2][:, :], pB[0][:, :], Mx[0][:, :], ALU.mult), inc="dv")
                P.emit("vector", lambda E: E.tensor_tensor(Mxd[:, :], Mx[1][:, :], Mx[0][:, :], ALU.subtract),
                       waits=[("ac", mx1_ac)], inc="dv")
                P.emit("vector", lambda E: E.tensor_tensor(Pm[1][:, :], pA[0][:, :], Mxd[:, :], ALU.mult), inc="dv")
                P.emit("vector", lambda E: E.tensor_tensor(Pm[3][:, :], pB[0][:, :], Mxd[:, :], ALU.mult), inc="dv")
                prep_dv = P.v("dv")

                wfn_ready = [0] * NWF
                hmix_dv = [0] * JH
                mdown_pe = [0] * JH
                hdv2 = [0, 0]
                hb_free = [0, 0]
                delta_free_dv = prep_dv
                prev_tile_dv = prep_dv
                u1cp_ac = [0, 0]
                g1cp_ac = [0, 0]
                for j in range(JH + 1):
                    bj = j % 2
                    wj = j % NWF
                    if j < JH:
                        wfree = mdown_pe[j - NWF] if j >= NWF else 0
                        if j >= NWF:
                            ph = []
                        elif aux is not None:
                            ph = [("pe", aux[j])]
                        else:
                            ph = [("pe", prev["pe"]), ("dv", prev["dv"]), ("ac", prev["ac"])]
                        P.emit("gpsimd", lambda E, j=j, wj=wj: E.dma_start(
                            out=wfn_bf[wj][:, :], in_=wfn_in[j]),
                            waits=[("pe", wfree)] + ph, inc=f"gf{wj}", incn=16)
                        wfn_ready[wj] = P.v(f"gf{wj}")
                    if j < JH:
                        pp = [("dv", prep_dv)] if j < 2 else []
                        P.emit("tensor", lambda E, bj=bj, wj=wj: [
                            E.matmul(pA[bj][:, :], wfn_bf[wj][:, k * 128:(k + 1) * 128],
                                     xn_slice(k, tglob, CH), start=(k == 0), stop=False)
                            for k in range(KD)][-1],
                            waits=[("ac", u1cp_ac[bj]), (f"gf{wj}", wfn_ready[wj])] + pp,
                            inc="pe")
                        P.emit("tensor", lambda E, bj=bj, wj=wj: E.matmul(
                            pA[bj][:, :], wfn_bf[wj][:, 4096:4224], Pm[0][:, :],
                            start=False, stop=True), inc="pe")
                        bA_pe = P.v("pe")
                        P.emit("tensor", lambda E, bj=bj, wj=wj: [
                            E.matmul(pB[bj][:, :], wfn_bf[wj][:, 2048 + k * 128:2048 + (k + 1) * 128],
                                     xn_slice(k, tglob, CH), start=(k == 0), stop=False)
                            for k in range(KD)][-1],
                            waits=[("ac", g1cp_ac[bj])] + pp, inc="pe")
                        P.emit("tensor", lambda E, bj=bj, wj=wj: E.matmul(
                            pB[bj][:, :], wfn_bf[wj][:, 4224:4352], Pm[2][:, :],
                            start=False, stop=True), inc="pe")
                        bB_pe = P.v("pe")
                        P.emit("tensor", lambda E, bj=bj, wj=wj: E.matmul(
                            pC[:, :], wfn_bf[wj][:, 4096:4224], Pm[1][:, :],
                            start=True, stop=True),
                            waits=[("dv", delta_free_dv)] + pp, inc="pe")
                        P.emit("tensor", lambda E, bj=bj, wj=wj: E.matmul(
                            pDg[:, :], wfn_bf[wj][:, 4224:4352], Pm[3][:, :],
                            start=True, stop=True), inc="pe")
                        dj_pe = P.v("pe")
                    if j >= 1:
                        pj = j - 1
                        for mk in range(2):
                            P.emit("tensor", lambda E, pj=pj, mk=mk: E.matmul(
                                pM[mk][:, :], wfn_bf[pj % NWF][:, 4352:4480], hb[mk][pj % 2][:, :],
                                start=(pj == 0), stop=(pj == JH - 1)),
                                waits=[("dv", hdv2[pj % 2])] if mk == 0 else [], inc="pe")
                        mdown_pe[pj] = P.v("pe")
                    if j >= JH:
                        break
                    P.emit("scalar", lambda E, bj=bj: E.activation(sk[0][:, :], pA[bj][:, :], AF.Silu),
                           waits=[("pe", bA_pe), ("dv", prev_tile_dv)], inc="ac")
                    P.emit("scalar", lambda E, bj=bj: E.copy(uk[0][:, :], pA[bj][:, :]), inc="ac")
                    u1cp_ac[bj] = P.v("ac")
                    P.emit("scalar", lambda E, bj=bj: E.copy(gk[0][:, :], pB[bj][:, :]),
                           waits=[("pe", bB_pe)], inc="ac")
                    g1cp_ac[bj] = P.v("ac")
                    s1_ac = P.v("ac")
                    P.emit("vector", lambda E: E.tensor_tensor(
                        uk[1][:, :], uk[0][:, :], pC[:, :], ALU.add),
                        waits=[("ac", u1cp_ac[bj]), ("pe", dj_pe)], inc="dv")
                    P.emit("scalar", lambda E: E.activation(sk[1][:, :], uk[1][:, :], AF.Silu),
                           waits=[("dv", P.v("dv"))], inc="ac")
                    s2_ac = P.v("ac")
                    P.emit("vector", lambda E: E.tensor_tensor(
                        gk[1][:, :], gk[0][:, :], pDg[:, :], ALU.add),
                        waits=[("ac", g1cp_ac[bj])], inc="dv")
                    delta_free_dv = P.v("dv")
                    P.emit("vector", lambda E: E.tensor_tensor(
                        sk[0][:, :], sk[0][:, :], wbc[0][:, :], ALU.mult),
                        waits=[("ac", s1_ac)], inc="dv")
                    P.emit("vector", lambda E, bj=bj: E.tensor_tensor(
                        hb[0][bj][:, :], sk[0][:, :], gk[0][:, :], ALU.mult),
                        waits=[("pe", hb_free[bj])], inc="dv")
                    P.emit("vector", lambda E: E.tensor_tensor(
                        sk[1][:, :], sk[1][:, :], wbc[1][:, :], ALU.mult),
                        waits=[("ac", s2_ac)], inc="dv")
                    P.emit("vector", lambda E, bj=bj: E.tensor_tensor(
                        hb[1][bj][:, :], sk[1][:, :], gk[1][:, :], ALU.mult), inc="dv")
                    hdv2[bj] = P.v("dv")
                    P.emit("vector", lambda E, bj=bj, j=j: E.tensor_tensor(
                        hmix[:, j * CH:(j + 1) * CH], hb[0][bj][:, :], hb[1][bj][:, :], ALU.add),
                        waits=[("pe", hmix_free_pe)], inc="dv")
                    hmix_dv[j] = P.v("dv")
                    prev_tile_dv = P.v("dv")
                    if j >= 1:
                        hb_free[(j - 1) % 2] = mdown_pe[j - 1]
                mq_pe = P.v("pe")
                P.emit("vector", lambda E: E.tensor_tensor(uk[0][:, :], pM[0][:, :], Mx[0][:, :], ALU.mult),
                       waits=[("pe", mq_pe)], inc="dv")
                P.emit("vector", lambda E: E.tensor_tensor(uk[1][:, :], pM[1][:, :], Mx[1][:, :], ALU.mult),
                       inc="dv")
                P.emit("vector", lambda E: E.tensor_tensor(Qsb[:, :], uk[0][:, :], uk[1][:, :], ALU.add),
                       waits=[("dv", P.v("dv"))], inc="dv")
            up_done = P.snap()

            with contextlib.ExitStack() as sdn:
                NWD = 4
                wdn_bf = [sdn.enter_context(nc.sbuf_tensor(f"wdnbf{c}_{i}", [128, H], BF16))
                          for i in range(NWD)]
                hrel = [sdn.enter_context(nc.sbuf_tensor(f"hrel{c}_{i}", [128, CH], F32)) for i in range(2)]
                sumfm = [sdn.enter_context(nc.sbuf_tensor(f"sumfm{c}_{i}", [128, CH], F32)) for i in range(2)]
                pso = [sdn.enter_context(nc.psum_tensor(f"pso{c}_{i}", [128, CH], F32)) for i in range(2)]
                pstr = [sdn.enter_context(nc.psum_tensor(f"pstr{c}_{i}", [128, CH], F32)) for i in range(2)]

                bar2 = [(s, up_done[s]) for s in ("dv", "ac", "pe")]
                stage_bar = ([(s, prev[s]) for s in ("dv", "ac", "pe")]
                             if c == 0 else [])
                stage_sg = {}
                for tb in range(4):
                    P.emit("sync", lambda E, tb=tb: E.dma_start(
                        out=stage[tb][:, :],
                        in_=x_in[tglob + tb * 128:tglob + (tb + 1) * 128, :]),
                        waits=list(stage_free_so) + stage_bar, inc=f"sg{tb}", incn=16)
                    stage_sg[tb] = P.v(f"sg{tb}")
                mm_pe_arr = [0] * JD
                hrel_free = [0, 0]
                sum_free = [0, 0]
                pso_free = [0, 0]
                pstr_free = [0, 0]
                sf_dv = [0, 0]
                stage_dv = 0
                for jd in range(JD + 1):
                    bd = jd % 2
                    wd = jd % NWD
                    if jd < JD:
                        if jd >= NWD:
                            wwaits = [("pe", mm_pe_arr[jd - NWD])]
                        elif jd < NWD - 1:
                            wwaits = [("pe", mdown_pe[JH - NWD + jd])]
                        else:
                            wwaits = [("dv", up_done["dv"])]
                        P.emit("gpsimd", lambda E, jd=jd, wd=wd: E.dma_start(
                            out=wdn_bf[wd][:, :], in_=wdn_in[jd]),
                            waits=wwaits, inc=f"gd{wd}", incn=16)
                        wd_tgt = P.v(f"gd{wd}")
                        hph = [("pe", dj_pe)] if jd < NWD else []
                        P.emit("sync", lambda E, jd=jd, bd=bd: E.dma_start(
                            out=hrel[bd][:, :], in_=hscr[jd * 128:(jd + 1) * 128, tloc:tloc + CH]),
                            waits=[("dv", hrel_free[bd]), ("sh0", at_done["sh0"]),
                                   ("sh1", at_done["sh1"])] + hph,
                            inc=f"sr{bd}", incn=16)
                        h_tgt = P.v(f"sr{bd}")
                        P.emit("tensor", lambda E, bd=bd, wd=wd: [
                            E.matmul(pso[bd][:, :], wdn_bf[wd][:, k * 128:(k + 1) * 128],
                                     hmix[:, k * CH:(k + 1) * CH], start=(k == 0), stop=False)
                            for k in range(JH - 4)][-1],
                            waits=[("pe", up_done["pe"]), ("ac", up_done["ac"]),
                                   ("dv", hmix_dv[JH - 5]), (f"gd{wd}", wd_tgt),
                                   ("dv", pso_free[bd])], inc="pe")
                        P.emit("tensor", lambda E, bd=bd, wd=wd: [
                            E.matmul(pso[bd][:, :], wdn_bf[wd][:, k * 128:(k + 1) * 128],
                                     hmix[:, k * CH:(k + 1) * CH], start=False, stop=False)
                            for k in range(JH - 4, JH)][-1],
                            waits=[("dv", up_done["dv"])], inc="pe")
                        P.emit("tensor", lambda E, jd=jd, bd=bd: E.matmul(
                            pso[bd][:, :], wsm[:, 4096 + jd * 128:4096 + (jd + 1) * 128],
                            Qsb[:, :], start=False, stop=True), inc="pe")
                        mm_pe_arr[jd] = P.v("pe")
                    if jd >= 1:
                        pd = jd - 1
                        bp = pd % 2
                        P.emit("tensor", lambda E, bp=bp: [
                            E.transpose(pstr[bp][:, bb * 128:(bb + 1) * 128],
                                        sumfm[bp][:, bb * 128:(bb + 1) * 128], id32[:, :])
                            for bb in range(4)][-1],
                            waits=[("dv", sf_dv[bp]), ("dv", pstr_free[bp])], inc="pe")
                        tr_pe = P.v("pe")
                        sum_free[bp] = tr_pe
                    if jd < JD:
                        P.emit("vector", lambda E, bd=bd: E.tensor_tensor(
                            sumfm[bd][:, :], hrel[bd][:, :], pso[bd][:, :], ALU.add),
                            waits=[("pe", mm_pe_arr[jd]), (f"sr{bd}", h_tgt),
                                   ("pe", sum_free[bd])], inc="dv")
                        pso_free[bd] = P.v("dv")
                        hrel_free[bd] = P.v("dv")
                        sf_dv[bd] = P.v("dv")
                    if jd >= 1:
                        pd = jd - 1
                        bp = pd % 2
                        for bb in range(4):
                            P.emit("vector", lambda E, bb=bb, bp=bp, pd=pd: E.tensor_tensor(
                                stage[bb][:, pd * 128:(pd + 1) * 128],
                                stage[bb][:, pd * 128:(pd + 1) * 128],
                                pstr[bp][:, bb * 128:(bb + 1) * 128], ALU.add),
                                waits=[("pe", tr_pe), (f"sg{bb}", stage_sg[bb])], inc="dv")
                        pstr_free[bp] = P.v("dv")
                        stage_dv = P.v("dv")
                hmix_free_pe = P.v("pe")
                for tb in range(4):
                    P.emit("sync", lambda E, tb=tb: E.dma_start(
                        out=out_hbm[tloc + tb * 128:tloc + (tb + 1) * 128, :],
                        in_=stage[tb][:, :]),
                        waits=[("dv", stage_dv)], inc=f"so{c}", incn=16)
                stage_free_so = [(f"so{c}", P.v(f"so{c}"))]
            return P.snap(), hmix_free_pe, stage_free_so, [
                mm_pe_arr[JD - NWD + s] for s in range(NWD)]

        prev = rt_done
        hmix_free_pe = 0
        stage_free_so = []
        aux = None
        for c in range(2):
            prev, hmix_free_pe, stage_free_so, aux = emit_ffn_chunk(
                c, prev, hmix_free_pe, stage_free_so, aux)

        P.emit("sync", waits=[("so0", P.v("so0")), ("so1", P.v("so1"))])
        P.check()
        with nc.Block() as block:
            P.realize(block, sems)
    return nc


_CACHE = {}


def kernel(**inputs):
    x = np.ascontiguousarray(np.asarray(inputs["L0_attn_x"], np.float32))
    wqg, wfn, wdn, wsmall = _retile_inputs(inputs)
    nc = build_program()

    in_maps = []
    for c in range(8):
        b, half = c // 2, c % 2
        if half == 1:
            xc = np.ascontiguousarray(x[b])
            gfix = np.ones((128, 1), np.float32)
        else:
            xc = np.ascontiguousarray(np.concatenate([x[b][1024:], x[b][:1024]], axis=0))
            gfix = np.zeros((128, 1), np.float32)
        in_maps.append({"x": xc, "wqg": wqg, "wfn": wfn, "wdn": wdn,
                        "wsmall": wsmall, "gfix": gfix})

    res = run_bass_kernel_spmd(nc, in_maps, core_ids=list(range(8)))
    out = np.empty((4, S, D), np.float32)
    for c in range(8):
        b, half = c // 2, c % 2
        out[b, half * 1024:(half + 1) * 1024, :] = res.results[c]["out"]
    return out


if __name__ == "__main__":
    import reference
    ins = {k: np.asarray(v) for k, v in reference.setup_inputs().items()}
    got = kernel(**ins)
    want = np.asarray(reference.reference(**{k: v for k, v in ins.items()}))
    err = np.abs(got - want)
    rel = np.sqrt(((got - want) ** 2).mean() / (want ** 2).mean())
    print("max abs err:", err.max(), "rel l2:", rel)


# version 34 (final)
# speedup vs baseline: 1.0721x; 1.0049x over previous
import contextlib
import numpy as np
import ml_dtypes

import concourse.bass as bass
import concourse.mybir as mybir
from concourse.bass_utils import run_bass_kernel_spmd

F32 = mybir.dt.float32
BF16 = mybir.dt.bfloat16
AF = mybir.ActivationFunctionType
ALU = mybir.AluOpType
AX = mybir.AxisListType.X

D = 2048
S = 2048
H = 5632
NE, R = 8, 16
KD = 16
JH = 44
JD = 16
TOWN = 1024
OWN0 = 1024
CH = 512
ALPHA = 2.0

ENGS = ("sync", "gpsimd", "tensor", "vector", "scalar")
SEM_NAMES = ("pe", "dv", "ac", "si", "gs", "sx0", "sx1", "sh0", "sh1",
             "gq0", "gq1", "gf0", "gf1", "gf2", "gf3", "gd0", "gd1", "gd2", "gd3", "sr0", "sr1",
             "sg0", "sg1", "sg2", "sg3", "so0", "so1")


class Prog:

    def __init__(self):
        self.st = {e: [] for e in ENGS}
        self.cnt = {s: 0 for s in SEM_NAMES}
        self.audit = []

    def v(self, sem):
        return self.cnt[sem]

    def snap(self):
        return dict(self.cnt)

    def emit(self, eng, fn=None, waits=(), inc=None, incn=1):
        if fn is not None and eng in ("vector", "scalar"):
            own = "dv" if eng == "vector" else "ac"
            waits = list(waits) + [(own, self.cnt[own])]
            if inc is None:
                inc, incn = own, 1
            else:
                assert inc == own
        wd = {}
        for s, v in waits:
            if v > 0:
                wd[s] = max(wd.get(s, 0), v)
        waits = list(wd.items())
        if inc is not None:
            self.cnt[inc] += incn
        self.st[eng].append((fn, tuple(waits), inc, incn))
        self.audit.extend(waits)
        return self.cnt[inc] if inc else None

    def check(self):
        for s, v in self.audit:
            assert v <= self.cnt[s], f"wait {s}>={v} but final {self.cnt[s]} -> hang"

    def realize(self, block, sems):
        for eng in ENGS:
            stream = self.st[eng]
            if not stream:
                continue

            def body(E, stream=stream):
                for fn, waits, inc, incn in stream:
                    for s, v in waits:
                        E.wait_ge(sems[s], v)
                    inst = fn(E) if fn is not None else None
                    if inc is not None:
                        inst.then_inc(sems[inc], incn)

            getattr(block, eng)(body)


def _retile_inputs(ins):
    W = np.ascontiguousarray(np.asarray(ins["L0_attn_W"], np.float32))
    w_up = np.ascontiguousarray(np.asarray(ins["L0_ffn_w_up"], np.float32))
    w_gate = np.ascontiguousarray(np.asarray(ins["L0_ffn_w_gate"], np.float32))
    w_down = np.ascontiguousarray(np.asarray(ins["L0_ffn_w_down"], np.float32))
    up_A = np.asarray(ins["L0_ffn_up_A"], np.float32)
    up_B = np.asarray(ins["L0_ffn_up_B"], np.float32)
    gate_A = np.asarray(ins["L0_ffn_gate_A"], np.float32)
    gate_B = np.asarray(ins["L0_ffn_gate_B"], np.float32)
    down_A = np.asarray(ins["L0_ffn_down_A"], np.float32)
    down_B = np.asarray(ins["L0_ffn_down_B"], np.float32)
    gate_w = np.asarray(ins["L0_ffn_gate_w"], np.float32)

    wqg = np.ascontiguousarray(
        W.reshape(2, 16, 128, KD, 128).transpose(1, 4, 0, 3, 2).reshape(16, 128, 4096))
    wu = w_up.reshape(JH, 128, KD, 128).transpose(0, 3, 2, 1).reshape(JH, 128, 2048)
    wg = w_gate.reshape(JH, 128, KD, 128).transpose(0, 3, 2, 1).reshape(JH, 128, 2048)
    ub = up_B.transpose(0, 2, 1).reshape(128, JH, 128).transpose(1, 0, 2)
    gb = gate_B.transpose(0, 2, 1).reshape(128, JH, 128).transpose(1, 0, 2)
    da = down_A.transpose(2, 0, 1).reshape(JH, 128, 128)
    wfn = np.ascontiguousarray(np.concatenate([wu, wg, ub, gb, da], axis=2))
    wdn = np.ascontiguousarray(
        w_down.reshape(JD, 128, JH, 128).transpose(0, 3, 2, 1).reshape(JD, 128, H))
    ua = up_A.transpose(2, 0, 1).reshape(D, 128).reshape(KD, 128, 128).transpose(1, 0, 2).reshape(128, 2048)
    ga = gate_A.transpose(2, 0, 1).reshape(D, 128).reshape(KD, 128, 128).transpose(1, 0, 2).reshape(128, 2048)
    db = down_B.transpose(0, 2, 1).reshape(128, 2048)
    gwt = gate_w.T.reshape(KD, 128, 8).transpose(1, 0, 2).reshape(128, 128)
    wsmall = np.ascontiguousarray(np.concatenate([ua, ga, db, gwt], axis=1))
    return wqg, wfn, wdn, wsmall


def build_program():
    nc = bass.Bass("TRN2", target_bir_lowering=False, debug=False)
    x_in = nc.dram_tensor("x", [S, D], F32, kind="ExternalInput").ap()
    wqg_in = nc.dram_tensor("wqg", [16, 128, 4096], F32, kind="ExternalInput").ap()
    wfn_in = nc.dram_tensor("wfn", [JH, 128, 4480], F32, kind="ExternalInput").ap()
    wdn_in = nc.dram_tensor("wdn", [JD, 128, H], F32, kind="ExternalInput").ap()
    wsm_in = nc.dram_tensor("wsmall", [128, 6272], F32, kind="ExternalInput").ap()
    gfix_in = nc.dram_tensor("gfix", [128, 1], F32, kind="ExternalInput").ap()
    out_hbm = nc.dram_tensor("out", [TOWN, D], F32, kind="ExternalOutput").ap()
    hscr = nc.dram_tensor("hscr", [D, TOWN], F32, kind="Internal").ap()

    ident = np.eye(128, dtype=np.float32)
    id32_d = nc.inline_tensor(ident, name="id32c")
    idb_d = nc.inline_tensor(ident.astype(ml_dtypes.bfloat16), name="idbc")
    expt_np = np.zeros((8, 128), dtype=np.float32)
    for e in range(NE):
        expt_np[e, e * R:(e + 1) * R] = ALPHA
    expt_d = nc.inline_tensor(expt_np, name="exptc")
    ones1_d = nc.inline_tensor(np.ones((1, 128), np.float32), name="ones1c")
    onescol_d = nc.inline_tensor(np.ones((128, 1), np.float32), name="onescolc")

    P = Prog()
    es = contextlib.ExitStack()
    with es:
        C = es.enter_context(nc.sbuf_tensor("consts", [128, 8], F32))
        id32 = es.enter_context(nc.sbuf_tensor("id32", [128, 128], F32))
        idb = es.enter_context(nc.sbuf_tensor("idb", [128, 128], BF16))
        expT = es.enter_context(nc.sbuf_tensor("expT", [8, 128], F32))
        ones1 = es.enter_context(nc.sbuf_tensor("ones1", [1, 128], F32))
        onescol = es.enter_context(nc.sbuf_tensor("onescol", [128, 1], F32))
        gfix = es.enter_context(nc.sbuf_tensor("gfixs", [128, 1], F32))
        wsm = es.enter_context(nc.sbuf_tensor("wsm", [128, 6272], BF16))
        xnT_own = es.enter_context(nc.sbuf_tensor("xnT_own", [128, KD * TOWN], BF16))
        M1s = es.enter_context(nc.sbuf_tensor("M1s", [8, TOWN], F32))
        M2s = es.enter_context(nc.sbuf_tensor("M2s", [8, TOWN], F32))
        w1own = es.enter_context(nc.sbuf_tensor("w1own", [1, TOWN], F32))
        w2own = es.enter_context(nc.sbuf_tensor("w2own", [1, TOWN], F32))
        sems = {s: es.enter_context(nc.semaphore(s)) for s in SEM_NAMES}

        xnT_oth_holder = [None]

        def xn_slice(k, t0, w):
            if t0 >= OWN0:
                assert t0 + w <= OWN0 + TOWN
                return xnT_own[:, k * TOWN + (t0 - OWN0): k * TOWN + (t0 - OWN0) + w]
            assert t0 + w <= TOWN
            return xnT_oth_holder[0][:, k * TOWN + t0: k * TOWN + t0 + w]

        P.emit("sync", lambda E: E.dma_start(out=id32[:, :], in_=id32_d.ap()), inc="si", incn=16)
        P.emit("sync", lambda E: E.dma_start(out=idb[:, :], in_=idb_d.ap()), inc="si", incn=16)
        P.emit("sync", lambda E: E.dma_start(out=expT[:, :], in_=expt_d.ap()), inc="si", incn=16)
        P.emit("sync", lambda E: E.dma_start(out=ones1[:, :], in_=ones1_d.ap()), inc="si", incn=16)
        P.emit("sync", lambda E: E.dma_start(out=onescol[:, :], in_=onescol_d.ap()), inc="si", incn=16)
        P.emit("sync", lambda E: E.dma_start(out=gfix[:, :], in_=gfix_in), inc="si", incn=16)
        P.emit("gpsimd", lambda E: E.dma_start(out=wsm[:, :], in_=wsm_in), inc="gs", incn=16)
        for col, val in ((0, 1.0 / 2048.0), (1, 1e-5), (2, 1e-20), (3, 1.0), (4, -1e30), (5, -1.0)):
            P.emit("vector", lambda E, c=col, v=val: E.memset(C[:, c:c + 1], v),
                   inc="dv" if col == 5 else None)
        consts_dv = P.v("dv")
        init_si = P.v("si")

        s1 = contextlib.ExitStack()
        xnT_oth = s1.enter_context(nc.sbuf_tensor("xnT_oth", [128, KD * TOWN], BF16))
        xnT_oth_holder[0] = xnT_oth

        with contextlib.ExitStack() as sla:
            xld = [sla.enter_context(nc.sbuf_tensor(f"xld{i}", [128, D], F32)) for i in range(2)]
            xnb = [sla.enter_context(nc.sbuf_tensor(f"xnb{i}", [128, D], BF16)) for i in range(2)]
            lnjunk = sla.enter_context(nc.sbuf_tensor("lnjunk", [128, D], F32))
            lst = sla.enter_context(nc.sbuf_tensor("lnstats", [128, 16], F32))
            wqg_bf = [sla.enter_context(nc.sbuf_tensor(f"wqgbf{i}", [128, 4096], BF16)) for i in range(2)]
            qsb = sla.enter_context(nc.sbuf_tensor("qsb", [128, S], F32))
            gsb = sla.enter_context(nc.sbuf_tensor("gsb", [128, S], F32))
            hsb = [sla.enter_context(nc.sbuf_tensor(f"hsb{i}", [128, S], F32)) for i in range(2)]
            ltok = sla.enter_context(nc.sbuf_tensor("ltok", [128, 128], F32))
            m12all = sla.enter_context(nc.sbuf_tensor("m12all", [128, 32], F32))
            etok = sla.enter_context(nc.sbuf_tensor("etok", [128, 32], F32))
            l2buf = sla.enter_context(nc.sbuf_tensor("l2buf", [128, 8], F32))
            maskall = sla.enter_context(nc.sbuf_tensor("maskall", [128, 128], F32))
            mtmp = sla.enter_context(nc.sbuf_tensor("mtmp", [128, 8], F32))
            part12 = sla.enter_context(nc.sbuf_tensor("part12", [128, 2], F32))
            Ssb = [sla.enter_context(nc.sbuf_tensor(f"Ssb{i}", [1, 1], F32)) for i in range(2)]
            erow = [sla.enter_context(nc.sbuf_tensor(f"erow{i}", [1, TOWN], F32)) for i in range(2)]
            spq = contextlib.ExitStack()
            psq = spq.enter_context(nc.psum_tensor("psq", [128, 1024], F32, side="right"))
            psg = spq.enter_context(nc.psum_tensor("psg", [128, 2048], F32, side="right"))
            ptok = sla.enter_context(nc.psum_tensor("ptok", [128, 128], F32))
            strp = contextlib.ExitStack()
            trp = [strp.enter_context(nc.psum_tensor("trp0", [128, 1024], BF16))]

            P.emit("gpsimd", lambda E: E.dma_start(out=wqg_bf[0][:, :], in_=wqg_in[0]),
                   inc="gq0", incn=16)
            P.emit("gpsimd", lambda E: E.dma_start(out=wqg_bf[1][:, :], in_=wqg_in[1]),
                   inc="gq1", incn=16)
            wqg_ready = [P.v("gq0"), P.v("gq1")]
            wqg_free = [0, 0]
            bank_q = [0, 0]
            bank_g = [0, 0, 0, 0]
            q_pe = [0] * 4
            g_pe = [0] * 4
            evac_dv = [0] * 4
            evac_ac = [0] * 4
            hsb_free = [0, 0]
            scan_dv = 0

            def attn_mms(j, ncs, extra, defer_g=False):
                b = j % 2
                for qg in range(2):
                    ps = psq if qg == 0 else psg
                    gnc = ncs % 4
                    P.emit("tensor", lambda E, b=b, qg=qg, ncs=ncs, ps=ps, gnc=gnc: [
                        E.matmul(ps[:, ((ncs % 2) if qg == 0 else gnc) * 512:
                                    ((ncs % 2) if qg == 0 else gnc) * 512 + 512],
                                 wqg_bf[b][:, qg * 2048 + k * 128:qg * 2048 + (k + 1) * 128],
                                 xn_slice(k, ncs * 512, 512),
                                 start=(k == 0), stop=(k == KD - 1)) for k in range(KD)][-1],
                        waits=extra + [(f"gq{b}", wqg_ready[b])] +
                              ([("dv", bank_q[ncs % 2])] if qg == 0 else [("ac", bank_g[gnc])]),
                        inc="pe")
                    if qg == 0:
                        q_pe[ncs] = P.v("pe")
                    else:
                        g_pe[ncs] = P.v("pe")
                P.emit("vector", lambda E, ncs=ncs: E.tensor_scalar(
                    qsb[:, ncs * 512:(ncs + 1) * 512], psq[:, (ncs % 2) * 512:(ncs % 2 + 1) * 512],
                    C[:, 2:3], None, ALU.max),
                    waits=[("pe", q_pe[ncs]), ("dv", scan_dv)], inc="dv")
                evac_dv[ncs] = P.v("dv")
                bank_q[ncs % 2] = P.v("dv")
                if not defer_g:
                    P.emit("scalar", lambda E, ncs=ncs: E.activation(
                        gsb[:, ncs * 512:(ncs + 1) * 512],
                        psg[:, (ncs % 4) * 512:(ncs % 4 + 1) * 512], AF.Sigmoid),
                        waits=[("pe", g_pe[ncs]), ("dv", scan_dv)], inc="ac")
                    evac_ac[ncs] = P.v("ac")
                    bank_g[ncs % 4] = P.v("ac")

            def attn_tail(j):
                b = j % 2
                P.emit("vector", lambda E: E.tensor_scalar(
                    gsb[:, OWN0:OWN0 + 1], gsb[:, OWN0:OWN0 + 1], gfix[:, 0:1], None, ALU.mult),
                    waits=[("ac", evac_ac[3]), ("si", init_si)], inc="dv")
                P.emit("vector", lambda E, b=b: E.tensor_tensor_scan(
                    hsb[b][:, :], gsb[:, :], qsb[:, :], 0.0, ALU.mult, ALU.add),
                    waits=[("dv", P.v("dv")), ("ac", evac_ac[3]), (f"sh{b}", hsb_free[b])],
                    inc="dv")
                sc = P.v("dv")
                P.emit("sync", lambda E, j=j, b=b: E.dma_start(
                    out=hscr[j * 128:(j + 1) * 128, :], in_=hsb[b][:, OWN0:OWN0 + TOWN]),
                    waits=[("dv", sc)], inc=f"sh{b}", incn=16)
                hsb_free[b] = P.v(f"sh{b}")
                return sc

            xld_free = [0, 0]
            xnb_free = [0, 0]
            trp_free = [0]
            for i in range(16):
                b = i % 2
                P.emit("sync", lambda E, b=b, i=i: E.dma_start(
                    out=xld[b][:, :], in_=x_in[i * 128:(i + 1) * 128, :]),
                    waits=[("dv", xld_free[b])], inc=f"sx{b}", incn=16)
                ld_tgt = P.v(f"sx{b}")
                o8 = 8 * b
                P.emit("scalar", lambda E, b=b, o8=o8: E.activation(
                    lnjunk[:, :], xld[b][:, :], AF.Identity, accum_out=lst[:, o8:o8 + 1]),
                    waits=[(f"sx{b}", ld_tgt), ("dv", consts_dv)], inc="ac")
                P.emit("scalar", lambda E, o8=o8: E.activation(
                    lst[:, o8 + 1:o8 + 2], lst[:, o8:o8 + 1], AF.Identity, scale=-1.0 / 2048.0),
                    waits=[("ac", P.v("ac"))], inc="ac")
                P.emit("scalar", lambda E, b=b, o8=o8: E.activation(
                    lnjunk[:, :], xld[b][:, :], AF.Square, bias=lst[:, o8 + 1:o8 + 2],
                    accum_out=lst[:, o8 + 2:o8 + 3]),
                    waits=[("ac", P.v("ac"))], inc="ac")
                P.emit("scalar", lambda E, o8=o8: E.activation(
                    lst[:, o8 + 3:o8 + 4], lst[:, o8 + 2:o8 + 3], AF.Ln, bias=C[:, 1:2],
                    scale=1.0 / 2048.0),
                    waits=[("ac", P.v("ac"))], inc="ac")
                P.emit("scalar", lambda E, o8=o8: E.activation(
                    lst[:, o8 + 4:o8 + 5], lst[:, o8 + 3:o8 + 4], AF.Exp, scale=-0.5),
                    waits=[("ac", P.v("ac"))], inc="ac")
                P.emit("vector", lambda E, b=b, o8=o8: E.tensor_scalar(
                    xnb[b][:, :], xld[b][:, :], lst[:, o8 + 1:o8 + 2], lst[:, o8 + 4:o8 + 5],
                    ALU.add, ALU.mult),
                    waits=[("ac", P.v("ac")), ("pe", xnb_free[b])], inc="dv")
                xld_free[b] = P.v("dv")
                xn_dv = P.v("dv")
                dst = xnT_own if i >= 8 else xnT_oth
                off = (i - 8 if i >= 8 else i) * 128
                for hh in range(2):
                    P.emit("tensor", lambda E, b=b, hh=hh: [
                        E.transpose(trp[0][:, jj * 128:(jj + 1) * 128],
                                    xnb[b][:, (hh * 8 + jj) * 128:(hh * 8 + jj + 1) * 128],
                                    idb[:, :]) for jj in range(8)][-1],
                        waits=[("dv", xn_dv), ("dv", trp_free[0]), ("si", init_si)], inc="pe")
                    pe_tr = P.v("pe")
                    P.emit("vector", lambda E, hh=hh, dst=dst, off=off: E.tensor_copy(
                        dst[:, hh * 8 * TOWN:(hh * 8 + 8) * TOWN].rearrange(
                            "p (j t) -> p j t", j=8)[:, :, off:off + 128],
                        trp[0][:, :].rearrange("p (j t) -> p j t", j=8)),
                        waits=[("pe", pe_tr)], inc="dv")
                    trp_free[0] = P.v("dv")
                xnb_free[b] = P.v("pe")
                P.emit("tensor", lambda E, i=i: [
                    E.matmul(ptok[:, i * 8:(i + 1) * 8],
                             xn_slice(k, i * 128, 128),
                             wsm[:, 6144 + k * 8:6144 + (k + 1) * 8],
                             start=(k == 0), stop=(k == KD - 1))
                    for k in range(KD)][-1],
                    waits=[("gs", 16), ("dv", P.v("dv"))], inc="pe")
                if i % 4 == 3:
                    cseq = i // 4
                    attn_mms(0, cseq, [("dv", P.v("dv"))], defer_g=True)
            P.emit("scalar", lambda E: E.activation(gsb[:, :], psg[:, :], AF.Sigmoid),
                   waits=[("pe", g_pe[3])], inc="ac")
            for ncs in range(4):
                evac_ac[ncs] = P.v("ac")
                bank_g[ncs] = P.v("ac")
            scan_dv = attn_tail(0)
            wqg_free[0] = P.v("pe")
            lgt_pe = P.v("pe")
            ln_all_dv = P.v("dv")
            strp.close()

            ltok_dv = 0
            maskall_dv = [0] * 8
            m12_dv = 0

            def routing_piece(step):
                nonlocal ltok_dv, m12_dv
                if step == 0:
                    P.emit("vector", lambda E: E.tensor_copy(ltok[:, :], ptok[:, :]),
                           waits=[("pe", lgt_pe)], inc="dv")
                    ltok_dv = P.v("dv")
                elif 1 <= step <= 8:
                    for i in (2 * (step - 1), 2 * (step - 1) + 1):
                        own = i >= 8
                        m1dst = maskall[:, (i - 8) * 16:(i - 8) * 16 + 8] if own else mtmp[:, 0:8]
                        m2dst = maskall[:, (i - 8) * 16 + 8:(i - 8) * 16 + 16] if own else mtmp[:, 0:8]
                        L = ltok[:, i * 8:(i + 1) * 8]
                        P.emit("vector", lambda E, L=L, i=i: E.reduce_max(
                            m12all[:, 2 * i:2 * i + 1], L, axis=AX),
                            waits=[("dv", ltok_dv)], inc="dv")
                        P.emit("vector", lambda E, L=L, d=m1dst, i=i: E.tensor_scalar(
                            d, L, m12all[:, 2 * i:2 * i + 1], None, ALU.is_ge),
                            waits=[("dv", P.v("dv"))], inc="dv")
                        P.emit("vector", lambda E, L=L, d=m1dst: E.scalar_tensor_tensor(
                            l2buf[:, :], d, C[:, 4:5], L, ALU.mult, ALU.add),
                            waits=[("dv", P.v("dv"))], inc="dv")
                        P.emit("vector", lambda E, i=i: E.reduce_max(
                            m12all[:, 2 * i + 1:2 * i + 2], l2buf[:, :], axis=AX),
                            waits=[("dv", P.v("dv"))], inc="dv")
                        P.emit("vector", lambda E, d=m2dst, i=i: E.tensor_scalar(
                            d, l2buf[:, :], m12all[:, 2 * i + 1:2 * i + 2], None, ALU.is_ge),
                            waits=[("dv", P.v("dv"))], inc="dv")
                        if i >= 8:
                            maskall_dv[i - 8] = P.v("dv")
                    if step == 8:
                        m12_dv = P.v("dv")

            for j in range(1, 16):
                b = j % 2
                if j >= 2:
                    P.emit("gpsimd", lambda E, j=j, b=b: E.dma_start(
                        out=wqg_bf[b][:, :], in_=wqg_in[j]),
                        waits=[("pe", wqg_free[b])], inc=f"gq{b}", incn=16)
                    wqg_ready[b] = P.v(f"gq{b}")
                routing_piece(j - 1)
                for ncs in range(4):
                    attn_mms(j, ncs, [])
                wqg_free[b] = P.v("pe")
                scan_dv = attn_tail(j)
            spq.close()
            plog = sla.enter_context(nc.psum_tensor("plog", [8, 1024], F32))

            P.emit("scalar", lambda E: E.activation(etok[:, :], m12all[:, :], AF.Exp),
                   waits=[("dv", m12_dv)], inc="ac")
            etok_ac = P.v("ac")
            P.emit("vector", lambda E: E.reduce_sum(
                part12[:, 0:1], etok[:, :].rearrange("p (i two) -> p two i", two=2)[:, 0, :],
                axis=AX), waits=[("ac", etok_ac)], inc="dv")
            P.emit("vector", lambda E: E.reduce_sum(
                part12[:, 1:2], etok[:, :].rearrange("p (i two) -> p two i", two=2)[:, 1, :],
                axis=AX), waits=[("dv", P.v("dv"))], inc="dv")
            part_dv = P.v("dv")
            with contextlib.ExitStack() as sco:
                pmsk = sco.enter_context(nc.psum_tensor("pmsk", [8, 1024], F32))
                psS = [sco.enter_context(nc.psum_tensor(f"psS{i}", [1, 1], F32)) for i in range(2)]
                for mk in range(2):
                    P.emit("tensor", lambda E, mk=mk: E.matmul(
                        psS[mk][:, :], onescol[:, :], part12[:, mk:mk + 1], start=True, stop=True),
                        waits=[("dv", part_dv)], inc="pe")
                sum_pe = P.v("pe")
                for oi in range(8):
                    P.emit("tensor", lambda E, oi=oi: E.transpose(
                        plog[0:8, oi * 128:(oi + 1) * 128],
                        maskall[:, oi * 16:oi * 16 + 8], id32[:, :]),
                        waits=[("dv", maskall_dv[oi])], inc="pe")
                    P.emit("tensor", lambda E, oi=oi: E.transpose(
                        pmsk[0:8, oi * 128:(oi + 1) * 128],
                        maskall[:, oi * 16 + 8:oi * 16 + 16], id32[:, :]), inc="pe")
                msk_pe = P.v("pe")
                P.emit("vector", lambda E: E.tensor_copy(M1s[:, :], plog[:, :]),
                       waits=[("pe", msk_pe)], inc="dv")
                P.emit("vector", lambda E: E.tensor_copy(M2s[:, :], pmsk[:, :]), inc="dv")
                P.emit("vector", lambda E: E.tensor_copy(Ssb[0][:, :], psS[0][:, :]),
                       waits=[("pe", sum_pe)], inc="dv")
                P.emit("vector", lambda E: E.tensor_copy(Ssb[1][:, :], psS[1][:, :]), inc="dv")
                coda1_dv = P.v("dv")
            with contextlib.ExitStack() as sco:
                pmr = [sco.enter_context(nc.psum_tensor(f"pmr{i}", [1, TOWN], F32)) for i in range(2)]
                for mk in range(2):
                    P.emit("tensor", lambda E, mk=mk: [
                        E.transpose(pmr[mk][0:1, oi * 128:(oi + 1) * 128],
                                    etok[:, 2 * (8 + oi) + mk:2 * (8 + oi) + mk + 1], id32[:, :])
                        for oi in range(8)][-1],
                        waits=[("ac", etok_ac), ("dv", coda1_dv)], inc="pe")
                row_pe = P.v("pe")
                for mk in range(2):
                    P.emit("vector", lambda E, mk=mk: E.tensor_copy(erow[mk][:, :], pmr[mk][0:1, :]),
                           waits=[("pe", row_pe)], inc="dv")
                P.emit("vector", lambda E: E.reciprocal(Ssb[0][:, :], Ssb[0][:, :]),
                       waits=[("dv", P.v("dv"))], inc="dv")
                P.emit("vector", lambda E: E.reciprocal(Ssb[1][:, :], Ssb[1][:, :]),
                       waits=[("dv", P.v("dv"))], inc="dv")
                P.emit("vector", lambda E: E.tensor_scalar(
                    w1own[:, :], erow[0][0:1, :], Ssb[0][:, :], None, ALU.mult),
                    waits=[("dv", P.v("dv"))], inc="dv")
                P.emit("vector", lambda E: E.tensor_scalar(
                    w2own[:, :], erow[1][0:1, :], Ssb[1][:, :], None, ALU.mult),
                    waits=[("dv", P.v("dv"))], inc="dv")
        rt_done = P.snap()
        at_done = rt_done
        s1.close()

        sR = es.enter_context(contextlib.ExitStack())
        hmix = sR.enter_context(nc.sbuf_tensor("hmix", [128, JH * CH], BF16, side="right"))
        sF = es.enter_context(contextlib.ExitStack())
        Qsb = sF.enter_context(nc.sbuf_tensor("Qsb", [128, CH], BF16))
        stage = [sF.enter_context(nc.sbuf_tensor(f"stage{i}", [128, D], F32)) for i in range(4)]

        def emit_ffn_chunk(c, prev, hmix_free_pe, stage_free_so, aux=None):
            tloc = c * CH
            tglob = OWN0 + tloc
            bar = [(s, prev[s]) for s in ("pe", "dv", "ac", "sh0", "sh1")]
            with contextlib.ExitStack() as sup:
                NWF = 4
                wfn_bf = [sup.enter_context(nc.sbuf_tensor(f"wfnbf{c}_{i}", [128, 4480], BF16))
                          for i in range(NWF)]
                Mx = [sup.enter_context(nc.sbuf_tensor(f"Mx{c}_{i}", [128, CH], F32)) for i in range(2)]
                wbc = [sup.enter_context(nc.sbuf_tensor(f"wbc{c}_{i}", [128, CH], F32)) for i in range(2)]
                Pm = [sup.enter_context(nc.sbuf_tensor(f"Pm{c}_{i}", [128, CH], BF16)) for i in range(4)]
                Mxd = sup.enter_context(nc.sbuf_tensor(f"Mxd{c}", [128, CH], F32))
                uk = [sup.enter_context(nc.sbuf_tensor(f"uk{c}_{i}", [128, CH], F32)) for i in range(2)]
                gk = [sup.enter_context(nc.sbuf_tensor(f"gk{c}_{i}", [128, CH], F32)) for i in range(2)]
                sk = [sup.enter_context(nc.sbuf_tensor(f"sk{c}_{i}", [128, CH], F32)) for i in range(2)]
                hb = [[sup.enter_context(nc.sbuf_tensor(f"hb{c}_{k}_{i}", [128, CH], BF16))
                       for i in range(2)] for k in range(2)]
                pA = [sup.enter_context(nc.psum_tensor(f"pA{c}_{i}", [128, CH], F32)) for i in range(2)]
                pB = [sup.enter_context(nc.psum_tensor(f"pB{c}_{i}", [128, CH], F32)) for i in range(2)]
                pC = sup.enter_context(nc.psum_tensor(f"pC{c}", [128, CH], F32))
                pDg = sup.enter_context(nc.psum_tensor(f"pDg{c}", [128, CH], F32))
                pM = [sup.enter_context(nc.psum_tensor(f"pM{c}_{i}", [128, CH], F32)) for i in range(2)]

                P.emit("tensor", lambda E: [
                    E.matmul(pC[:, :], wsm[:, k * 128:(k + 1) * 128],
                             xn_slice(k, tglob, CH), start=(k == 0), stop=(k == KD - 1))
                    for k in range(KD)][-1], waits=[("dv", prev["dv"])], inc="pe")
                P.emit("tensor", lambda E: [
                    E.matmul(pDg[:, :], wsm[:, 2048 + k * 128:2048 + (k + 1) * 128],
                             xn_slice(k, tglob, CH), start=(k == 0), stop=(k == KD - 1))
                    for k in range(KD)][-1], inc="pe")
                P.emit("tensor", lambda E: E.matmul(pM[0][:, :], expT[:, :],
                                                    M1s[0:8, tloc:tloc + CH], start=True, stop=True),
                       inc="pe")
                P.emit("tensor", lambda E: E.matmul(pM[1][:, :], expT[:, :],
                                                    M2s[0:8, tloc:tloc + CH], start=True, stop=True),
                       inc="pe")
                P.emit("tensor", lambda E: E.matmul(pA[1][:, :], ones1[:, :],
                                                    w1own[0:1, tloc:tloc + CH], start=True, stop=True),
                       inc="pe")
                P.emit("tensor", lambda E: E.matmul(pB[1][:, :], ones1[:, :],
                                                    w2own[0:1, tloc:tloc + CH], start=True, stop=True),
                       inc="pe")
                prep_pe = P.v("pe")
                P.emit("scalar", lambda E: E.copy(Mx[0][:, :], pM[0][:, :]),
                       waits=[("pe", prep_pe)] + bar, inc="ac")
                mx0_ac = P.v("ac")
                P.emit("scalar", lambda E: E.copy(Mx[1][:, :], pM[1][:, :]), inc="ac")
                mx1_ac = P.v("ac")
                P.emit("scalar", lambda E: E.copy(wbc[0][:, :], pA[1][:, :]), inc="ac")
                P.emit("scalar", lambda E: E.copy(wbc[1][:, :], pB[1][:, :]), inc="ac")
                prep_ac = P.v("ac")
                P.emit("vector", lambda E: E.tensor_tensor(Pm[0][:, :], pC[:, :], Mx[0][:, :], ALU.mult),
                       waits=[("pe", prep_pe), ("ac", mx0_ac)] + bar, inc="dv")
                pm0_dv = P.v("dv")
                P.emit("vector", lambda E: E.tensor_tensor(Pm[2][:, :], pDg[:, :], Mx[0][:, :], ALU.mult), inc="dv")
                pm2_dv = P.v("dv")
                P.emit("vector", lambda E: E.tensor_tensor(Mxd[:, :], Mx[1][:, :], Mx[0][:, :], ALU.subtract),
                       waits=[("ac", mx1_ac)], inc="dv")
                P.emit("vector", lambda E: E.tensor_tensor(Pm[1][:, :], pC[:, :], Mxd[:, :], ALU.mult), inc="dv")
                P.emit("vector", lambda E: E.tensor_tensor(Pm[3][:, :], pDg[:, :], Mxd[:, :], ALU.mult), inc="dv")
                prep_dv = P.v("dv")

                wfn_ready = [0] * NWF
                hmix_dv = [0] * JH
                mdown_pe = [0] * JH
                hdv2 = [0, 0]
                hb_free = [0, 0]
                delta_free_dv = prep_dv
                prev_tile_dv = prep_dv
                u1cp_ac = [0, 0]
                g1cp_ac = [0, 0]
                for j in range(JH + 1):
                    bj = j % 2
                    wj = j % NWF
                    if j < JH:
                        wfree = mdown_pe[j - NWF] if j >= NWF else 0
                        if j >= NWF:
                            ph = []
                        elif aux is not None:
                            ph = [("pe", aux[j])]
                        else:
                            ph = [("pe", prev["pe"]), ("dv", prev["dv"]), ("ac", prev["ac"])]
                        P.emit("gpsimd", lambda E, j=j, wj=wj: E.dma_start(
                            out=wfn_bf[wj][:, :], in_=wfn_in[j]),
                            waits=[("pe", wfree)] + ph, inc=f"gf{wj}", incn=16)
                        wfn_ready[wj] = P.v(f"gf{wj}")
                    if j < JH:
                        pp = [("dv", prep_dv), ("ac", prep_ac)] if j == 1 else []
                        P.emit("tensor", lambda E, bj=bj, wj=wj: [
                            E.matmul(pA[bj][:, :], wfn_bf[wj][:, k * 128:(k + 1) * 128],
                                     xn_slice(k, tglob, CH), start=(k == 0), stop=False)
                            for k in range(KD)][-1],
                            waits=[("ac", u1cp_ac[bj]), (f"gf{wj}", wfn_ready[wj])] + pp,
                            inc="pe")
                        P.emit("tensor", lambda E, bj=bj, wj=wj: E.matmul(
                            pA[bj][:, :], wfn_bf[wj][:, 4096:4224], Pm[0][:, :],
                            start=False, stop=True),
                            waits=[("dv", pm0_dv)] if j == 0 else [], inc="pe")
                        bA_pe = P.v("pe")
                        P.emit("tensor", lambda E, bj=bj, wj=wj: [
                            E.matmul(pB[bj][:, :], wfn_bf[wj][:, 2048 + k * 128:2048 + (k + 1) * 128],
                                     xn_slice(k, tglob, CH), start=(k == 0), stop=False)
                            for k in range(KD)][-1],
                            waits=[("ac", g1cp_ac[bj])] + pp, inc="pe")
                        P.emit("tensor", lambda E, bj=bj, wj=wj: E.matmul(
                            pB[bj][:, :], wfn_bf[wj][:, 4224:4352], Pm[2][:, :],
                            start=False, stop=True),
                            waits=[("dv", pm2_dv)] if j == 0 else [], inc="pe")
                        bB_pe = P.v("pe")
                        P.emit("tensor", lambda E, bj=bj, wj=wj: E.matmul(
                            pC[:, :], wfn_bf[wj][:, 4096:4224], Pm[1][:, :],
                            start=True, stop=True),
                            waits=[("dv", delta_free_dv)], inc="pe")
                        P.emit("tensor", lambda E, bj=bj, wj=wj: E.matmul(
                            pDg[:, :], wfn_bf[wj][:, 4224:4352], Pm[3][:, :],
                            start=True, stop=True), inc="pe")
                        dj_pe = P.v("pe")
                    if j >= 1:
                        pj = j - 1
                        for mk in range(2):
                            P.emit("tensor", lambda E, pj=pj, mk=mk: E.matmul(
                                pM[mk][:, :], wfn_bf[pj % NWF][:, 4352:4480], hb[mk][pj % 2][:, :],
                                start=(pj == 0), stop=(pj == JH - 1)),
                                waits=[("dv", hdv2[pj % 2])] if mk == 0 else [], inc="pe")
                        mdown_pe[pj] = P.v("pe")
                    if j >= JH:
                        break
                    P.emit("scalar", lambda E, bj=bj: E.activation(sk[0][:, :], pA[bj][:, :], AF.Silu),
                           waits=[("pe", bA_pe), ("dv", prev_tile_dv)], inc="ac")
                    P.emit("scalar", lambda E, bj=bj: E.copy(uk[0][:, :], pA[bj][:, :]), inc="ac")
                    u1cp_ac[bj] = P.v("ac")
                    P.emit("scalar", lambda E, bj=bj: E.copy(gk[0][:, :], pB[bj][:, :]),
                           waits=[("pe", bB_pe)], inc="ac")
                    g1cp_ac[bj] = P.v("ac")
                    s1_ac = P.v("ac")
                    P.emit("vector", lambda E: E.tensor_tensor(
                        uk[1][:, :], uk[0][:, :], pC[:, :], ALU.add),
                        waits=[("ac", u1cp_ac[bj]), ("pe", dj_pe)], inc="dv")
                    P.emit("scalar", lambda E: E.activation(sk[1][:, :], uk[1][:, :], AF.Silu),
                           waits=[("dv", P.v("dv"))], inc="ac")
                    s2_ac = P.v("ac")
                    P.emit("vector", lambda E: E.tensor_tensor(
                        gk[1][:, :], gk[0][:, :], pDg[:, :], ALU.add),
                        waits=[("ac", g1cp_ac[bj])], inc="dv")
                    delta_free_dv = P.v("dv")
                    P.emit("vector", lambda E: E.tensor_tensor(
                        sk[0][:, :], sk[0][:, :], wbc[0][:, :], ALU.mult),
                        waits=[("ac", s1_ac)], inc="dv")
                    P.emit("vector", lambda E, bj=bj: E.tensor_tensor(
                        hb[0][bj][:, :], sk[0][:, :], gk[0][:, :], ALU.mult),
                        waits=[("pe", hb_free[bj])], inc="dv")
                    P.emit("vector", lambda E: E.tensor_tensor(
                        sk[1][:, :], sk[1][:, :], wbc[1][:, :], ALU.mult),
                        waits=[("ac", s2_ac)], inc="dv")
                    P.emit("vector", lambda E, bj=bj: E.tensor_tensor(
                        hb[1][bj][:, :], sk[1][:, :], gk[1][:, :], ALU.mult), inc="dv")
                    hdv2[bj] = P.v("dv")
                    P.emit("vector", lambda E, bj=bj, j=j: E.tensor_tensor(
                        hmix[:, j * CH:(j + 1) * CH], hb[0][bj][:, :], hb[1][bj][:, :], ALU.add),
                        waits=[("pe", hmix_free_pe)], inc="dv")
                    hmix_dv[j] = P.v("dv")
                    prev_tile_dv = P.v("dv")
                    if j >= 1:
                        hb_free[(j - 1) % 2] = mdown_pe[j - 1]
                mq_pe = P.v("pe")
                P.emit("vector", lambda E: E.tensor_tensor(uk[0][:, :], pM[0][:, :], Mx[0][:, :], ALU.mult),
                       waits=[("pe", mq_pe)], inc="dv")
                P.emit("vector", lambda E: E.tensor_tensor(uk[1][:, :], pM[1][:, :], Mx[1][:, :], ALU.mult),
                       inc="dv")
                P.emit("vector", lambda E: E.tensor_tensor(Qsb[:, :], uk[0][:, :], uk[1][:, :], ALU.add),
                       waits=[("dv", P.v("dv"))], inc="dv")
            up_done = P.snap()

            with contextlib.ExitStack() as sdn:
                NWD = 4
                wdn_bf = [sdn.enter_context(nc.sbuf_tensor(f"wdnbf{c}_{i}", [128, H], BF16))
                          for i in range(NWD)]
                hrel = [sdn.enter_context(nc.sbuf_tensor(f"hrel{c}_{i}", [128, CH], F32)) for i in range(2)]
                sumfm = [sdn.enter_context(nc.sbuf_tensor(f"sumfm{c}_{i}", [128, CH], F32)) for i in range(2)]
                pso = [sdn.enter_context(nc.psum_tensor(f"pso{c}_{i}", [128, CH], F32)) for i in range(2)]
                pstr = [sdn.enter_context(nc.psum_tensor(f"pstr{c}_{i}", [128, CH], F32)) for i in range(2)]

                bar2 = [(s, up_done[s]) for s in ("dv", "ac", "pe")]
                stage_bar = ([(s, prev[s]) for s in ("dv", "ac", "pe")]
                             if c == 0 else [])
                stage_sg = {}
                for tb in range(4):
                    P.emit("sync", lambda E, tb=tb: E.dma_start(
                        out=stage[tb][:, :],
                        in_=x_in[tglob + tb * 128:tglob + (tb + 1) * 128, :]),
                        waits=list(stage_free_so) + stage_bar, inc=f"sg{tb}", incn=16)
                    stage_sg[tb] = P.v(f"sg{tb}")
                mm_pe_arr = [0] * JD
                hrel_free = [0, 0]
                sum_free = [0, 0]
                pso_free = [0, 0]
                pstr_free = [0, 0]
                sf_dv = [0, 0]
                stage_dv = 0
                for jd in range(JD + 1):
                    bd = jd % 2
                    wd = jd % NWD
                    if jd < JD:
                        if jd >= NWD:
                            wwaits = [("pe", mm_pe_arr[jd - NWD])]
                        elif jd < NWD - 1:
                            wwaits = [("pe", mdown_pe[JH - NWD + jd])]
                        else:
                            wwaits = [("dv", up_done["dv"])]
                        P.emit("gpsimd", lambda E, jd=jd, wd=wd: E.dma_start(
                            out=wdn_bf[wd][:, :], in_=wdn_in[jd]),
                            waits=wwaits, inc=f"gd{wd}", incn=16)
                        wd_tgt = P.v(f"gd{wd}")
                        hph = [("pe", dj_pe)] if jd < NWD else []
                        P.emit("sync", lambda E, jd=jd, bd=bd: E.dma_start(
                            out=hrel[bd][:, :], in_=hscr[jd * 128:(jd + 1) * 128, tloc:tloc + CH]),
                            waits=[("dv", hrel_free[bd]), ("sh0", at_done["sh0"]),
                                   ("sh1", at_done["sh1"])] + hph,
                            inc=f"sr{bd}", incn=16)
                        h_tgt = P.v(f"sr{bd}")
                        P.emit("tensor", lambda E, bd=bd, wd=wd: [
                            E.matmul(pso[bd][:, :], wdn_bf[wd][:, k * 128:(k + 1) * 128],
                                     hmix[:, k * CH:(k + 1) * CH], start=(k == 0), stop=False)
                            for k in range(JH - 4)][-1],
                            waits=[("pe", up_done["pe"]), ("ac", up_done["ac"]),
                                   ("dv", hmix_dv[JH - 5]), (f"gd{wd}", wd_tgt),
                                   ("dv", pso_free[bd])], inc="pe")
                        P.emit("tensor", lambda E, bd=bd, wd=wd: [
                            E.matmul(pso[bd][:, :], wdn_bf[wd][:, k * 128:(k + 1) * 128],
                                     hmix[:, k * CH:(k + 1) * CH], start=False, stop=False)
                            for k in range(JH - 4, JH)][-1],
                            waits=[("dv", up_done["dv"])], inc="pe")
                        P.emit("tensor", lambda E, jd=jd, bd=bd: E.matmul(
                            pso[bd][:, :], wsm[:, 4096 + jd * 128:4096 + (jd + 1) * 128],
                            Qsb[:, :], start=False, stop=True), inc="pe")
                        mm_pe_arr[jd] = P.v("pe")
                    if jd >= 1:
                        pd = jd - 1
                        bp = pd % 2
                        P.emit("tensor", lambda E, bp=bp: [
                            E.transpose(pstr[bp][:, bb * 128:(bb + 1) * 128],
                                        sumfm[bp][:, bb * 128:(bb + 1) * 128], id32[:, :])
                            for bb in range(4)][-1],
                            waits=[("dv", sf_dv[bp]), ("dv", pstr_free[bp])], inc="pe")
                        tr_pe = P.v("pe")
                        sum_free[bp] = tr_pe
                    if jd < JD:
                        P.emit("vector", lambda E, bd=bd: E.tensor_tensor(
                            sumfm[bd][:, :], hrel[bd][:, :], pso[bd][:, :], ALU.add),
                            waits=[("pe", mm_pe_arr[jd]), (f"sr{bd}", h_tgt),
                                   ("pe", sum_free[bd])], inc="dv")
                        pso_free[bd] = P.v("dv")
                        hrel_free[bd] = P.v("dv")
                        sf_dv[bd] = P.v("dv")
                    if jd >= 1:
                        pd = jd - 1
                        bp = pd % 2
                        for bb in range(4):
                            P.emit("vector", lambda E, bb=bb, bp=bp, pd=pd: E.tensor_tensor(
                                stage[bb][:, pd * 128:(pd + 1) * 128],
                                stage[bb][:, pd * 128:(pd + 1) * 128],
                                pstr[bp][:, bb * 128:(bb + 1) * 128], ALU.add),
                                waits=[("pe", tr_pe), (f"sg{bb}", stage_sg[bb])], inc="dv")
                        pstr_free[bp] = P.v("dv")
                        stage_dv = P.v("dv")
                hmix_free_pe = P.v("pe")
                for tb in range(4):
                    P.emit("sync", lambda E, tb=tb: E.dma_start(
                        out=out_hbm[tloc + tb * 128:tloc + (tb + 1) * 128, :],
                        in_=stage[tb][:, :]),
                        waits=[("dv", stage_dv)], inc=f"so{c}", incn=16)
                stage_free_so = [(f"so{c}", P.v(f"so{c}"))]
            return P.snap(), hmix_free_pe, stage_free_so, [
                mm_pe_arr[JD - NWD + s] for s in range(NWD)]

        prev = rt_done
        hmix_free_pe = 0
        stage_free_so = []
        aux = None
        for c in range(2):
            prev, hmix_free_pe, stage_free_so, aux = emit_ffn_chunk(
                c, prev, hmix_free_pe, stage_free_so, aux)

        P.emit("sync", waits=[("so0", P.v("so0")), ("so1", P.v("so1"))])
        P.check()
        with nc.Block() as block:
            P.realize(block, sems)
    return nc


_CACHE = {}


def kernel(**inputs):
    x = np.ascontiguousarray(np.asarray(inputs["L0_attn_x"], np.float32))
    wqg, wfn, wdn, wsmall = _retile_inputs(inputs)
    nc = build_program()

    in_maps = []
    for c in range(8):
        b, half = c // 2, c % 2
        if half == 1:
            xc = np.ascontiguousarray(x[b])
            gfix = np.ones((128, 1), np.float32)
        else:
            xc = np.ascontiguousarray(np.concatenate([x[b][1024:], x[b][:1024]], axis=0))
            gfix = np.zeros((128, 1), np.float32)
        in_maps.append({"x": xc, "wqg": wqg, "wfn": wfn, "wdn": wdn,
                        "wsmall": wsmall, "gfix": gfix})

    res = run_bass_kernel_spmd(nc, in_maps, core_ids=list(range(8)))
    out = np.empty((4, S, D), np.float32)
    for c in range(8):
        b, half = c // 2, c % 2
        out[b, half * 1024:(half + 1) * 1024, :] = res.results[c]["out"]
    return out


if __name__ == "__main__":
    import reference
    ins = {k: np.asarray(v) for k, v in reference.setup_inputs().items()}
    got = kernel(**ins)
    want = np.asarray(reference.reference(**{k: v for k, v in ins.items()}))
    err = np.abs(got - want)
    rel = np.sqrt(((got - want) ** 2).mean() / (want ** 2).mean())
    print("max abs err:", err.max(), "rel l2:", rel)
